# version 02
# baseline (speedup 1.0000x reference)
import numpy as np
import ml_dtypes
from contextlib import ExitStack

import concourse.bass as bass
import concourse.mybir as mybir
from concourse.bass_utils import run_bass_kernel_spmd

F32 = mybir.dt.float32
BF16 = mybir.dt.bfloat16
AF = mybir.ActivationFunctionType
ALU = mybir.AluOpType
AX = mybir.AxisListType

NCORES = 8
D = 1024
NKC = 8
B = 4
S = 4096
TPC = 2048
EPS_LN = 1e-5

BF = ml_dtypes.bfloat16



def mcform(wt, m_tile=128, k_tile=128):
    K, M = wt.shape
    assert K % k_tile == 0 and M % m_tile == 0, (K, M, k_tile, m_tile)
    n_kc, n_mc = K // k_tile, M // m_tile
    return np.ascontiguousarray(
        wt.reshape(n_kc, k_tile, n_mc, m_tile).transpose(2, 1, 0, 3)
        .reshape(n_mc, k_tile, n_kc * m_tile).astype(np.float32))


def biasform(b_vec, m_tile=128):
    M = b_vec.shape[0]
    n_mc = M // m_tile
    return np.ascontiguousarray(b_vec.reshape(n_mc, m_tile).T.astype(np.float32))


def fm(x_tokmajor):
    return np.ascontiguousarray(x_tokmajor.T)



class P:
    ENGINES = ("sync", "tensor", "vector", "scalar", "gpsimd")

    def __init__(self, name="prog"):
        self.nc = bass.Bass("TRN2", target_bir_lowering=False)
        self.es = ExitStack()
        self.es.__enter__()
        self.q = {e: [] for e in self.ENGINES}
        self.sems = {}
        self.semv = {}
        self._uid = 0

    def uid(self):
        self._uid += 1
        return self._uid

    def din(self, name, shape, dt=F32):
        return self.nc.dram_tensor(name, list(shape), dt, kind="ExternalInput").ap()

    def dout(self, name, shape, dt=F32):
        return self.nc.dram_tensor(name, list(shape), dt, kind="ExternalOutput").ap()

    def dscratch(self, name, shape, dt=F32):
        return self.nc.dram_tensor(name, list(shape), dt, kind="Internal").ap()

    def sb(self, name, part, free, dt=F32):
        return self.es.enter_context(self.nc.sbuf_tensor(name, [part, free], dt))

    def psum(self, name, part=128, free=512, dt=F32):
        return self.es.enter_context(self.nc.psum_tensor(name, [part, free], dt))

    def _sem(self, name):
        if name not in self.sems:
            self.sems[name] = self.es.enter_context(self.nc.semaphore(name))
            self.semv[name] = 0
        return self.sems[name]

    def v(self, name):
        return self.semv.get(name, 0)

    def barrier(self, engines=("pe", "ve", "ac", "gp")):
        return [(s, self.v(s)) for s in engines if self.v(s) > 0]

    def emit(self, eng, fn, waits=(), sem=None, inc=1):
        waits = tuple(w for w in waits if w is not None and w[1] > 0)
        for s, _ in waits:
            self._sem(s)
        tgt = None
        if sem is not None:
            self._sem(sem)
            self.semv[sem] += inc
            tgt = self.semv[sem]
        self.q[eng].append((fn, waits, sem, inc))
        return tgt

    def dma(self, queue, out, in_, waits=(), sem=None):
        if sem is None:
            sem = queue + "dma"
        return self.emit(queue, lambda e: e.dma_start(out=out, in_=in_),
                         waits, sem=sem, inc=16)

    def build(self):
        block = self.es.enter_context(self.nc.Block())
        for eng in self.ENGINES:
            ops = self.q[eng]
            if not ops:
                continue

            def body(e, ops=ops):
                for fn, waits, sem, inc in ops:
                    for s, v in waits:
                        e.wait_ge(self.sems[s], v)
                    ins = fn(e)
                    if sem is not None:
                        ins.then_inc(self.sems[sem], inc)
            getattr(block, eng)(body)
        self.es.close()
        return self.nc


class BankPool:
    def __init__(self, p, prefix, n, free=512):
        self.p = p
        self.banks = [p.psum(f"{prefix}{i}", 128, free) for i in range(n)]
        self.free_at = [() for _ in self.banks]
        self.i = 0

    def acquire(self):
        b = self.i % len(self.banks)
        self.i += 1
        return b, self.banks[b], self.free_at[b]

    def release(self, b, waits):
        self.free_at[b] = tuple(waits)



def setup_consts(p, fvals=()):
    c = {}
    c["ones_bf"] = p.sb("ones_bf", 128, 1, BF16)
    c["ones1f"] = p.sb("ones1f", 1, 128, F32)
    p.emit("vector", lambda e: e.memset(c["ones_bf"][:, :], 1.0), sem="ve")
    p.emit("vector", lambda e: e.memset(c["ones1f"][:, :], 1.0), sem="ve")
    for i, fv in enumerate(fvals):
        t = p.sb(f"cf{i}", 128, 1, F32)
        p.emit("vector", lambda e, t=t, fv=fv: e.memset(t[:, :], fv), sem="ve")
        p.nc.const_aps.register(F32, float(fv), t[:, :])
    c["ready"] = ("ve", p.v("ve"))
    return c


def make_ln_res(p, Tmax):
    r = {}
    r["Tmax"] = Tmax
    r["sq"] = [p.sb("ln_sq0", 128, 512, BF16), p.sb("ln_sq1", 128, 512, BF16)]
    rows = p.sb("ln_rows", 1, 4 * Tmax, F32)
    r["srow"] = rows[0:1, 0:Tmax]
    r["qrow"] = rows[0:1, Tmax:2 * Tmax]
    r["tmp"] = rows[0:1, 2 * Tmax:3 * Tmax]
    r["mrrow"] = rows[0:1, 3 * Tmax:4 * Tmax]
    r["mb"] = p.sb("ln_mb", 128, Tmax, F32)
    r["rb"] = p.sb("ln_rb", 128, Tmax, F32)
    r["ps_s"] = p.psum("ln_ps_s", 128, 512)
    r["ps_q"] = p.psum("ln_ps_q", 128, 512)
    r["ps_b"] = p.psum("ln_ps_b", 128, 512)
    return r


def emit_ln(p, xb, T, consts, lr, *, xb_waits, z_out, n_kc=NKC, cstride=None,
            off=0):
    if cstride is None:
        cstride = T
    n_tc = T // 512
    bar = p.barrier()

    def C(kc, c0, c1):
        return (kc * cstride + off + c0, kc * cstride + off + c1)

    ev_s = ev_q = None
    sqv_prev = {}
    for tc in range(n_tc):
        for kc in range(n_kc):
            w = list(xb_waits) + [consts["ready"]] + bar if kc == 0 else []
            if kc == 0 and ev_s is not None:
                w.append(("ac", ev_s))
            p.emit("tensor",
                   lambda e, kc=kc, tc=tc: e.matmul(
                       lr["ps_s"][0:1, :],
                       consts["ones_bf"][:, :],
                       xb[:, slice(*C(kc, tc * 512, (tc + 1) * 512))],
                       start=(kc == 0), stop=(kc == n_kc - 1)),
                   waits=w, sem="pe")
        ev_s = p.emit("scalar",
                      lambda e, tc=tc: e.activation(
                          lr["srow"][0:1, tc * 512:(tc + 1) * 512],
                          lr["ps_s"][0:1, :], AF.Copy),
                      waits=[("pe", p.v("pe"))], sem="ac")
        for kc in range(n_kc):
            sqb = lr["sq"][kc % 2]
            w = list(xb_waits) + bar
            if (tc, kc) in sqv_prev or (tc > 0 or kc >= 2):
                prev = sqv_prev.get((kc % 2))
                if prev is not None:
                    w.append(("pe", prev))
            sv = p.emit("scalar",
                        lambda e, kc=kc, tc=tc, sqb=sqb: e.activation(
                            sqb[:, :],
                            xb[:, slice(*C(kc, tc * 512, (tc + 1) * 512))],
                            AF.Square),
                        waits=w, sem="ac")
            w2 = [("ac", sv)]
            if kc == 0 and ev_q is not None:
                w2.append(("ac", ev_q))
            mv = p.emit("tensor",
                        lambda e, kc=kc, sqb=sqb: e.matmul(
                            lr["ps_q"][0:1, :], consts["ones_bf"][:, :],
                            sqb[:, :], start=(kc == 0), stop=(kc == n_kc - 1)),
                        waits=w2, sem="pe")
            sqv_prev[kc % 2] = mv
        ev_q = p.emit("scalar",
                      lambda e, tc=tc: e.activation(
                          lr["qrow"][0:1, tc * 512:(tc + 1) * 512],
                          lr["ps_q"][0:1, :], AF.Copy),
                      waits=[("pe", p.v("pe"))], sem="ac")

    v3 = p.emit("scalar", lambda e: e.mul(lr["srow"][0:1, 0:T], lr["srow"][0:1, 0:T],
                                          1.0 / D),
                waits=[("ac", ev_s), ("ac", ev_q)], sem="ac")
    v4 = p.emit("vector", lambda e: e.tensor_mul(lr["tmp"][0:1, 0:T],
                                                 lr["srow"][0:1, 0:T],
                                                 lr["srow"][0:1, 0:T]),
                waits=[("ac", v3)], sem="ve")
    v5a = p.emit("vector",
                 lambda e: e.tensor_scalar(lr["qrow"][0:1, 0:T],
                                           lr["qrow"][0:1, 0:T], 1.0 / D,
                                           EPS_LN, ALU.mult, ALU.add),
                 waits=[], sem="ve")
    v5 = p.emit("vector",
                lambda e: e.tensor_sub(lr["qrow"][0:1, 0:T], lr["qrow"][0:1, 0:T],
                                       lr["tmp"][0:1, 0:T]),
                waits=[], sem="ve")
    v6 = p.emit("scalar", lambda e: e.activation(lr["qrow"][0:1, 0:T],
                                                 lr["qrow"][0:1, 0:T], AF.Sqrt),
                waits=[("ve", v5)], sem="ac")
    v6b = p.emit("vector", lambda e: e.tensor_copy(lr["tmp"][0:1, 0:T],
                                                   lr["srow"][0:1, 0:T]),
                 sem="ve")
    v7 = p.emit("vector", lambda e: e.reciprocal(lr["qrow"][0:1, 0:T],
                                                 lr["qrow"][0:1, 0:T]),
                waits=[("ac", v6)], sem="ve")
    v7b = p.emit("vector", lambda e: e.memset(lr["ps_b"] if False else
                                              lr["mb"][0:1, 0:8], 0.0),
                 sem="ve")
    v8 = p.emit("vector",
                lambda e: e.scalar_tensor_tensor(lr["mrrow"][0:1, 0:T],
                                                 lr["tmp"][0:1, 0:T], -1.0,
                                                 lr["qrow"][0:1, 0:T],
                                                 ALU.mult, ALU.mult),
                waits=[("ve", v7)], sem="ve")

    eprev = None
    for tc in range(n_tc):
        w = [consts["ready"], ("ve", v8)]
        if eprev is not None:
            w.append(("ac", eprev))
        g1 = p.emit("tensor",
                    lambda e, tc=tc: e.matmul(lr["ps_b"][:, :],
                                              consts["ones1f"][:, :],
                                              lr["qrow"][0:1, tc * 512:(tc + 1) * 512],
                                              start=True, stop=True),
                    waits=w, sem="pe")
        e1 = p.emit("scalar",
                    lambda e, tc=tc: e.activation(
                        lr["rb"][:, tc * 512:(tc + 1) * 512], lr["ps_b"][:, :],
                        AF.Copy),
                    waits=[("pe", g1)], sem="ac")
        g2 = p.emit("tensor",
                    lambda e, tc=tc: e.matmul(lr["ps_b"][:, :],
                                              consts["ones1f"][:, :],
                                              lr["mrrow"][0:1, tc * 512:(tc + 1) * 512],
                                              start=True, stop=True),
                    waits=[("ac", e1)], sem="pe")
        e2 = p.emit("scalar",
                    lambda e, tc=tc: e.activation(
                        lr["mb"][:, tc * 512:(tc + 1) * 512], lr["ps_b"][:, :],
                        AF.Copy),
                    waits=[("pe", g2)], sem="ac")
        eprev = e2
    bcast_done = ("ac", eprev)

    z_ready = []
    for kc in range(n_kc):
        p.emit("vector",
               lambda e, kc=kc: e.tensor_mul(z_out[:, slice(*C(kc, 0, T))],
                                             xb[:, slice(*C(kc, 0, T))],
                                             lr["rb"][:, 0:T]),
               waits=[bcast_done], sem="ve")
        vz = p.emit("vector",
                    lambda e, kc=kc: e.tensor_add(z_out[:, slice(*C(kc, 0, T))],
                                                  z_out[:, slice(*C(kc, 0, T))],
                                                  lr["mb"][:, 0:T]),
                    sem="ve")
        z_ready.append(("ve", vz))
    return z_ready, {"mb": lr["mb"], "rb": lr["rb"]}


def emit_gemm(p, tag, *, wdram, k_tile, n_kc, m_tile, n_mc, rhs_fn, T, pool,
              evac, rhs_waits=(), wbuf=None):
    n_tc = T // 512
    if wbuf is None:
        wbuf = [p.sb(f"w{tag}{i}", k_tile, n_kc * m_tile, BF16) for i in range(2)]
    nb = len(wbuf)
    slab_ready = [None] * n_mc
    group_end = [None] * n_mc
    pe0 = ("pe", p.v("pe"))

    def issue_dma(mc):
        w = [pe0] if mc < nb else [("pe", group_end[mc - nb])]
        slab_ready[mc] = p.dma("gpsimd", wbuf[mc % nb][:, :], wdram[mc], waits=w,
                               sem="wdma")

    for mc in range(min(nb, n_mc)):
        issue_dma(mc)

    for mc in range(n_mc):
        for tc in range(n_tc):
            bidx, bank, bfree = pool.acquire()
            for kc in range(n_kc):
                w = []
                if kc == 0:
                    w = [("wdma", slab_ready[mc])] + list(bfree) + list(rhs_waits)
                p.emit("tensor",
                       lambda e, mc=mc, tc=tc, kc=kc, bank=bank: e.matmul(
                           bank[0:m_tile, :],
                           wbuf[mc % nb][0:k_tile, kc * m_tile:(kc + 1) * m_tile],
                           rhs_fn(kc)[:, tc * 512:(tc + 1) * 512],
                           start=(kc == 0), stop=(kc == n_kc - 1)),
                       waits=w, sem="pe")
            done = evac(mc, tc, bank[0:m_tile, :], [("pe", p.v("pe"))])
            pool.release(bidx, [done])
        group_end[mc] = p.v("pe")
        if mc + nb < n_mc:
            issue_dma(mc + nb)
    return group_end[-1]


def evac_to(p, out_fn, func=AF.Copy, bias_sb=None, scale=1.0):
    if bias_sb is None:
        def ev(mc, tc, bank_ap, waits):
            v = p.emit("scalar",
                       lambda e, mc=mc, tc=tc, bank_ap=bank_ap: e.activation(
                           out_fn(mc, tc), bank_ap, func, bias=0.0, scale=scale),
                       waits=waits, sem="ac")
            return ("ac", v)
    else:
        assert func is AF.Copy and scale == 1.0

        def ev(mc, tc, bank_ap, waits):
            v = p.emit("vector",
                       lambda e, mc=mc, tc=tc, bank_ap=bank_ap: e.tensor_scalar(
                           out_fn(mc, tc), bank_ap, bias_sb[:, mc:mc + 1], None,
                           ALU.add),
                       waits=waits, sem="ve")
            return ("ve", v)
    return ev


def emit_sandwich_close(p, tag, inner_b, T, consts, lr, *, inner_waits, x_dram,
                        out_dram, n_kc=NKC, bufs=None, cstride=None, off=0):
    if cstride is None:
        cstride = T
    zw, _ = emit_ln(p, inner_b, T, consts, lr, xb_waits=inner_waits,
                    z_out=inner_b, n_kc=n_kc, cstride=cstride, off=off)
    if bufs is None:
        xch = [p.sb(f"xc{tag}{i}", 128, T, F32) for i in range(2)]
        och = [p.sb(f"oc{tag}{i}", 128, T, F32) for i in range(2)]
    else:
        xch, och = bufs
    st0 = ("stdma", p.v("stdma"))
    add_v = {}
    st_v = {}
    for kc in range(n_kc):
        w_x = list(inner_waits) if kc < 2 else []
        if kc >= 2:
            w_x.append(("ve", add_v[kc - 2]))
        xv = p.dma("sync", xch[kc % 2][:, :],
                   x_dram.rearrange("(kc p) t -> kc p t", p=128)[kc],
                   waits=w_x, sem="xre")
        w_a = [zw[kc], ("xre", xv)]
        w_a.append(("stdma", st_v[kc - 2]) if kc >= 2 else st0)
        av = p.emit("vector",
                    lambda e, kc=kc: e.tensor_add(
                        och[kc % 2][:, :], xch[kc % 2][:, :],
                        inner_b[:, kc * cstride + off: kc * cstride + off + T]),
                    waits=w_a, sem="ve")
        add_v[kc] = av
        st_v[kc] = p.dma("sync",
                         out_dram.rearrange("(kc p) t -> kc p t", p=128)[kc],
                         och[kc % 2][:, :], waits=[("ve", av)], sem="stdma")
    return st_v[n_kc - 1]


def finish(p):
    p.emit("sync", lambda e: e.nop(), waits=[("stdma", p.v("stdma"))])
    return p.build()



def build_l1ffn():
    p = P("l1ffn")
    TH = 1024
    x_d = p.din("x", [D, TPC])
    w1_d = p.din("w1", [32, 128, 8 * 128])
    b1_d = p.din("b1", [128, 32])
    w2_d = p.din("w2", [8, 128, 32 * 128])
    b2_d = p.din("b2", [128, 8])
    out_d = p.dout("out", [D, TPC])

    consts = setup_consts(p)
    lr = make_ln_res(p, TH)
    pool = BankPool(p, "gb", 4)
    b1_s = p.sb("b1s", 128, 32)
    b2_s = p.sb("b2s", 128, 8)
    p.dma("sync", b1_s[:, :], b1_d)
    bv = p.dma("sync", b2_s[:, :], b2_d)
    bias_ready = ("syncdma", bv)

    xb = p.sb("xb", 128, NKC * TH, BF16)
    z = p.sb("z", 128, NKC * TH, BF16)
    h1 = p.sb("h1", 128, 32 * TH, BF16)
    inner = p.sb("inner", 128, NKC * TH, BF16)
    w1buf = [p.sb(f"w1b{i}", 128, 8 * 128, BF16) for i in range(2)]
    w2buf = [p.sb(f"w2b{i}", 128, 32 * 128, BF16) for i in range(2)]
    clbufs = ([p.sb("clx0", 128, TH, F32), p.sb("clx1", 128, TH, F32)],
              [p.sb("clo0", 128, TH, F32), p.sb("clo1", 128, TH, F32)])

    for half in range(2):
        xs = x_d[:, half * TH:(half + 1) * TH]
        os_ = out_d[:, half * TH:(half + 1) * TH]
        bar = p.barrier()
        xv = p.dma("gpsimd", xb[:, :].rearrange("p (kc t) -> p kc t", kc=NKC),
                   xs.rearrange("(kc p) t -> p kc t", p=128), waits=bar,
                   sem="xldc")
        zw, _ = emit_ln(p, xb, TH, consts, lr, xb_waits=[("xldc", xv)], z_out=z)

        def ev1(mc, tc, bank_ap, waits):
            v = p.emit("vector",
                       lambda e, mc=mc, tc=tc, bank_ap=bank_ap: e.tensor_scalar(
                           h1[:, mc * TH + tc * 512: mc * TH + (tc) * 512 + 512],
                           bank_ap, b1_s[:, mc:mc + 1], None, ALU.add),
                       waits=list(waits) + [bias_ready], sem="ve")
            return ("ve", v)

        emit_gemm(p, f"g1h{half}", wdram=w1_d, k_tile=128, n_kc=8, m_tile=128,
                  n_mc=32, rhs_fn=lambda kc: z[:, kc * TH:(kc + 1) * TH],
                  T=TH, pool=pool, evac=ev1, rhs_waits=zw, wbuf=w1buf)
        h1_done = [("ve", p.v("ve"))]

        def ev2(mc, tc, bank_ap, waits):
            v = p.emit("vector",
                       lambda e, mc=mc, tc=tc, bank_ap=bank_ap: e.tensor_scalar(
                           inner[:, mc * TH + tc * 512: mc * TH + tc * 512 + 512],
                           bank_ap, b2_s[:, mc:mc + 1], None, ALU.add),
                       waits=list(waits) + [bias_ready], sem="ve")
            return ("ve", v)

        emit_gemm(p, f"g2h{half}", wdram=w2_d, k_tile=128, n_kc=32, m_tile=128,
                  n_mc=8, rhs_fn=lambda kc: h1[:, kc * TH:(kc + 1) * TH],
                  T=TH, pool=pool, evac=ev2, rhs_waits=h1_done, wbuf=w2buf)
        inner_waits = [("ve", p.v("ve"))]
        emit_sandwich_close(p, f"cl{half}", inner, TH, consts, lr,
                            inner_waits=inner_waits, x_dram=xs, out_dram=os_,
                            bufs=clbufs)
    return finish(p)


def host_l1ffn(inputs, stream):
    W1, b1 = inputs["L1_ffn_W1"], inputs["L1_ffn_b1"]
    W2, b2 = inputs["L1_ffn_W2"], inputs["L1_ffn_b2"]
    in_maps = []
    for c in range(NCORES):
        in_maps.append({
            "x": np.ascontiguousarray(stream[c]),
            "w1": mcform(np.ascontiguousarray(W1[c].T)),
            "b1": biasform(b1[c]),
            "w2": mcform(np.ascontiguousarray(W2[c].T)),
            "b2": biasform(b2[c]),
        })
    return in_maps


def run_l1ffn(inputs, stream):
    nc = build_l1ffn()
    res = run_bass_kernel_spmd(nc, host_l1ffn(inputs, stream),
                               core_ids=list(range(NCORES)))
    return [np.asarray(r["out"]) for r in res.results]



def build_l0ffn():
    p = P("l0ffn")
    T = TPC
    x_d = p.din("x", [D, T])
    bp_d = p.din("bigp", [2, 4, 128, 2048])
    bq_d = p.din("bigq", [2, 4, 128, 2048])
    bias_d = p.din("bias", [128, 8])
    out_d = p.dout("out", [D, T])

    consts = setup_consts(p)
    lr = make_ln_res(p, T)
    pool = BankPool(p, "gb", 4)

    bias_s = p.sb("bias_s", 128, 8)
    p.dma("sync", bias_s[:, :], bias_d)
    bias_ready = ("syncdma", p.v("syncdma"))

    bp_s = [p.sb(f"bp_s{i}", 128, 2048) for i in range(2)]
    bq_s = [p.sb(f"bq_s{i}", 128, 2048) for i in range(2)]
    tl = {}
    tmp = p.sb("tl_tmp", 128, 2048)
    acc = p.sb("tl_acc", 128, 2048)
    tl["r"] = p.sb("tl_r", 128, 2048, BF16)
    tl["i"] = p.sb("tl_i", 128, 2048, BF16)
    tl["in"] = p.sb("tl_in", 128, 2048, BF16)
    tl_ready = None
    for part, nm in ((0, "r"), (1, "i")):
        for cc in range(4):
            bw = [("ve", p.v("ve"))] if (part or cc >= 2) else []
            p.dma("sync", bp_s[cc % 2][:, :], bp_d[part][cc], waits=bw,
                  sem="syncdma")
            lv = p.dma("sync", bq_s[cc % 2][:, :], bq_d[part][cc], waits=bw,
                       sem="syncdma")
            w = [("syncdma", lv)]
            if cc == 0:
                p.emit("vector",
                       lambda e, c2=cc % 2: e.tensor_mul(acc[:, :],
                                                         bp_s[c2][:, :],
                                                         bq_s[c2][:, :]),
                       waits=w, sem="ve")
            else:
                p.emit("vector",
                       lambda e, c2=cc % 2: e.tensor_mul(tmp[:, :],
                                                         bp_s[c2][:, :],
                                                         bq_s[c2][:, :]),
                       waits=w, sem="ve")
                p.emit("vector", lambda e: e.tensor_add(acc[:, :], acc[:, :],
                                                        tmp[:, :]), sem="ve")
        p.emit("vector", lambda e, nm=nm: e.tensor_copy(tl[nm][:, :], acc[:, :]),
               sem="ve")
        if nm == "i":
            tl_ready = p.emit("vector",
                              lambda e: e.tensor_scalar_mul(tl["in"][:, :],
                                                            acc[:, :], -1.0),
                              sem="ve")
        else:
            tl_ready = p.v("ve")
    tl_wait = ("ve", tl_ready)

    def lhsT_fn(kc, mc):
        if kc < 4 and mc < 4:
            return tl["r"][:, kc * 512 + mc * 128: kc * 512 + mc * 128 + 128]
        if kc < 4:
            return tl["i"][:, kc * 512 + (mc - 4) * 128: kc * 512 + (mc - 4) * 128 + 128]
        if mc < 4:
            return tl["in"][:, (kc - 4) * 512 + mc * 128: (kc - 4) * 512 + mc * 128 + 128]
        return tl["r"][:, (kc - 4) * 512 + (mc - 4) * 128: (kc - 4) * 512 + (mc - 4) * 128 + 128]

    xb = p.sb("xb", 128, NKC * T, BF16)
    z = xb
    inner = p.sb("inner", 128, NKC * T, BF16)

    xv = p.dma("gpsimd", xb[:, :].rearrange("p (kc t) -> p kc t", kc=NKC),
               x_d.rearrange("(kc p) t -> p kc t", p=128), sem="xldc")
    zw, _ = emit_ln(p, xb, T, consts, lr, xb_waits=[("xldc", xv)], z_out=z)

    def ev(mc, tc, bank_ap, waits):
        v = p.emit("vector",
                   lambda e, mc=mc, tc=tc, bank_ap=bank_ap: e.tensor_scalar(
                       inner[:, mc * T + tc * 512: mc * T + tc * 512 + 512],
                       bank_ap, bias_s[:, mc:mc + 1], None, ALU.add),
                   waits=list(waits) + [bias_ready], sem="ve")
        return ("ve", v)

    n_tc = T // 512
    for mc in range(8):
        banks = []
        for tc in range(n_tc):
            bidx, bank, bfree = pool.acquire()
            banks.append((bidx, bank))
            for kc in range(8):
                w = []
                if kc == 0:
                    w = [tl_wait] + list(bfree) + zw
                p.emit("tensor",
                       lambda e, mc=mc, tc=tc, kc=kc, bank=bank: e.matmul(
                           bank[:, :], lhsT_fn(kc, mc),
                           z[:, kc * T + tc * 512: kc * T + (tc + 1) * 512],
                           start=(kc == 0), stop=(kc == 7)),
                       waits=w, sem="pe")
        ge = p.v("pe")
        for tc, (bidx, bank) in enumerate(banks):
            done = ev(mc, tc, bank[:, :], [("pe", ge)])
            pool.release(bidx, [done])

    emit_sandwich_close(p, "cl", inner, T, consts, lr,
                        inner_waits=[("ve", p.v("ve"))], x_dram=x_d, out_dram=out_d,
                        bufs=([bp_s[0], bp_s[1]], [tmp, acc]))
    return finish(p)


def host_l0ffn(inputs, stream):
    G1R, G2R = inputs["L0_ffn_G1_R"], inputs["L0_ffn_G2_R"]
    G1I, G2I = inputs["L0_ffn_G1_I"], inputs["L0_ffn_G2_I"]
    br, bi = inputs["L0_ffn_bias_real"], inputs["L0_ffn_bias_imag"]

    def expand(G1, G2):
        A = G1[0].transpose(1, 2, 0)
        Q = G2[:, :, 0, :].transpose(0, 2, 1)
        bigp = np.empty((4, 128, 2048), np.float32)
        bigq = np.empty((4, 128, 2048), np.float32)
        for c in range(4):
            fp = np.repeat(np.repeat(A[c], 16, axis=0), 16, axis=1)
            fq = np.tile(np.tile(Q[c], (32, 1)), (1, 32))
            bigp[c] = fp.reshape(4, 128, 512).transpose(1, 0, 2).reshape(128, 2048)
            bigq[c] = fq.reshape(4, 128, 512).transpose(1, 0, 2).reshape(128, 2048)
        return bigp, bigq

    bpr, bqr = expand(G1R, G2R)
    bpi, bqi = expand(G1I, G2I)
    bigp = np.stack([bpr, bpi]).astype(np.float32)
    bigq = np.stack([bqr, bqi]).astype(np.float32)
    bias = biasform(np.concatenate([br, bi]))
    in_maps = []
    for c in range(NCORES):
        in_maps.append({"x": np.ascontiguousarray(stream[c]),
                        "bigp": bigp, "bigq": bigq, "bias": bias})
    return in_maps


def run_l0ffn(inputs, stream):
    nc = build_l0ffn()
    res = run_bass_kernel_spmd(nc, host_l0ffn(inputs, stream),
                               core_ids=list(range(NCORES)))
    return [np.asarray(r["out"]) for r in res.results]



def build_l2ffn():
    p = P("l2ffn")
    TH = 1024
    x_d = p.din("x", [D, TPC])
    w1_d = p.din("w1", [32, 128, 8 * 128])
    w2_d = p.din("w2", [32, 128, 8 * 128])
    h1_d = p.din("h1", [32, 128, 32 * 128])
    h2_d = p.din("h2", [32, 128, 32 * 128])
    w3_d = p.din("w3", [8, 128, 32 * 128])
    s_d = p.din("scales", [128, 32 * 4 + 8])
    out_d = p.dout("out", [D, TPC])

    consts = setup_consts(p)
    lr = make_ln_res(p, TH)
    pool = BankPool(p, "gb", 4)
    s_s = p.sb("s_s", 128, 32 * 4 + 8)
    p.dma("sync", s_s[:, :], s_d)
    s_ready = ("syncdma", p.v("syncdma"))

    xb = p.sb("xb", 128, NKC * TH, BF16)
    z = xb
    ha = p.sb("ha", 128, 32 * TH, BF16)
    hb = p.sb("hb", 128, 32 * TH, BF16)
    wbufA = [p.sb(f"wA{i}", 128, 8 * 128, BF16) for i in range(2)]
    wbufB = [p.sb(f"wB{i}", 128, 32 * 128, BF16) for i in range(2)]
    clbufs = ([p.sb("clx0", 128, TH, F32), p.sb("clx1", 128, TH, F32)],
              [p.sb("clo0", 128, TH, F32), p.sb("clo1", 128, TH, F32)])

    for half in range(2):
        xs = x_d[:, half * TH:(half + 1) * TH]
        os_ = out_d[:, half * TH:(half + 1) * TH]
        bar = p.barrier()
        xv = p.dma("gpsimd", xb[:, :].rearrange("p (kc t) -> p kc t", kc=NKC),
                   xs.rearrange("(kc p) t -> p kc t", p=128), waits=bar,
                   sem="xldc")
        zw, _ = emit_ln(p, xb, TH, consts, lr, xb_waits=[("xldc", xv)], z_out=z)

        def ev_w1(mc, tc, bank_ap, waits):
            v = p.emit("scalar",
                       lambda e, mc=mc, tc=tc, bank_ap=bank_ap: e.activation(
                           ha[:, mc * TH + tc * 512: mc * TH + tc * 512 + 512],
                           bank_ap, AF.Silu, scale=s_s[:, mc:mc + 1]),
                       waits=list(waits) + [s_ready], sem="ac")
            return ("ac", v)

        emit_gemm(p, f"w1h{half}", wdram=w1_d, k_tile=128, n_kc=8, m_tile=128,
                  n_mc=32, rhs_fn=lambda kc: z[:, kc * TH:(kc + 1) * TH],
                  T=TH, pool=pool, evac=ev_w1, rhs_waits=zw, wbuf=wbufA)
        ha_done = ("ac", p.v("ac"))

        def ev_w2(mc, tc, bank_ap, waits):
            sl = ha[:, mc * TH + tc * 512: mc * TH + tc * 512 + 512]
            v = p.emit("vector",
                       lambda e, sl=sl, mc=mc, bank_ap=bank_ap: e.scalar_tensor_tensor(
                           sl, bank_ap, s_s[:, 32 + mc:32 + mc + 1], sl,
                           ALU.mult, ALU.mult),
                       waits=list(waits) + [s_ready, ha_done], sem="ve")
            return ("ve", v)

        emit_gemm(p, f"w2h{half}", wdram=w2_d, k_tile=128, n_kc=8, m_tile=128,
                  n_mc=32, rhs_fn=lambda kc: z[:, kc * TH:(kc + 1) * TH],
                  T=TH, pool=pool, evac=ev_w2, rhs_waits=zw, wbuf=wbufA)
        h_done = [("ve", p.v("ve"))]

        def ev_h1(mc, tc, bank_ap, waits):
            v = p.emit("scalar",
                       lambda e, mc=mc, tc=tc, bank_ap=bank_ap: e.activation(
                           hb[:, mc * TH + tc * 512: mc * TH + tc * 512 + 512],
                           bank_ap, AF.Gelu, scale=s_s[:, 64 + mc:64 + mc + 1]),
                       waits=list(waits) + [s_ready], sem="ac")
            return ("ac", v)

        emit_gemm(p, f"h1h{half}", wdram=h1_d, k_tile=128, n_kc=32, m_tile=128,
                  n_mc=32, rhs_fn=lambda kc: ha[:, kc * TH:(kc + 1) * TH],
                  T=TH, pool=pool, evac=ev_h1, rhs_waits=h_done, wbuf=wbufB)
        hb_done = [("ac", p.v("ac"))]

        def ev_h2(mc, tc, bank_ap, waits):
            v = p.emit("scalar",
                       lambda e, mc=mc, tc=tc, bank_ap=bank_ap: e.activation(
                           ha[:, mc * TH + tc * 512: mc * TH + tc * 512 + 512],
                           bank_ap, AF.Gelu, scale=s_s[:, 96 + mc:96 + mc + 1]),
                       waits=list(waits) + [s_ready], sem="ac")
            return ("ac", v)

        emit_gemm(p, f"h2h{half}", wdram=h2_d, k_tile=128, n_kc=32, m_tile=128,
                  n_mc=32, rhs_fn=lambda kc: hb[:, kc * TH:(kc + 1) * TH],
                  T=TH, pool=pool, evac=ev_h2, rhs_waits=hb_done, wbuf=wbufB)
        h2_done = [("ac", p.v("ac"))]

        def ev_w3(mc, tc, bank_ap, waits):
            v = p.emit("vector",
                       lambda e, mc=mc, tc=tc, bank_ap=bank_ap: e.tensor_scalar(
                           z[:, mc * TH + tc * 512: mc * TH + tc * 512 + 512],
                           bank_ap, s_s[:, 128 + mc:128 + mc + 1], None, ALU.mult),
                       waits=list(waits) + [s_ready], sem="ve")
            return ("ve", v)

        emit_gemm(p, f"w3h{half}", wdram=w3_d, k_tile=128, n_kc=32, m_tile=128,
                  n_mc=8, rhs_fn=lambda kc: ha[:, kc * TH:(kc + 1) * TH],
                  T=TH, pool=pool, evac=ev_w3, rhs_waits=h2_done, wbuf=wbufB)

        emit_sandwich_close(p, f"cl{half}", z, TH, consts, lr,
                            inner_waits=[("ve", p.v("ve"))], x_dram=xs,
                            out_dram=os_, bufs=clbufs)
    return finish(p)


def host_l2ffn(inputs, stream):
    scales = np.concatenate([
        biasform(inputs["L2_ffn_w1_s"]), biasform(inputs["L2_ffn_w2_s"]),
        biasform(inputs["L2_ffn_h1_s"]), biasform(inputs["L2_ffn_h2_s"]),
        biasform(inputs["L2_ffn_w3_s"])], axis=1)
    shared = {
        "w1": mcform(np.ascontiguousarray(inputs["L2_ffn_w1_w"].T)),
        "w2": mcform(np.ascontiguousarray(inputs["L2_ffn_w2_w"].T)),
        "h1": mcform(np.ascontiguousarray(inputs["L2_ffn_h1_w"].T)),
        "h2": mcform(np.ascontiguousarray(inputs["L2_ffn_h2_w"].T)),
        "w3": mcform(np.ascontiguousarray(inputs["L2_ffn_w3_w"].T)),
        "scales": np.ascontiguousarray(scales),
    }
    return [dict(shared, x=np.ascontiguousarray(stream[c])) for c in range(NCORES)]


def run_l2ffn(inputs, stream):
    nc = build_l2ffn()
    res = run_bass_kernel_spmd(nc, host_l2ffn(inputs, stream),
                               core_ids=list(range(NCORES)))
    return [np.asarray(r["out"]) for r in res.results]



def build_l3ffn(debug=False):
    p = P("l3ffn")
    TH = 1024
    NTCH = TH // 128
    x_d = p.din("x", [D, TPC])
    xtm_d = p.din("xtm", [TPC, D])
    wr_d = p.din("wr", [8, 128, 16])
    sg_d = p.din("sg", [8, 128, 8 * 128])
    su_d = p.din("su", [8, 128, 8 * 128])
    sd_d = p.din("sd", [8, 128, 1024])
    wg_d = p.din("wg", [16, 4, 128, 8 * 128])
    wu_d = p.din("wu", [16, 4, 128, 8 * 128])
    wd_d = p.din("wd", [16, 4, 128, 1024])
    out_d = p.dout("out", [TPC, D])

    consts = setup_consts(p)
    lr = make_ln_res(p, TH)
    pool = BankPool(p, "gb", 4)

    xb = p.sb("xb", 128, NKC * TH, BF16)
    z = xb
    hsh = p.sb("hsh", 128, NKC * TH, BF16)
    hga = p.sb("hga", 128, 4 * TH, BF16)
    acc = p.sb("acc", 128, NTCH * D)
    prow = p.sb("prow", 128, NTCH * 16)
    indt = p.sb("indt", 128, NTCH * 16)
    cwt = p.sb("cwt", 128, NTCH * 16)
    statB = p.sb("statB", 128, NTCH)
    statC = p.sb("statC", 128, NTCH)
    statD = p.sb("statD", 128, NTCH)
    sgbuf = [p.sb(f"sgb{i}", 128, 8 * 128, BF16) for i in range(2)]
    wgbuf = [p.sb(f"wgb{i}", 128, 8 * 128, BF16) for i in range(2)]
    sdbuf = p.sb("sdb", 128, 8 * 1024, BF16)
    wdbuf = [p.sb(f"wdb{i}", 128, 4 * 1024, BF16) for i in range(2)]
    wrbuf = p.sb("wrbuf", 128, 8 * 16, BF16)
    scol = p.sb("scol", 128, NTCH)
    qcol = p.sb("qcol", 128, NTCH)
    mcol = p.sb("mcol", 128, NTCH)
    rcol = p.sb("rcol", 128, NTCH)
    mrcol = p.sb("mrcol", 128, NTCH)
    trash = p.sb("trash", 128, D)
    xch = [p.sb(f"xtmc{i}", 128, D) for i in range(2)]
    och = [p.sb(f"otmc{i}", 128, D) for i in range(2)]

    dbg_prow = dbg_cwt = dbg_acc = None
    if debug:
        dbg_prow = p.dout("dbg_prow", [128, 8 * 16])
        dbg_cwt = p.dout("dbg_cwt", [128, 8 * 16])
        dbg_acc = p.dout("dbg_acc", [128, 8 * 1024])
        dbg_st = p.dout("dbg_st", [128, 40])
    wrv = p.dma("gpsimd", wrbuf[:, :].rearrange("p (kc f) -> p kc f", kc=8),
                wr_d.rearrange("kc p f -> p kc f"), sem="wdma")
    sdv = p.dma("gpsimd", sdbuf[:, :].rearrange("p (kc f) -> p kc f", kc=8),
                sd_d.rearrange("kc p f -> p kc f"), sem="wdma")

    def pch(tile, tch, w=16):
        return tile[:, tch * w:(tch + 1) * w]

    for half in range(2):
        bar = p.barrier()
        xv = p.dma("gpsimd", xb[:, :].rearrange("p (kc t) -> p kc t", kc=NKC),
                   x_d[:, half * TH:(half + 1) * TH]
                   .rearrange("(kc p) t -> p kc t", p=128),
                   waits=bar, sem="xldc")
        zw, _ = emit_ln(p, xb, TH, consts, lr, xb_waits=[("xldc", xv)], z_out=z)

        for tch in range(NTCH):
            bidx, bank, bfree = pool.acquire()
            for kc in range(NKC):
                w = []
                if kc == 0:
                    w = [("wdma", wrv)] + list(bfree) + zw
                p.emit("tensor",
                       lambda e, tch=tch, kc=kc, bank=bank: e.matmul(
                           bank[:, 0:16],
                           z[:, kc * TH + tch * 128: kc * TH + (tch + 1) * 128],
                           wrbuf[:, kc * 16:(kc + 1) * 16],
                           start=(kc == 0), stop=(kc == NKC - 1)),
                       waits=w, sem="pe")
            ev = p.emit("scalar",
                        lambda e, tch=tch, bank=bank: e.activation(
                            pch(prow, tch), bank[:, 0:16], AF.Copy),
                        waits=[("pe", p.v("pe"))], sem="ac")
            pool.release(bidx, [("ac", ev)])
        router_done = ("ac", p.v("ac"))

        for tch in range(NTCH):
            p.emit("vector",
                   lambda e, tch=tch: e.reduce_max(statB[:, tch:tch + 1],
                                                   pch(prow, tch),
                                                   axis=AX.X, negate=True),
                   waits=[router_done], sem="ve")
        vb = p.v("ve")
        for tch in range(NTCH):
            p.emit("scalar",
                   lambda e, tch=tch: e.activation(pch(prow, tch), pch(prow, tch),
                                                   AF.Exp,
                                                   bias=statB[:, tch:tch + 1],
                                                   scale=1.0,
                                                   accum_out=statC[:, tch:tch + 1]),
                   waits=[("ve", vb)], sem="ac")
        va = p.v("ac")
        p.emit("vector", lambda e: e.reciprocal(statC[:, 0:NTCH], statC[:, 0:NTCH]),
               waits=[("ac", va)], sem="ve")
        p.emit("vector", lambda e: e.memset(statD[:, :], 0.0), sem="ve")
        for tch in range(NTCH):
            p.emit("vector",
                   lambda e, tch=tch: e.tensor_scalar(pch(prow, tch),
                                                      pch(prow, tch),
                                                      statC[:, tch:tch + 1], None,
                                                      ALU.mult),
                   sem="ve")
        for tch in range(NTCH):
            p.emit("vector",
                   lambda e, tch=tch: e.reduce_max(statB[:, tch:tch + 1],
                                                   pch(prow, tch), axis=AX.X),
                   sem="ve")
        for tch in range(NTCH):
            p.emit("vector",
                   lambda e, tch=tch: e.tensor_scalar(pch(indt, tch),
                                                      pch(prow, tch),
                                                      statB[:, tch:tch + 1], None,
                                                      ALU.is_ge),
                   sem="ve")
        for tch in range(NTCH):
            p.emit("vector",
                   lambda e, tch=tch: e.scalar_tensor_tensor(
                       pch(indt, tch), pch(indt, tch), -1e9, pch(prow, tch),
                       ALU.mult, ALU.add),
                   sem="ve")
        for tch in range(NTCH):
            p.emit("vector",
                   lambda e, tch=tch: e.reduce_max(statD[:, tch:tch + 1],
                                                   pch(indt, tch), axis=AX.X),
                   sem="ve")
        for tch in range(NTCH):
            p.emit("vector",
                   lambda e, tch=tch: e.tensor_scalar(pch(indt, tch),
                                                      pch(prow, tch),
                                                      statD[:, tch:tch + 1], None,
                                                      ALU.is_ge),
                   sem="ve")
        for tch in range(NTCH):
            p.emit("vector",
                   lambda e, tch=tch: e.tensor_mul(pch(cwt, tch), pch(prow, tch),
                                                   pch(indt, tch)),
                   sem="ve")
        cw_done = ("ve", p.v("ve"))

        def ev_sg(mc, tc, bank_ap, waits):
            v = p.emit("scalar",
                       lambda e, mc=mc, tc=tc, bank_ap=bank_ap: e.activation(
                           hsh[:, mc * TH + tc * 512: mc * TH + tc * 512 + 512],
                           bank_ap, AF.Silu),
                       waits=waits, sem="ac")
            return ("ac", v)

        emit_gemm(p, f"sg{half}", wdram=sg_d, k_tile=128, n_kc=8, m_tile=128,
                  n_mc=8, rhs_fn=lambda kc: z[:, kc * TH:(kc + 1) * TH], T=TH,
                  pool=pool, evac=ev_sg, rhs_waits=zw, wbuf=sgbuf)
        sg_done = ("ac", p.v("ac"))

        def ev_su(mc, tc, bank_ap, waits):
            sl = hsh[:, mc * TH + tc * 512: mc * TH + tc * 512 + 512]
            v = p.emit("vector",
                       lambda e, sl=sl, bank_ap=bank_ap: e.scalar_tensor_tensor(
                           sl, bank_ap, 1.0, sl, ALU.mult, ALU.mult),
                       waits=list(waits) + [sg_done], sem="ve")
            return ("ve", v)

        emit_gemm(p, f"su{half}", wdram=su_d, k_tile=128, n_kc=8, m_tile=128,
                  n_mc=8, rhs_fn=lambda kc: z[:, kc * TH:(kc + 1) * TH], T=TH,
                  pool=pool, evac=ev_su, rhs_waits=zw, wbuf=sgbuf)
        hsh_done = [("ve", p.v("ve"))]

        def tm_gemm(hid, n_fc, wbuf_ap, wready, evac, rhs_ready):
            for tch in range(NTCH):
                for dc in range(2):
                    bidx, bank, bfree = pool.acquire()
                    for fc in range(n_fc):
                        w = []
                        if fc == 0:
                            w = [wready] + list(bfree) + list(rhs_ready)
                        p.emit("tensor",
                               lambda e, tch=tch, dc=dc, fc=fc, bank=bank: e.matmul(
                                   bank[:, :],
                                   hid[:, fc * TH + tch * 128: fc * TH + (tch + 1) * 128],
                                   wbuf_ap[:, fc * 1024 + dc * 512:
                                           fc * 1024 + dc * 512 + 512],
                                   start=(fc == 0), stop=(fc == n_fc - 1)),
                               waits=w, sem="pe")
                    done = evac(tch, dc, bank[:, :], [("pe", p.v("pe"))])
                    pool.release(bidx, [done])

        def ev_sd(tch, dc, bank_ap, waits):
            v = p.emit("vector",
                       lambda e, tch=tch, dc=dc, bank_ap=bank_ap: e.tensor_copy(
                           acc[:, tch * D + dc * 512: tch * D + dc * 512 + 512],
                           bank_ap),
                       waits=waits, sem="ve")
            return ("ve", v)

        tm_gemm(hsh, 8, sdbuf[:, :], ("wdma", sdv), ev_sd, hsh_done)

        wd_ready = [None] * 16
        wd_ready[0] = p.dma("gpsimd",
                            wdbuf[0][:, :].rearrange("p (fc f) -> p fc f", fc=4),
                            wd_d[0].rearrange("fc p f -> p fc f"), waits=[("pe", p.v("pe"))], sem="wdma")
        for ex in range(16):
            exbar = [("pe", p.v("pe"))]
            if ex + 1 < 16:
                wd_ready[ex + 1] = p.dma(
                    "gpsimd",
                    wdbuf[(ex + 1) % 2][:, :].rearrange("p (fc f) -> p fc f", fc=4),
                    wd_d[ex + 1].rearrange("fc p f -> p fc f"), waits=exbar, sem="wdma")

            def ev_g(mc, tc, bank_ap, waits):
                v = p.emit("scalar",
                           lambda e, mc=mc, tc=tc, bank_ap=bank_ap: e.activation(
                               hga[:, mc * TH + tc * 512: mc * TH + tc * 512 + 512],
                               bank_ap, AF.Silu),
                           waits=waits, sem="ac")
                return ("ac", v)

            emit_gemm(p, f"wg{half}_{ex}", wdram=wg_d[ex], k_tile=128, n_kc=8,
                      m_tile=128, n_mc=4,
                      rhs_fn=lambda kc: z[:, kc * TH:(kc + 1) * TH], T=TH,
                      pool=pool, evac=ev_g, rhs_waits=zw, wbuf=wgbuf)
            g_done = ("ac", p.v("ac"))

            def ev_u(mc, tc, bank_ap, waits):
                sl = hga[:, mc * TH + tc * 512: mc * TH + tc * 512 + 512]
                v = p.emit("vector",
                           lambda e, sl=sl, bank_ap=bank_ap: e.scalar_tensor_tensor(
                               sl, bank_ap, 1.0, sl, ALU.mult, ALU.mult),
                           waits=list(waits) + [g_done], sem="ve")
                return ("ve", v)

            emit_gemm(p, f"wu{half}_{ex}", wdram=wu_d[ex], k_tile=128, n_kc=8,
                      m_tile=128, n_mc=4,
                      rhs_fn=lambda kc: z[:, kc * TH:(kc + 1) * TH], T=TH,
                      pool=pool, evac=ev_u, rhs_waits=zw, wbuf=wgbuf)
            hga_done = [("ve", p.v("ve")), cw_done]

            def ev_wd(tch, dc, bank_ap, waits, ex=ex):
                v = p.emit("vector",
                           lambda e, tch=tch, dc=dc, bank_ap=bank_ap:
                           e.scalar_tensor_tensor(
                               acc[:, tch * D + dc * 512: tch * D + dc * 512 + 512],
                               bank_ap,
                               cwt[:, tch * 16 + ex: tch * 16 + ex + 1],
                               acc[:, tch * D + dc * 512: tch * D + dc * 512 + 512],
                               ALU.mult, ALU.add),
                           waits=waits, sem="ve")
                return ("ve", v)

            tm_gemm(hga, 4, wdbuf[ex % 2][:, :], ("wdma", wd_ready[ex]), ev_wd,
                    hga_done)

        acc_done = ("ve", p.v("ve"))
        if debug and half == 0:
            p.dma("sync", dbg_prow, prow[:, :], waits=[acc_done], sem="stdma")
            p.dma("sync", dbg_cwt, cwt[:, :], waits=[acc_done], sem="stdma")
            p.dma("sync", dbg_acc, acc[:, :], waits=[acc_done], sem="stdma")
        for tch in range(NTCH):
            p.emit("vector",
                   lambda e, tch=tch: e.reduce_sum(scol[:, tch:tch + 1],
                                                   acc[:, tch * D:(tch + 1) * D],
                                                   axis=AX.X),
                   waits=[acc_done], sem="ve")
        for tch in range(NTCH):
            p.emit("scalar",
                   lambda e, tch=tch: e.activation(trash[:, :],
                                                   acc[:, tch * D:(tch + 1) * D],
                                                   AF.Square,
                                                   accum_out=qcol[:, tch:tch + 1]),
                   waits=[acc_done], sem="ac")
        st_ready = [("ve", p.v("ve")), ("ac", p.v("ac"))]
        p.emit("vector", lambda e: e.tensor_scalar(mcol[:, :], scol[:, :],
                                                   1.0 / D, None, ALU.mult),
               waits=st_ready, sem="ve")
        p.emit("vector", lambda e: e.tensor_scalar(rcol[:, :], qcol[:, :],
                                                   1.0 / D, EPS_LN, ALU.mult,
                                                   ALU.add),
               sem="ve")
        p.emit("vector", lambda e: e.tensor_mul(mrcol[:, :], mcol[:, :],
                                                mcol[:, :]),
               sem="ve")
        p.emit("vector", lambda e: e.memset(trash[:, 0:8], 0.0), sem="ve")
        p.emit("vector", lambda e: e.tensor_sub(rcol[:, :], rcol[:, :],
                                                mrcol[:, :]),
               sem="ve")
        vv = p.emit("scalar", lambda e: e.activation(rcol[:, :], rcol[:, :],
                                                     AF.Sqrt),
                    waits=[("ve", p.v("ve"))], sem="ac")
        p.emit("vector", lambda e: e.tensor_copy(mrcol[:, :], mcol[:, :]),
               sem="ve")
        vv2 = p.emit("vector", lambda e: e.reciprocal(rcol[:, :], rcol[:, :]),
                     waits=[("ac", vv)], sem="ve")
        p.emit("vector", lambda e: e.memset(trash[:, 0:8], 0.0), sem="ve")
        vv3 = p.emit("vector",
                     lambda e: e.scalar_tensor_tensor(mrcol[:, :], mrcol[:, :],
                                                      -1.0, rcol[:, :],
                                                      ALU.mult, ALU.mult),
                     sem="ve")
        close_ready = ("ve", vv3)
        if debug and half == 0:
            for j, tile in enumerate((scol, qcol, mcol, rcol, mrcol)):
                p.dma("sync", dbg_st[:, j * 8:(j + 1) * 8], tile[:, :],
                      waits=[close_ready], sem="stdma")

        add_v = {}
        st_v = {}
        st0 = ("stdma", p.v("stdma"))
        for tch in range(NTCH):
            gtch = half * NTCH + tch
            w_x = [close_ready] if tch < 2 else [("ve", add_v[tch - 2])]
            xv2 = p.dma("sync", xch[tch % 2][:, :],
                        xtm_d.rearrange("(n p) d -> n p d", p=128)[gtch],
                        waits=w_x, sem="xre")
            wz = [close_ready, ("xre", xv2)]
            wz.append(("stdma", st_v[tch - 2]) if tch >= 2 else st0)
            p.emit("vector",
                   lambda e, tch=tch: e.tensor_scalar(
                       och[tch % 2][:, :], acc[:, tch * D:(tch + 1) * D],
                       rcol[:, tch:tch + 1], mrcol[:, tch:tch + 1],
                       ALU.mult, ALU.add),
                   waits=wz, sem="ve")
            av = p.emit("vector",
                        lambda e, tch=tch: e.tensor_add(och[tch % 2][:, :],
                                                        och[tch % 2][:, :],
                                                        xch[tch % 2][:, :]),
                        sem="ve")
            add_v[tch] = av
            st_v[tch] = p.dma("sync",
                              out_d.rearrange("(n p) d -> n p d", p=128)[gtch],
                              och[tch % 2][:, :], waits=[("ve", av)],
                              sem="stdma")
    return finish(p)


def host_l3ffn(inputs, stream):
    Wr = inputs["L3_ffn_Wr"]
    Sg, Su, Sd = inputs["L3_ffn_Sg"], inputs["L3_ffn_Su"], inputs["L3_ffn_Sd"]
    Wg, Wu, Wd = inputs["L3_ffn_Wg"], inputs["L3_ffn_Wu"], inputs["L3_ffn_Wd"]
    shared = {
        "wr": np.ascontiguousarray(Wr.reshape(8, 128, 16).astype(np.float32)),
        "sg": mcform(np.ascontiguousarray(Sg)),
        "su": mcform(np.ascontiguousarray(Su)),
        "sd": np.ascontiguousarray(Sd.reshape(8, 128, 1024).astype(np.float32)),
        "wg": np.stack([mcform(np.ascontiguousarray(Wg[e])) for e in range(16)]),
        "wu": np.stack([mcform(np.ascontiguousarray(Wu[e])) for e in range(16)]),
        "wd": np.stack([np.ascontiguousarray(Wd[e].reshape(4, 128, 1024))
                        for e in range(16)]).astype(np.float32),
    }
    return [dict(shared, x=np.ascontiguousarray(stream[c]),
                 xtm=np.ascontiguousarray(stream[c].T)) for c in range(NCORES)]


def run_l3ffn(inputs, stream):
    nc = build_l3ffn()
    res = run_bass_kernel_spmd(nc, host_l3ffn(inputs, stream),
                               core_ids=list(range(NCORES)))
    return [np.ascontiguousarray(np.asarray(r["out"]).T) for r in res.results]



EPS_L0 = 1e-6


def build_l0attn(debug=False):
    p = P("l0attn")
    T = S
    TH = 2048
    x_d = p.din("x", [D, T])
    wlam_d = p.din("wlam", [1, 128, 8 * 16])
    wb_d = p.din("wb", [1, 128, 8 * 16])
    blam_d = p.din("blam", [16, 1])
    wc_d = p.din("wc", [8, 16, 128])
    wgate_d = p.din("wgate", [8, 128, 8 * 128])
    rmsw_d = p.din("rmsw", [128, 8])
    out_d = p.dout("out", [D, T])

    consts = setup_consts(p)
    lr = make_ln_res(p, TH)
    pool = BankPool(p, "gb", 4)

    blam_s = p.sb("blam_s", 16, 1)
    rmsw_s = p.sb("rmsw_s", 128, 8)
    wc_s = p.sb("wc_s", 16, 8 * 128, BF16)
    p.dma("sync", blam_s[:, :], blam_d)
    bv = p.dma("sync", rmsw_s[:, :], rmsw_d)
    wcv = p.dma("gpsimd", wc_s[:, :].rearrange("p (mc f) -> p mc f", mc=8),
                wc_d.rearrange("mc p f -> p mc f"), sem="wdma")
    small_ready = ("syncdma", bv)

    xb = p.sb("xb", 128, NKC * T, BF16)
    inner = xb
    lam = p.sb("lam", 16, T)
    bxb = p.sb("bxb", 16, T, BF16)
    hsb = bxb
    u_t = p.sb("u_t", 128, 512, BF16)
    w_t = p.sb("w_t", 128, 8 * 512, BF16)
    sqs = p.sb("sqs", 128, 512, BF16)
    rrow2 = p.sb("rrow2", 1, 512)
    rb2 = p.sb("rb2", 128, 512)
    wlam_buf = [p.sb(f"wlamb{i}", 128, 8 * 16, BF16) for i in range(2)]
    wgate_s = p.sb("wgate_s", 128, 8 * 1024, BF16)

    xv = p.dma("gpsimd", xb[:, :].rearrange("p (kc t) -> p kc t", kc=NKC),
               x_d.rearrange("(kc p) t -> p kc t", p=128), sem="xldc")
    zw = []
    for h in range(2):
        zwh, _ = emit_ln(p, xb, TH, consts, lr, xb_waits=[("xldc", xv)],
                         z_out=xb, cstride=T, off=h * TH)
        zw += zwh
    if debug:
        dbg_z = p.dout("dbg_z", [128, NKC * T], BF16)
        p.dma("sync", dbg_z, xb[:, :], waits=zw, sem="stdma")

    def ev_lam(mc, tc, bank_ap, waits):
        v = p.emit("scalar",
                   lambda e, tc=tc, bank_ap=bank_ap: e.activation(
                       lam[0:16, tc * 512:(tc + 1) * 512], bank_ap, AF.Sigmoid,
                       bias=blam_s[:, 0:1], scale=1.0),
                   waits=list(waits) + [small_ready], sem="ac")
        return ("ac", v)

    emit_gemm(p, "lam", wdram=wlam_d, k_tile=128, n_kc=8, m_tile=16, n_mc=1,
              rhs_fn=lambda kc: xb[:, kc * T:(kc + 1) * T], T=T, pool=pool,
              evac=ev_lam, rhs_waits=zw, wbuf=wlam_buf)

    def ev_bx(mc, tc, bank_ap, waits):
        v = p.emit("scalar",
                   lambda e, tc=tc, bank_ap=bank_ap: e.activation(
                       bxb[0:16, tc * 512:(tc + 1) * 512], bank_ap, AF.Copy),
                   waits=waits, sem="ac")
        return ("ac", v)

    emit_gemm(p, "bx", wdram=wb_d, k_tile=128, n_kc=8, m_tile=16, n_mc=1,
              rhs_fn=lambda kc: xb[:, kc * T:(kc + 1) * T], T=T, pool=pool,
              evac=ev_bx, rhs_waits=zw, wbuf=wlam_buf)

    sv = p.emit("vector",
                lambda e: e.tensor_tensor_scan(hsb[0:16, :], lam[0:16, :],
                                               bxb[0:16, :], 0.0,
                                               ALU.mult, ALU.add),
                waits=[("ac", p.v("ac"))], sem="ve")
    hs_ready = ("ve", sv)

    n_tc = T // 512
    wgv = p.dma("gpsimd", wgate_s[:, :].rearrange("p (mc f) -> p mc f", mc=8),
                wgate_d.rearrange("mc p f -> p mc f"), sem="wdma")
    wg_ready = ("wdma", wgv)
    wv_ready = ("wdma", wcv)

    prev_wvv = None
    prev_inner_v = None
    for tc in range(n_tc):
        wtile_done = []
        for mc in range(NKC):
            b1, bankg, bfree1 = pool.acquire()
            for kc in range(NKC):
                w = []
                if kc == 0:
                    w = [wg_ready] + list(bfree1) + (zw if tc == 0 else [])
                p.emit("tensor",
                       lambda e, mc=mc, tc=tc, kc=kc, bankg=bankg: e.matmul(
                           bankg[:, :],
                           wgate_s[:, mc * 1024 + kc * 128: mc * 1024 + (kc + 1) * 128],
                           xb[:, kc * T + tc * 512: kc * T + (tc + 1) * 512],
                           start=(kc == 0), stop=(kc == NKC - 1)),
                       waits=w, sem="pe")
            gv = p.v("pe")
            uw = [("pe", gv)]
            if prev_wvv is not None:
                uw.append(("ve", prev_wvv))
            uv = p.emit("scalar",
                        lambda e, bankg=bankg: e.activation(u_t[:, :],
                                                            bankg[:, :],
                                                            AF.Sigmoid),
                        waits=uw, sem="ac")
            pool.release(b1, [("ac", uv)])
            b2, banky, bfree2 = pool.acquire()
            p.emit("tensor",
                   lambda e, mc=mc, tc=tc, banky=banky: e.matmul(
                       banky[:, :], wc_s[0:16, mc * 128:(mc + 1) * 128],
                       hsb[0:16, tc * 512:(tc + 1) * 512],
                       start=True, stop=True),
                   waits=[wv_ready, hs_ready] + list(bfree2), sem="pe")
            yv = p.v("pe")
            wvv = p.emit("vector",
                         lambda e, mc=mc, banky=banky: e.tensor_mul(
                             w_t[:, mc * 512:(mc + 1) * 512], u_t[:, :],
                             banky[:, :]),
                         waits=[("pe", yv), ("ac", uv)], sem="ve")
            pool.release(b2, [("ve", wvv)])
            prev_wvv = wvv
            wtile_done.append(("ve", wvv))
        sq_prev = None
        for mc in range(NKC):
            w = [wtile_done[mc]]
            if sq_prev is not None:
                w.append(("pe", sq_prev))
            qv = p.emit("scalar",
                        lambda e, mc=mc: e.activation(
                            sqs[:, :], w_t[:, mc * 512:(mc + 1) * 512],
                            AF.Square),
                        waits=w, sem="ac")
            mv = p.emit("tensor",
                        lambda e, mc=mc: e.matmul(
                            lr["ps_q"][0:1, :], consts["ones_bf"][:, :],
                            sqs[:, :], start=(mc == 0), stop=(mc == NKC - 1)),
                        waits=[("ac", qv), consts["ready"]], sem="pe")
            sq_prev = mv
        e1 = p.emit("scalar", lambda e: e.activation(rrow2[0:1, :],
                                                     lr["ps_q"][0:1, :], AF.Copy),
                    waits=[("pe", p.v("pe"))], sem="ac")
        e2 = p.emit("vector",
                    lambda e: e.tensor_scalar(rrow2[0:1, :], rrow2[0:1, :],
                                              1.0 / D, EPS_L0, ALU.mult,
                                              ALU.add),
                    waits=[("ac", e1)], sem="ve")
        e3 = p.emit("scalar", lambda e: e.activation(rrow2[0:1, :], rrow2[0:1, :],
                                                     AF.Sqrt),
                    waits=[("ve", e2)], sem="ac")
        e4 = p.emit("vector", lambda e: e.reciprocal(rrow2[0:1, :], rrow2[0:1, :]),
                    waits=[("ac", e3)], sem="ve")
        g1 = p.emit("tensor",
                    lambda e: e.matmul(lr["ps_b"][:, :], consts["ones1f"][:, :],
                                       rrow2[0:1, :], start=True, stop=True),
                    waits=[("ve", e4), ("ac", p.v("ac"))], sem="pe")
        w5 = [("pe", g1)]
        if prev_inner_v is not None:
            w5.append(("ve", prev_inner_v))
        e5 = p.emit("scalar", lambda e: e.activation(rb2[:, :], lr["ps_b"][:, :],
                                                     AF.Copy),
                    waits=w5, sem="ac")
        rdy = ("ac", e5)
        for mc in range(NKC):
            prev_inner_v = p.emit("vector",
                   lambda e, mc=mc, tc=tc: e.scalar_tensor_tensor(
                       inner[:, mc * T + tc * 512: mc * T + tc * 512 + 512],
                       w_t[:, mc * 512:(mc + 1) * 512], rmsw_s[:, mc:mc + 1],
                       rb2[:, :], ALU.mult, ALU.mult),
                   waits=[rdy, small_ready], sem="ve")
    inner_waits = [("ve", p.v("ve"))]
    if debug:
        dbg_lam = p.dout("dbg_lam", [16, T])
        dbg_hs = p.dout("dbg_hs", [16, T], BF16)
        dbg_inner = p.dout("dbg_inner", [128, NKC * T], BF16)
        p.dma("sync", dbg_lam, lam[0:16, :], waits=inner_waits, sem="stdma")
        p.dma("sync", dbg_hs, hsb[0:16, :], waits=inner_waits, sem="stdma")
        p.dma("sync", dbg_inner, inner[:, :], waits=inner_waits, sem="stdma")
    clbufs = ([p.sb("clx0", 128, TH, F32), p.sb("clx1", 128, TH, F32)],
              [p.sb("clo0", 128, TH, F32), p.sb("clo1", 128, TH, F32)])
    for h in range(2):
        emit_sandwich_close(p, f"cl{h}", inner, TH, consts, lr,
                            inner_waits=inner_waits,
                            x_dram=x_d[:, h * TH:(h + 1) * TH],
                            out_dram=out_d[:, h * TH:(h + 1) * TH],
                            cstride=T, off=h * TH, bufs=clbufs)
    return finish(p)


def host_l0attn(inputs, xfm_batches):
    Wl = inputs["L0_attn_W_lambda"]
    in_maps = []
    for c in range(NCORES):
        b = c // 2
        in_maps.append({
            "x": xfm_batches[b],
            "wlam": mcform(np.ascontiguousarray(Wl.T), m_tile=16),
            "wb": mcform(np.ascontiguousarray(inputs["L0_attn_W_B"].T), m_tile=16),
            "blam": np.ascontiguousarray(
                inputs["L0_attn_b_lambda"].reshape(16, 1).astype(np.float32)),
            "wc": mcform(np.ascontiguousarray(inputs["L0_attn_W_C"].T),
                         k_tile=16),
            "wgate": mcform(np.ascontiguousarray(inputs["L0_attn_W_gate"].T)),
            "rmsw": biasform(inputs["L0_attn_rms_weight"]),
        })
    return in_maps


def run_l0attn(inputs, xfm_batches):
    nc = build_l0attn()
    res = run_bass_kernel_spmd(nc, host_l0attn(inputs, xfm_batches),
                               core_ids=list(range(NCORES)))
    return [np.asarray(res.results[2 * b]["out"]) for b in range(4)]



def build_l1attn():
    p = P("l1attn")
    T = S
    TH = 2048
    x_d = p.din("x", [D, T])
    wpd_d = p.din("wpd", [8, 128, 8 * 128])
    wpc_d = p.din("wpc", [8, 128, 8 * 128])
    wpr_d = p.din("wpr", [8, 128, 8 * 128])
    wdl_d = p.din("wdl", [8, 128, 8 * 128])
    bd_d = p.din("bd", [128, 8])
    a1_d = p.din("a1", [128, 8])
    b1_d = p.din("b1", [128, 8])
    dsc_d = p.din("dsc", [128, 1])
    out_d = p.dout("out", [D, T])
    delta_d = p.dscratch("delta_sc", [D, T], BF16)
    inner_d = p.dscratch("inner_sc", [D, T], BF16)

    consts = setup_consts(p, fvals=(-1.0, 1.0))
    TQ = 1024
    lr = make_ln_res(p, TQ)
    pool = BankPool(p, "gb", 4)

    bd_s = p.sb("bd_s", 128, 8)
    a1_s = p.sb("a1_s", 128, 8)
    b1_s = p.sb("b1_s", 128, 8)
    d_s = p.sb("d_s", 128, 1)
    for nm, dd in (("bd_s", bd_d), ("a1_s", a1_d), ("b1_s", b1_d), ("dsc", dsc_d)):
        pass
    p.dma("sync", bd_s[:, :], bd_d)
    p.dma("sync", a1_s[:, :], a1_d)
    p.dma("sync", b1_s[:, :], b1_d)
    sv = p.dma("sync", d_s[:, :], dsc_d)
    small_ready = ("syncdma", sv)

    xb = p.sb("xb", 128, NKC * T, BF16)
    z = xb
    tpre = p.sb("tpre", 128, NKC * TH, BF16)
    drow = [p.sb(f"drow{i}", 128, T, BF16) for i in range(2)]
    cdrow = p.sb("cdrow", 128, T, BF16)
    sp0 = p.sb("sp0", 128, 512)
    sp1 = p.sb("sp1", 128, 512)
    sp2 = p.sb("sp2", 128, 512)
    dstage = p.sb("dstage", 128, 512, BF16)
    estage = p.sb("estage", 128, 512)
    ystage = p.sb("ystage", 128, 512)
    istage = [p.sb(f"istage{i}", 128, 512, BF16) for i in range(2)]
    ucc = p.sb("ucc", 128, 64)
    csc = p.sb("csc", 128, 64)
    uc_t = p.sb("uc_t", 128, 8)
    cs_t = p.sb("cs_t", 128, 8)
    c1_t = p.sb("c1_t", 128, 8)
    c2_t = p.sb("c2_t", 128, 8)
    tstage = p.sb("tstage", 128, 512)
    wbufA = [p.sb(f"wbA{i}", 128, 8 * 128, BF16) for i in range(2)]

    xv = p.dma("gpsimd", xb[:, :].rearrange("p (kc t) -> p kc t", kc=NKC),
               x_d.rearrange("(kc p) t -> p kc t", p=128), sem="xldc")
    zw = []
    for h in range(4):
        zwh, _ = emit_ln(p, xb, TQ, consts, lr, xb_waits=[("xldc", xv)],
                         z_out=xb, cstride=T, off=h * TQ)
        zw += zwh

    prev_sp = None
    for th in range(2):
        def ev_tp(mc, tc, bank_ap, waits, th=th):
            v = p.emit("scalar",
                       lambda e, mc=mc, tc=tc, bank_ap=bank_ap: e.activation(
                           tpre[:, mc * TH + tc * 512: mc * TH + tc * 512 + 512],
                           bank_ap, AF.Copy),
                       waits=waits, sem="ac")
            return ("ac", v)

        emit_gemm(p, f"tp{th}", wdram=wpd_d, k_tile=128, n_kc=8, m_tile=128,
                  n_mc=8,
                  rhs_fn=lambda kc, th=th: z[:, kc * T + th * TH: kc * T + (th + 1) * TH],
                  T=TH, pool=pool, evac=ev_tp, rhs_waits=zw, wbuf=wbufA)
        tpre_done = [("ac", p.v("ac"))]

        def ev_dl(mc, tc, bank_ap, waits, th=th):
            nonlocal_list = []
            w0 = list(waits) + [small_ready]
            if prev_spv[0] is not None:
                w0.append(("stdma2", prev_spv[0]))
            v0 = p.emit("vector",
                        lambda e, mc=mc, bank_ap=bank_ap: e.tensor_scalar(
                            sp0[:, :], bank_ap, bd_s[:, mc:mc + 1], None,
                            ALU.add),
                        waits=w0, sem="ve")
            v1 = p.emit("scalar",
                        lambda e: e.activation(sp1[:, :], sp0[:, :], AF.Abs),
                        waits=[("ve", v0)], sem="ac")
            p.emit("scalar",
                   lambda e: e.activation(sp1[:, :], sp1[:, :], AF.Exp,
                                          scale=-1.0),
                   sem="ac")
            p.emit("scalar",
                   lambda e: e.activation(sp1[:, :], sp1[:, :], AF.Ln,
                                          bias=1.0),
                   sem="ac")
            v4 = p.emit("scalar",
                        lambda e: e.activation(sp2[:, :], sp0[:, :], AF.Relu),
                        sem="ac")
            v5 = p.emit("vector",
                        lambda e: e.tensor_add(dstage[:, :], sp1[:, :],
                                               sp2[:, :]),
                        waits=[("ac", v4)], sem="ve")
            dv = p.dma("sync",
                       delta_d[mc * 128:(mc + 1) * 128,
                               th * TH + tc * 512: th * TH + tc * 512 + 512],
                       dstage[:, :], waits=[("ve", v5)], sem="stdma2")
            prev_spv[0] = dv
            return ("ve", v5)

        prev_spv = [prev_sp]
        emit_gemm(p, f"dl{th}", wdram=wdl_d, k_tile=128, n_kc=8, m_tile=128,
                  n_mc=8, rhs_fn=lambda kc: tpre[:, kc * TH:(kc + 1) * TH],
                  T=TH, pool=pool, evac=ev_dl, rhs_waits=tpre_done, wbuf=wbufA)
        prev_sp = prev_spv[0]
    delta_done = ("stdma2", p.v("stdma2"))

    def ev_cs(mc, tc, bank_ap, waits):
        v0 = p.emit("vector",
                    lambda e, mc=mc, tc=tc, bank_ap=bank_ap: e.tensor_mul(
                        tstage[:, :], bank_ap,
                        z[:, mc * T + tc * 512: mc * T + tc * 512 + 512]),
                    waits=waits, sem="ve")
        p.emit("vector",
               lambda e, mc=mc, tc=tc, bank_ap=bank_ap: e.reduce_sum(
                   csc[:, mc * 8 + tc: mc * 8 + tc + 1], bank_ap, axis=AX.X),
               sem="ve")
        v2 = p.emit("vector",
                    lambda e, mc=mc, tc=tc: e.reduce_sum(
                        ucc[:, mc * 8 + tc: mc * 8 + tc + 1], tstage[:, :],
                        axis=AX.X),
                    sem="ve")
        return ("ve", v2)

    emit_gemm(p, "cs", wdram=wpc_d, k_tile=128, n_kc=8, m_tile=128, n_mc=8,
              rhs_fn=lambda kc: z[:, kc * T:(kc + 1) * T], T=T, pool=pool,
              evac=ev_cs, rhs_waits=zw, wbuf=wbufA)
    cs_done = ("ve", p.v("ve"))
    for mc in range(8):
        p.emit("vector",
               lambda e, mc=mc: e.reduce_sum(uc_t[:, mc:mc + 1],
                                             ucc[:, mc * 8:(mc + 1) * 8],
                                             axis=AX.X),
               waits=[cs_done], sem="ve")
    for mc in range(8):
        p.emit("vector",
               lambda e, mc=mc: e.reduce_sum(cs_t[:, mc:mc + 1],
                                             csc[:, mc * 8:(mc + 1) * 8],
                                             axis=AX.X),
               sem="ve")
    p.emit("vector", lambda e: e.tensor_mul(c1_t[:, :], b1_s[:, :], uc_t[:, :]),
           waits=[small_ready], sem="ve")
    cv = p.emit("vector", lambda e: e.tensor_mul(c2_t[:, :], b1_s[:, :],
                                                 cs_t[:, :]),
                sem="ve")
    c_done = ("ve", cv)

    wres_s = p.sb("wres_s", 128, 8 * 1024, BF16)
    wres_v = p.dma("gpsimd", wres_s[:, :].rearrange("p (mc f) -> p mc f", mc=8),
                   wpr_d.rearrange("mc p f -> p mc f"), sem="wdma")
    dload = [None, None]
    dload[0] = p.dma("sync", drow[0][:, :], delta_d[0:128, :],
                     waits=[delta_done], sem="xre")
    prev_scan = None
    prev_exp = {}
    prev_inner = None
    for kc in range(NKC):
        if kc + 1 < NKC:
            w = []
            if kc >= 1:
                w = [("ve", prev_scan), ("ac", prev_exp.get(kc - 1, 0))]
            dload[(kc + 1) % 2] = p.dma(
                "sync", drow[(kc + 1) % 2][:, :],
                delta_d[(kc + 1) * 128:(kc + 2) * 128, :],
                waits=w, sem="xre")
        scv = p.emit("vector",
                     lambda e, kc=kc: e.tensor_tensor_scan(
                         cdrow[:, :], drow[kc % 2][:, :], drow[kc % 2][:, :],
                         0.0, ALU.add, ALU.bypass),
                     waits=[("xre", dload[kc % 2]), c_done] +
                           ([("ve", prev_scan)] if prev_scan else []),
                     sem="ve")
        prev_scan = scv
        for tc in range(T // 512):
            ev = p.emit("scalar",
                        lambda e, kc=kc, tc=tc: e.activation(
                            estage[:, :],
                            drow[kc % 2][:, tc * 512:(tc + 1) * 512], AF.Exp,
                            scale=a1_s[:, kc:kc + 1]),
                        waits=[("xre", dload[kc % 2])], sem="ac")
            prev_exp[kc] = ev
            t2 = p.emit("vector",
                        lambda e, kc=kc, tc=tc: e.tensor_scalar(
                            ystage[:, :], cdrow[:, tc * 512:(tc + 1) * 512],
                            c2_t[:, kc:kc + 1], None, ALU.mult),
                        waits=[("ve", scv)], sem="ve")
            t3 = p.emit("vector",
                        lambda e, kc=kc: e.scalar_tensor_tensor(
                            ystage[:, :], estage[:, :], c1_t[:, kc:kc + 1],
                            ystage[:, :], ALU.mult, ALU.add),
                        waits=[("ac", ev)], sem="ve")
            bidx, bank, bfree = pool.acquire()
            for kin in range(NKC):
                w = []
                if kin == 0:
                    w = [("wdma", wres_v)] + list(bfree)
                p.emit("tensor",
                       lambda e, kc=kc, tc=tc, kin=kin, bank=bank: e.matmul(
                           bank[:, :],
                           wres_s[:, kc * 1024 + kin * 128: kc * 1024 + (kin + 1) * 128],
                           z[:, kin * T + tc * 512: kin * T + (tc + 1) * 512],
                           start=(kin == 0), stop=(kin == NKC - 1)),
                       waits=w, sem="pe")
            iv = p.emit("vector",
                        lambda e, kc=kc, tc=tc, bank=bank: e.scalar_tensor_tensor(
                            istage[tc % 2][:, :], bank[:, :], d_s[:, 0:1],
                            ystage[:, :], ALU.mult, ALU.add),
                        waits=[("pe", p.v("pe")), ("ve", t3), small_ready] +
                              ([("stdma2", prev_inner)] if prev_inner else []),
                        sem="ve")
            pool.release(bidx, [("ve", iv)])
            prev_inner = p.dma("sync",
                               inner_d[kc * 128:(kc + 1) * 128,
                                       tc * 512:(tc + 1) * 512],
                               istage[tc % 2][:, :], waits=[("ve", iv)],
                               sem="stdma2")


    inner_done = ("stdma2", p.v("stdma2"))

    clbufs = ([p.sb("clx0", 128, TQ, F32), p.sb("clx1", 128, TQ, F32)],
              [p.sb("clo0", 128, TQ, F32), p.sb("clo1", 128, TQ, F32)])
    for th in range(4):
        ibar = p.barrier()
        ivv = p.dma("gpsimd",
                    tpre[:, 0:NKC * TQ].rearrange("p (kc t) -> p kc t", kc=NKC),
                    inner_d[:, th * TQ:(th + 1) * TQ]
                    .rearrange("(kc p) t -> p kc t", p=128),
                    waits=[inner_done] + ibar, sem="xldc")
        emit_sandwich_close(p, f"cl{th}", tpre, TQ, consts, lr,
                            inner_waits=[("xldc", ivv)],
                            x_dram=x_d[:, th * TQ:(th + 1) * TQ],
                            out_dram=out_d[:, th * TQ:(th + 1) * TQ],
                            bufs=clbufs)
    return finish(p)


def host_l1attn(inputs, xfm_batches):
    Wp = inputs["L1_attn_W_proj"]
    in_maps = []
    shared = {
        "wpd": mcform(np.ascontiguousarray(Wp[0:1024].T)),
        "wpc": mcform(np.ascontiguousarray(Wp[2048:3072].T)),
        "wpr": mcform(np.ascontiguousarray(Wp[3072:4096].T)),
        "wdl": mcform(np.ascontiguousarray(inputs["L1_attn_W_delta"].T)),
        "bd": biasform(inputs["L1_attn_b_delta"]),
        "a1": biasform(inputs["L1_attn_A"][0, 0]),
        "b1": biasform(inputs["L1_attn_Bp"][0, :, 0]),
        "dsc": np.ascontiguousarray(
            np.repeat(inputs["L1_attn_D"].astype(np.float32), 128).reshape(128, 1)),
    }
    for c in range(NCORES):
        in_maps.append(dict(shared, x=xfm_batches[c // 2]))
    return in_maps


def run_l1attn(inputs, xfm_batches):
    nc = build_l1attn()
    res = run_bass_kernel_spmd(nc, host_l1attn(inputs, xfm_batches),
                               core_ids=list(range(NCORES)))
    return [np.asarray(res.results[2 * b]["out"]) for b in range(4)]



def build_l2attn(debug=False, mini=None):
    p = P("l2attn")
    T = S
    NH = 8
    x_d = p.din("x", [D, T])
    wq_d = p.din("wq", [8, 128, 8 * 128])
    wk_d = p.din("wk", [8, 128, 8 * 128])
    wv_d = p.din("wv", [8, 128, 512])
    bq_d = p.din("bq", [128, 8])
    bk_d = p.din("bk", [128, 8])
    bv_d = p.din("bv", [1, 512])
    qpos_d = p.din("qpos", [4, T])
    kpos_d = p.din("kpos", [4, T])
    cmask_d = p.din("cmask", [4, 128, 512])
    attn_d = p.dout("attn", [T, 512], BF16)
    ksc_d = p.dscratch("ksc", [8, 128, T], BF16)

    consts = setup_consts(p, fvals=(0.125,))
    TQ = 512
    lr = make_ln_res(p, TQ)
    pool = BankPool(p, "gb", 4)

    bq_s = p.sb("bq_s", 128, 8)
    bk_s = p.sb("bk_s", 128, 8)
    bq8_s = p.sb("bq8_s", 128, 8)
    bv_s = p.sb("bv_s", 1, 512, BF16)
    onesrow = p.sb("onesrow", 1, 128, BF16)
    cmask_s = p.sb("cmask_s", 128, 4 * 512, BF16)
    bqv = p.dma("sync", bq_s[:, :], bq_d)
    bkv = p.dma("sync", bk_s[:, :], bk_d)
    bvv = p.dma("gpsimd", bv_s[:, :], bv_d, sem="xldc")
    mv = p.dma("gpsimd", cmask_s[:, :].rearrange("p (o f) -> p o f", o=4),
               cmask_d.rearrange("o p f -> p o f"), sem="mld")
    mask_ready = ("mld", mv)
    vv0 = p.emit("vector", lambda e: e.memset(onesrow[:, :], 1.0), sem="ve")
    b8 = p.emit("vector",
                lambda e: e.tensor_scalar(bq8_s[:, :], bq_s[:, :], 0.125, None,
                                          ALU.mult),
                waits=[("syncdma", bqv)], sem="ve")
    small_ready = [("syncdma", bkv), ("ve", b8), ("ve", vv0)]

    xb = p.sb("xb", 128, NKC * T, BF16)
    z = xb
    qt_a = p.sb("qt_a", 128, 4 * T, BF16)
    qt_b = p.sb("qt_b", 128, 4 * T, BF16)

    def qhat(h):
        return qt_a[:, (h % 4) * T:(h % 4 + 1) * T] if h < 4 else \
            qt_b[:, (h - 4) * T:(h - 3) * T]

    def khat(h):
        return xb[:, h * T:(h + 1) * T]

    vTx = p.sb("vTx", 128, 32 * 528, BF16)
    p_sb = [p.sb(f"p_sb{i}", 128, 512, BF16) for i in range(2)]
    rv_t = p.sb("rv_t", 128, 4)
    ast = [p.sb(f"ast{i}", 128, 64, BF16) for i in range(2)]
    kstage = [p.sb(f"kst{i}", 128, 512, BF16) for i in range(2)]
    wbufA = [p.sb(f"wbA{i}", 128, 8 * 128, BF16) for i in range(2)]

    xv = p.dma("gpsimd", xb[:, :].rearrange("p (kc t) -> p kc t", kc=NKC),
               x_d.rearrange("(kc p) t -> p kc t", p=128), sem="xldc")
    zw = []
    for hh in range(8):
        zwh, _ = emit_ln(p, xb, TQ, consts, lr, xb_waits=[("xldc", xv)],
                         z_out=xb, cstride=T, off=hh * TQ)
        zw += zwh

    wv_s = p.sb("wv_s", 128, 8 * 512, BF16)
    wvv = p.dma("gpsimd", wv_s[:, :].rearrange("p (kc f) -> p kc f", kc=8),
                wv_d.rearrange("kc p f -> p kc f"), sem="wdma")
    for tch in range(32):
        bidx, bank, bfree = pool.acquire()
        for kc in range(NKC):
            w = []
            if kc == 0:
                w = [("wdma", wvv)] + list(bfree) + zw + small_ready + \
                    [("xldc", bvv)]
            p.emit("tensor",
                   lambda e, tch=tch, kc=kc, bank=bank: e.matmul(
                       bank[:, :],
                       z[:, kc * T + tch * 128: kc * T + (tch + 1) * 128],
                       wv_s[:, kc * 512:(kc + 1) * 512],
                       start=(kc == 0), stop=False),
                   waits=w, sem="pe")
        p.emit("tensor",
               lambda e, bank=bank: e.matmul(bank[:, :], onesrow[0:1, :],
                                             bv_s[0:1, :], start=False,
                                             stop=True),
               waits=[], sem="pe")
        pv = p.v("pe")
        last = None
        for h in range(NH):
            last = p.emit("scalar",
                          lambda e, tch=tch, h=h, bank=bank: e.activation(
                              vTx[:, tch * 528 + h * 66: tch * 528 + h * 66 + 64],
                              bank[:, h * 64:(h + 1) * 64], AF.Copy),
                          waits=[("pe", pv)] if h == 0 else [], sem="ac")
        pool.release(bidx, [("ac", last)])
    ov = p.emit("vector",
                lambda e: e.memset(
                    vTx[:, :].rearrange("p (g c) -> p g c", c=66)[:, :, 64:65],
                    1.0),
                sem="ve")
    vT_done = [("ac", p.v("ac")), ("ve", ov)]

    def ev_q(mc, tc, bank_ap, waits):
        v = p.emit("vector",
                   lambda e, mc=mc, tc=tc, bank_ap=bank_ap: e.tensor_scalar(
                       qhat(mc)[:, tc * 512:(tc + 1) * 512],
                       bank_ap, 0.125, bq8_s[:, mc:mc + 1], ALU.mult, ALU.add),
                   waits=list(waits) + small_ready, sem="ve")
        return ("ve", v)

    emit_gemm(p, "q", wdram=wq_d, k_tile=128, n_kc=8, m_tile=128, n_mc=8,
              rhs_fn=lambda kc: z[:, kc * T:(kc + 1) * T], T=T, pool=pool,
              evac=ev_q, rhs_waits=zw, wbuf=wbufA)

    kst_v = [None, None]

    def ev_k(mc, tc, bank_ap, waits):
        idx = (mc * 8 + tc) % 2
        w = list(waits) + small_ready
        if kst_v[idx] is not None:
            w.append(("ksd", kst_v[idx]))
        v = p.emit("vector",
                   lambda e, bank_ap=bank_ap: e.tensor_scalar(
                       kstage[idx][:, :], bank_ap, bk_s[:, mc:mc + 1], None,
                       ALU.add),
                   waits=w, sem="ve")
        kst_v[idx] = p.dma("sync", ksc_d[mc, :, tc * 512:(tc + 1) * 512],
                           kstage[idx][:, :], waits=[("ve", v)], sem="ksd")
        return ("ve", v)

    emit_gemm(p, "k", wdram=wk_d, k_tile=128, n_kc=8, m_tile=128, n_mc=8,
              rhs_fn=lambda kc: z[:, kc * T:(kc + 1) * T], T=T, pool=pool,
              evac=ev_k, rhs_waits=zw, wbuf=wbufA)

    zdead = [("pe", p.v("pe")), ("ksd", p.v("ksd")), ("ve", p.v("ve"))]
    for h in range(NH):
        p.dma("gpsimd", khat(h), ksc_d[h], waits=zdead if h == 0 else [],
              sem="khl")
    big_done = ("khl", p.v("khl"))
    for h in range(NH):
        p.dma("gpsimd", khat(h)[64:68, :], kpos_d,
              waits=[big_done] if h == 0 else [], sem="mld")
        p.dma("gpsimd", qhat(h)[64:68, :], qpos_d,
              waits=zdead if h == 0 else [], sem="mld")
    hat_ready = ("mld", p.v("mld"))

    if debug:
        dq = p.dout("dq", [128, 8 * T], BF16)
        dk = p.dout("dk", [128, 8 * T], BF16)
        dv = p.dout("dv", [128, 32 * 528], BF16)
        dmask = p.dout("dmask", [128, 4 * 512], BF16)
        dbg_sc = p.dout("dbg_sc", [128, 512])
        wdbg = [hat_ready, ("ve", p.v("ve")), ("ac", p.v("ac")), mask_ready]
        for h in range(NH):
            p.dma("sync", dq[:, h * T:(h + 1) * T], qhat(h), waits=wdbg,
                  sem="stdma")
            p.dma("sync", dk[:, h * T:(h + 1) * T], khat(h), waits=wdbg,
                  sem="stdma")
        p.dma("sync", dv, vTx[:, :], waits=wdbg, sem="stdma")
        p.dma("sync", dmask, cmask_s[:, :], waits=wdbg, sem="stdma")

    ps_o = p.psum("ps_o", 128, 512)
    out_banks = [lr["ps_s"], lr["ps_q"], lr["ps_b"], ps_o]
    out_free = [p.barrier(("ac", "ve")) for _ in range(4)]
    pv_hist = []
    NHx, NQG = mini if mini else (NH, 8)
    for h in range(NHx):
        for qg in range(NQG):
            n_kt = 4 * qg + 4
            exp_v = {}
            pv_done = {}
            for kt in range(n_kt):
                bs, bank_s, bfree_s = pool.acquire()
                w1 = list(bfree_s)
                if kt == 0:
                    w1 += vT_done + [("ve", p.v("ve"))] + [hat_ready]
                p.emit("tensor",
                       lambda e, kt=kt, qg=qg, h=h, bank_s=bank_s: e.matmul(
                           bank_s[:, :],
                           khat(h)[:, kt * 128:(kt + 1) * 128],
                           qhat(h)[:, qg * 512:(qg + 1) * 512],
                           start=True, stop=True),
                       waits=w1, sem="pe")
                mm2 = p.v("pe")
                off = kt * 128 - qg * 512
                wexp = [("pe", mm2)]
                if off >= 0:
                    oi = off // 128
                    mk = p.emit("vector",
                                lambda e, oi=oi, bank_s=bank_s: e.tensor_add(
                                    bank_s[:, :], bank_s[:, :],
                                    cmask_s[:, oi * 512:(oi + 1) * 512]),
                                waits=[("pe", mm2), mask_ready], sem="ve")
                    wexp = [("ve", mk)]
                if len(pv_hist) >= 2:
                    wexp.append(("pe", pv_hist[-2]))
                ev = p.emit("scalar",
                            lambda e, kt=kt, bank_s=bank_s: e.activation(
                                p_sb[kt % 2][:, :], bank_s[:, :], AF.Exp),
                            waits=wexp, sem="ac")
                exp_v[kt] = ev
                pool.release(bs, [("ac", ev)])
                for qc in range(4):
                    w3 = [("ac", ev)]
                    if kt == 0:
                        w3 += list(out_free[qc])
                    p.emit("tensor",
                           lambda e, kt=kt, qc=qc, h=h: e.matmul(
                               out_banks[qc][:, 0:65],
                               p_sb[kt % 2][:, qc * 128:(qc + 1) * 128],
                               vTx[:, kt * 528 + h * 66: kt * 528 + h * 66 + 65],
                               start=(kt == 0), stop=(kt == n_kt - 1)),
                           waits=w3, sem="pe")
                pv_done[kt] = p.v("pe")
                pv_hist.append(pv_done[kt])
            pvv = p.v("pe")
            for qc in range(4):
                p.emit("vector",
                       lambda e, qc=qc: e.reciprocal(rv_t[:, qc:qc + 1],
                                                     out_banks[qc][:, 64:65]),
                       waits=[("pe", pvv)], sem="ve")
            for qc in range(4):
                evv = p.emit("vector",
                             lambda e, qc=qc: e.tensor_scalar(
                                 ast[qc % 2][:, :], out_banks[qc][:, 0:64],
                                 rv_t[:, qc:qc + 1], None, ALU.mult),
                             waits=[("stdma", p.v("stdma"))] if qc < 2 else
                                   [("stdma", p.v("stdma") - 16)],
                             sem="ve")
                p.dma("sync",
                      attn_d[qg * 512 + qc * 128: qg * 512 + (qc + 1) * 128,
                             h * 64:(h + 1) * 64],
                      ast[qc % 2][:, :], waits=[("ve", evv)], sem="stdma")
                out_free[qc] = [("ve", evv)]
    return finish(p)


def build_l3attn(debug=False, mini=None):
    p = P("l3attn")
    T = S
    NH = 8
    x_d = p.din("x", [D, T])
    wq_d = p.din("wq", [8, 128, 8 * 128])
    wk_d = p.din("wk", [8, 128, 8 * 128])
    wv_d = p.din("wv", [8, 128, 512])
    bq_d = p.din("bq", [128, 8])
    bk_d = p.din("bk", [128, 8])
    bv_d = p.din("bv", [1, 512])
    cmask_d = p.din("cmask", [8, 128, 512])
    attn_d = p.dout("attn", [T, 512], BF16)
    ksc_d = p.dscratch("ksc", [8, 128, T], BF16)

    consts = setup_consts(p, fvals=(0.125,))
    TQ = 512
    lr = make_ln_res(p, TQ)
    pool = BankPool(p, "gb", 4)

    bq_s = p.sb("bq_s", 128, 8)
    bk_s = p.sb("bk_s", 128, 8)
    bq8_s = p.sb("bq8_s", 128, 8)
    bv_s = p.sb("bv_s", 1, 512, BF16)
    onesrow = p.sb("onesrow", 1, 128, BF16)
    cmask_s = p.sb("cmask_s", 128, 8 * 512, BF16)
    bqv = p.dma("sync", bq_s[:, :], bq_d)
    bkv = p.dma("sync", bk_s[:, :], bk_d)
    bvv = p.dma("gpsimd", bv_s[:, :], bv_d, sem="xldc")
    mv = p.dma("gpsimd", cmask_s[:, :].rearrange("p (o f) -> p o f", o=8),
               cmask_d.rearrange("o p f -> p o f"), sem="mld")
    mask_ready = ("mld", mv)
    vv0 = p.emit("vector", lambda e: e.memset(onesrow[:, :], 1.0), sem="ve")
    b8 = p.emit("vector",
                lambda e: e.tensor_scalar(bq8_s[:, :], bq_s[:, :], 0.125, None,
                                          ALU.mult),
                waits=[("syncdma", bqv)], sem="ve")
    small_ready = [("syncdma", bkv), ("ve", b8), ("ve", vv0)]

    xb = p.sb("xb", 128, NKC * T, BF16)
    z = xb
    qt_a = p.sb("qt_a", 128, 4 * T, BF16)
    qt_b = p.sb("qt_b", 128, 4 * T, BF16)

    def qhat(h):
        return qt_a[:, (h % 4) * T:(h % 4 + 1) * T] if h < 4 else \
            qt_b[:, (h - 4) * T:(h - 3) * T]

    def khat(h):
        return xb[:, h * T:(h + 1) * T]

    vTx = p.sb("vTx", 128, 32 * 528, BF16)
    p_sb = [p.sb(f"p_sb{i}", 128, 512, BF16) for i in range(2)]
    rv_t = p.sb("rv_t", 128, 4)
    ast = [p.sb(f"ast{i}", 128, 64, BF16) for i in range(2)]
    kstage = [p.sb(f"kst{i}", 128, 512, BF16) for i in range(2)]
    wbufA = [p.sb(f"wbA{i}", 128, 8 * 128, BF16) for i in range(2)]

    xv = p.dma("gpsimd", xb[:, :].rearrange("p (kc t) -> p kc t", kc=NKC),
               x_d.rearrange("(kc p) t -> p kc t", p=128), sem="xldc")
    zw = []
    for hh in range(8):
        zwh, _ = emit_ln(p, xb, TQ, consts, lr, xb_waits=[("xldc", xv)],
                         z_out=xb, cstride=T, off=hh * TQ)
        zw += zwh

    wv_s = p.sb("wv_s", 128, 8 * 512, BF16)
    wvv = p.dma("gpsimd", wv_s[:, :].rearrange("p (kc f) -> p kc f", kc=8),
                wv_d.rearrange("kc p f -> p kc f"), sem="wdma")
    for tch in range(32):
        bidx, bank, bfree = pool.acquire()
        for kc in range(NKC):
            w = []
            if kc == 0:
                w = [("wdma", wvv)] + list(bfree) + zw + small_ready + \
                    [("xldc", bvv)]
            p.emit("tensor",
                   lambda e, tch=tch, kc=kc, bank=bank: e.matmul(
                       bank[:, :],
                       z[:, kc * T + tch * 128: kc * T + (tch + 1) * 128],
                       wv_s[:, kc * 512:(kc + 1) * 512],
                       start=(kc == 0), stop=False),
                   waits=w, sem="pe")
        p.emit("tensor",
               lambda e, bank=bank: e.matmul(bank[:, :], onesrow[0:1, :],
                                             bv_s[0:1, :], start=False,
                                             stop=True),
               waits=[], sem="pe")
        pv = p.v("pe")
        last = None
        for h in range(NH):
            last = p.emit("scalar",
                          lambda e, tch=tch, h=h, bank=bank: e.activation(
                              vTx[:, tch * 528 + h * 66: tch * 528 + h * 66 + 64],
                              bank[:, h * 64:(h + 1) * 64], AF.Copy),
                          waits=[("pe", pv)] if h == 0 else [], sem="ac")
        pool.release(bidx, [("ac", last)])
    ov = p.emit("vector",
                lambda e: e.memset(
                    vTx[:, :].rearrange("p (g c) -> p g c", c=66)[:, :, 64:65],
                    1.0),
                sem="ve")
    vT_done = [("ac", p.v("ac")), ("ve", ov)]

    def ev_q(mc, tc, bank_ap, waits):
        v = p.emit("vector",
                   lambda e, mc=mc, tc=tc, bank_ap=bank_ap: e.tensor_scalar(
                       qhat(mc)[:, tc * 512:(tc + 1) * 512],
                       bank_ap, 0.125, bq8_s[:, mc:mc + 1], ALU.mult, ALU.add),
                   waits=list(waits) + small_ready, sem="ve")
        return ("ve", v)

    emit_gemm(p, "q", wdram=wq_d, k_tile=128, n_kc=8, m_tile=128, n_mc=8,
              rhs_fn=lambda kc: z[:, kc * T:(kc + 1) * T], T=T, pool=pool,
              evac=ev_q, rhs_waits=zw, wbuf=wbufA)

    kst_v = [None, None]

    def ev_k(mc, tc, bank_ap, waits):
        idx = (mc * 8 + tc) % 2
        w = list(waits) + small_ready
        if kst_v[idx] is not None:
            w.append(("ksd", kst_v[idx]))
        v = p.emit("vector",
                   lambda e, bank_ap=bank_ap: e.tensor_scalar(
                       kstage[idx][:, :], bank_ap, bk_s[:, mc:mc + 1], None,
                       ALU.add),
                   waits=w, sem="ve")
        kst_v[idx] = p.dma("sync", ksc_d[mc, :, tc * 512:(tc + 1) * 512],
                           kstage[idx][:, :], waits=[("ve", v)], sem="ksd")
        return ("ve", v)

    emit_gemm(p, "k", wdram=wk_d, k_tile=128, n_kc=8, m_tile=128, n_mc=8,
              rhs_fn=lambda kc: z[:, kc * T:(kc + 1) * T], T=T, pool=pool,
              evac=ev_k, rhs_waits=zw, wbuf=wbufA)

    zdead = [("pe", p.v("pe")), ("ksd", p.v("ksd")), ("ve", p.v("ve"))]
    for h in range(NH):
        p.dma("gpsimd", khat(h), ksc_d[h], waits=zdead if h == 0 else [],
              sem="mld")
    hat_ready = ("mld", p.v("mld"))

    if debug:
        dq = p.dout("dq", [128, 8 * T], BF16)
        dk = p.dout("dk", [128, 8 * T], BF16)
        dv = p.dout("dv", [128, 32 * 528], BF16)
        dmask = p.dout("dmask", [128, 4 * 512], BF16)
        dbg_sc = p.dout("dbg_sc", [128, 512])
        wdbg = [hat_ready, ("ve", p.v("ve")), ("ac", p.v("ac")), mask_ready]
        for h in range(NH):
            p.dma("sync", dq[:, h * T:(h + 1) * T], qhat(h), waits=wdbg,
                  sem="stdma")
            p.dma("sync", dk[:, h * T:(h + 1) * T], khat(h), waits=wdbg,
                  sem="stdma")
        p.dma("sync", dv, vTx[:, :], waits=wdbg, sem="stdma")
        p.dma("sync", dmask, cmask_s[:, :], waits=wdbg, sem="stdma")

    ps_o = p.psum("ps_o", 128, 512)
    out_banks = [lr["ps_s"], lr["ps_q"], lr["ps_b"], ps_o]
    out_free = [p.barrier(("ac", "ve")) for _ in range(4)]
    pv_hist = []
    NHx, NQG = mini if mini else (NH, 8)
    for h in range(NHx):
        for qg in range(NQG):
            kts = list(range(max(0, 4 * (qg - 1)), 4 * qg + 4))
            exp_v = {}
            pv_done = {}
            for ki, kt in enumerate(kts):
                bs, bank_s, bfree_s = pool.acquire()
                w1 = list(bfree_s)
                if ki == 0:
                    w1 += vT_done + [("ve", p.v("ve"))] + [hat_ready]
                p.emit("tensor",
                       lambda e, kt=kt, qg=qg, h=h, bank_s=bank_s: e.matmul(
                           bank_s[:, :],
                           khat(h)[:, kt * 128:(kt + 1) * 128],
                           qhat(h)[:, qg * 512:(qg + 1) * 512],
                           start=True, stop=True),
                       waits=w1, sem="pe")
                mm2 = p.v("pe")
                off = kt * 128 - qg * 512
                oi = (off + 512) // 128
                mk = p.emit("vector",
                            lambda e, oi=oi, bank_s=bank_s: e.tensor_add(
                                bank_s[:, :], bank_s[:, :],
                                cmask_s[:, oi * 512:(oi + 1) * 512]),
                            waits=[("pe", mm2), mask_ready], sem="ve")
                wexp = [("ve", mk)]
                if len(pv_hist) >= 2:
                    wexp.append(("pe", pv_hist[-2]))
                ev = p.emit("scalar",
                            lambda e, ki=ki, bank_s=bank_s: e.activation(
                                p_sb[ki % 2][:, :], bank_s[:, :], AF.Exp),
                            waits=wexp, sem="ac")
                exp_v[kt] = ev
                pool.release(bs, [("ac", ev)])
                for qc in range(4):
                    w3 = [("ac", ev)]
                    if ki == 0:
                        w3 += list(out_free[qc])
                    p.emit("tensor",
                           lambda e, kt=kt, ki=ki, qc=qc, h=h: e.matmul(
                               out_banks[qc][:, 0:65],
                               p_sb[ki % 2][:, qc * 128:(qc + 1) * 128],
                               vTx[:, kt * 528 + h * 66: kt * 528 + h * 66 + 65],
                               start=(ki == 0), stop=(ki == len(kts) - 1)),
                           waits=w3, sem="pe")
                pv_done[kt] = p.v("pe")
                pv_hist.append(pv_done[kt])
            pvv = p.v("pe")
            for qc in range(4):
                p.emit("vector",
                       lambda e, qc=qc: e.reciprocal(rv_t[:, qc:qc + 1],
                                                     out_banks[qc][:, 64:65]),
                       waits=[("pe", pvv)], sem="ve")
            for qc in range(4):
                evv = p.emit("vector",
                             lambda e, qc=qc: e.tensor_scalar(
                                 ast[qc % 2][:, :], out_banks[qc][:, 0:64],
                                 rv_t[:, qc:qc + 1], None, ALU.mult),
                             waits=[("stdma", p.v("stdma"))] if qc < 2 else
                                   [("stdma", p.v("stdma") - 16)],
                             sem="ve")
                p.dma("sync",
                      attn_d[qg * 512 + qc * 128: qg * 512 + (qc + 1) * 128,
                             h * 64:(h + 1) * 64],
                      ast[qc % 2][:, :], waits=[("ve", evv)], sem="stdma")
                out_free[qc] = [("ve", evv)]
    return finish(p)


def host_l3attn(inputs, xfm_batches):
    Wq, Wk, Wv = (inputs["L3_attn_Wq"], inputs["L3_attn_Wk"],
                  inputs["L3_attn_Wv"])
    cmask = np.zeros((8, 128, 512), np.float32)
    kk = np.arange(128)[:, None]
    qq = np.arange(512)[None, :]
    for oi in range(8):
        off = oi * 128 - 512
        d = qq - kk - off
        cmask[oi] = np.where((d >= 0) & (d < 512), 0.0, -1e30)

    def padheads(w):
        wp = np.zeros((1024, 1024), np.float32)
        for h in range(8):
            wp[h * 128:h * 128 + 64] = w[h * 64:(h + 1) * 64]
        return wp

    zeros8 = np.zeros((128, 8), np.float32)
    zv = np.zeros((1, 512), np.float32)
    in_maps = []
    for c in range(NCORES):
        b, g = c // 2, c % 2
        hs = slice(512 * g, 512 * (g + 1))
        in_maps.append({
            "x": xfm_batches[b],
            "wq": mcform(np.ascontiguousarray(padheads(Wq.T[:, hs].T).T)),
            "wk": mcform(np.ascontiguousarray(padheads(Wk.T[:, hs].T).T)),
            "wv": np.ascontiguousarray(Wv[hs].T.reshape(8, 128, 512)),
            "bq": zeros8, "bk": zeros8, "bv": zv,
            "cmask": cmask,
        })
    return in_maps


def run_l3attn(inputs, xfm_batches):
    nc = build_l3attn()
    res = run_bass_kernel_spmd(nc, host_l3attn(inputs, xfm_batches),
                               core_ids=list(range(NCORES)))
    outs = []
    for b in range(4):
        outs.append(np.concatenate(
            [np.asarray(res.results[2 * b]["attn"]),
             np.asarray(res.results[2 * b + 1]["attn"])], axis=1))
    return outs


def host_l2attn(inputs, xfm_batches):
    wqkv = inputs["L2_attn_w_qkv"]
    bqkv = inputs["L2_attn_b_qkv"]
    qpos = np.zeros((4, S), np.float32)
    kpos = np.zeros((4, S), np.float32)
    pos = np.arange(S)
    qpos[0] = -(pos & ~255).astype(np.float32)
    qpos[1] = -(pos & 255).astype(np.float32)
    qpos[2] = 1.0
    qpos[3] = 1.0
    kpos[0] = 1.0
    kpos[1] = 1.0
    kpos[2] = (pos & ~255).astype(np.float32)
    kpos[3] = (pos & 255).astype(np.float32)
    cmask = np.zeros((4, 128, 512), np.float32)
    kk = np.arange(128)[:, None]
    qq = np.arange(512)[None, :]
    for oi in range(4):
        cmask[oi] = np.where(qq - kk >= oi * 128, 0.0, -1e30)
    def padheads(w):
        wp = np.zeros((1024, 1024), np.float32)
        for h in range(8):
            wp[h * 128:h * 128 + 64] = w[h * 64:(h + 1) * 64]
        return wp

    def padbias(bb):
        bp = np.zeros(1024, np.float32)
        for h in range(8):
            bp[h * 128:h * 128 + 64] = bb[h * 64:(h + 1) * 64]
        return bp

    in_maps = []
    for c in range(NCORES):
        b, g = c // 2, c % 2
        qs = slice(512 * g, 512 * (g + 1))
        ks = slice(1024 + 512 * g, 1024 + 512 * (g + 1))
        vs = slice(2048 + 512 * g, 2048 + 512 * (g + 1))
        in_maps.append({
            "x": xfm_batches[b],
            "wq": mcform(np.ascontiguousarray(padheads(wqkv[qs]).T)),
            "wk": mcform(np.ascontiguousarray(padheads(wqkv[ks]).T)),
            "wv": np.ascontiguousarray(wqkv[vs].T.reshape(8, 128, 512)),
            "bq": biasform(padbias(bqkv[qs])),
            "bk": biasform(padbias(bqkv[ks])),
            "bv": np.ascontiguousarray(bqkv[vs].reshape(1, 512)),
            "qpos": qpos, "kpos": kpos, "cmask": cmask,
        })
    return in_maps


def run_l2attn(inputs, xfm_batches):
    nc = build_l2attn()
    res = run_bass_kernel_spmd(nc, host_l2attn(inputs, xfm_batches),
                               core_ids=list(range(NCORES)))
    outs = []
    for b in range(4):
        outs.append(np.concatenate(
            [np.asarray(res.results[2 * b]["attn"]),
             np.asarray(res.results[2 * b + 1]["attn"])], axis=1))
    return outs



def build_attnout():
    p = P("attnout")
    x_d = p.din("x", [D, TPC])
    attn_d = p.din("attn", [D, TPC], BF16)
    wo_d = p.din("wo", [8, 128, 8 * 128])
    bo_d = p.din("bo", [128, 8])
    out_d = p.dout("out", [D, TPC])

    consts = setup_consts(p)
    lr = make_ln_res(p, TPC)
    pool = BankPool(p, "gb", 4)
    bo_s = p.sb("bo_s", 128, 8)
    bv = p.dma("sync", bo_s[:, :], bo_d)
    bias_ready = ("syncdma", bv)

    ab = p.sb("ab", 128, NKC * TPC, BF16)
    inner = p.sb("inner", 128, NKC * TPC, BF16)
    av = p.dma("sync", ab[:, :].rearrange("p (kc t) -> p kc t", kc=NKC),
               attn_d.rearrange("(kc p) t -> p kc t", p=128), sem="xre")

    def ev(mc, tc, bank_ap, waits):
        v = p.emit("vector",
                   lambda e, mc=mc, tc=tc, bank_ap=bank_ap: e.tensor_scalar(
                       inner[:, mc * TPC + tc * 512: mc * TPC + tc * 512 + 512],
                       bank_ap, bo_s[:, mc:mc + 1], None, ALU.add),
                   waits=list(waits) + [bias_ready], sem="ve")
        return ("ve", v)

    emit_gemm(p, "o", wdram=wo_d, k_tile=128, n_kc=8, m_tile=128, n_mc=8,
              rhs_fn=lambda kc: ab[:, kc * TPC:(kc + 1) * TPC], T=TPC,
              pool=pool, evac=ev, rhs_waits=[("xre", av)])
    emit_sandwich_close(p, "cl", inner, TPC, consts, lr,
                        inner_waits=[("ve", p.v("ve"))], x_dram=x_d,
                        out_dram=out_d)
    return finish(p)


def host_attnout(w_out, b_out, attn_tm_batches, stream):
    wo = mcform(np.ascontiguousarray(w_out.T))
    bo = biasform(b_out)
    in_maps = []
    for c in range(NCORES):
        b, half = c // 2, c % 2
        a_fm = np.ascontiguousarray(
            attn_tm_batches[b][half * TPC:(half + 1) * TPC].T)
        in_maps.append({"x": np.ascontiguousarray(stream[c]), "attn": a_fm,
                        "wo": wo, "bo": bo})
    return in_maps


def run_attnout(w_out, b_out, attn_tm_batches, stream):
    nc = build_attnout()
    res = run_bass_kernel_spmd(nc, host_attnout(w_out, b_out, attn_tm_batches,
                                                stream),
                               core_ids=list(range(NCORES)))
    return [np.asarray(r["out"]) for r in res.results]



def _batches_to_stream(xfm):
    return [np.ascontiguousarray(xfm[c // 2][:, (c % 2) * TPC:(c % 2 + 1) * TPC])
            for c in range(NCORES)]

def _stream_to_batches(stream):
    return [np.ascontiguousarray(np.concatenate([stream[2 * b], stream[2 * b + 1]],
                                                axis=1)) for b in range(B)]

def kernel(**inputs):
    inputs = {k: np.asarray(v) for k, v in inputs.items()}
    x0 = inputs["L0_attn_x"].astype(np.float32)
    xfm = [np.ascontiguousarray(x0[b].T) for b in range(B)]

    xfm = run_l0attn(inputs, xfm)
    stream = _batches_to_stream(xfm)
    stream = run_l0ffn(inputs, stream)

    xfm = _stream_to_batches(stream)
    xfm = run_l1attn(inputs, xfm)
    stream = _batches_to_stream(xfm)
    stream = run_l1ffn(inputs, stream)

    xfm = _stream_to_batches(stream)
    attn = run_l2attn(inputs, xfm)
    stream = run_attnout(inputs["L2_attn_w_out"], inputs["L2_attn_b_out"],
                         attn, stream)
    stream = run_l2ffn(inputs, stream)

    xfm = _stream_to_batches(stream)
    attn = run_l3attn(inputs, xfm)
    stream = run_attnout(inputs["L3_attn_Wo"],
                         np.zeros(D, np.float32), attn, stream)
    stream = run_l3ffn(inputs, stream)

    out = np.empty((B, S, D), np.float32)
    for b in range(B):
        out[b] = np.concatenate([stream[2 * b], stream[2 * b + 1]], axis=1).T
    return out


# version 03
# speedup vs baseline: 1.1696x; 1.1696x over previous
import numpy as np
import ml_dtypes
from contextlib import ExitStack

import concourse.bass as bass
import concourse.mybir as mybir
from concourse.bass_utils import run_bass_kernel_spmd

F32 = mybir.dt.float32
BF16 = mybir.dt.bfloat16
AF = mybir.ActivationFunctionType
ALU = mybir.AluOpType
AX = mybir.AxisListType

NCORES = 8
D = 1024
NKC = 8
B = 4
S = 4096
TPC = 2048
EPS_LN = 1e-5

BF = ml_dtypes.bfloat16



def mcform(wt, m_tile=128, k_tile=128):
    K, M = wt.shape
    assert K % k_tile == 0 and M % m_tile == 0, (K, M, k_tile, m_tile)
    n_kc, n_mc = K // k_tile, M // m_tile
    return np.ascontiguousarray(
        wt.reshape(n_kc, k_tile, n_mc, m_tile).transpose(2, 1, 0, 3)
        .reshape(n_mc, k_tile, n_kc * m_tile).astype(np.float32))


def biasform(b_vec, m_tile=128):
    M = b_vec.shape[0]
    n_mc = M // m_tile
    return np.ascontiguousarray(b_vec.reshape(n_mc, m_tile).T.astype(np.float32))


def fm(x_tokmajor):
    return np.ascontiguousarray(x_tokmajor.T)



class P:
    ENGINES = ("sync", "tensor", "vector", "scalar", "gpsimd")

    def __init__(self, name="prog"):
        self.nc = bass.Bass("TRN2", target_bir_lowering=False)
        self.es = ExitStack()
        self.es.__enter__()
        self.q = {e: [] for e in self.ENGINES}
        self.sems = {}
        self.semv = {}
        self._uid = 0

    def uid(self):
        self._uid += 1
        return self._uid

    def din(self, name, shape, dt=F32):
        return self.nc.dram_tensor(name, list(shape), dt, kind="ExternalInput").ap()

    def dout(self, name, shape, dt=F32):
        return self.nc.dram_tensor(name, list(shape), dt, kind="ExternalOutput").ap()

    def dscratch(self, name, shape, dt=F32):
        return self.nc.dram_tensor(name, list(shape), dt, kind="Internal").ap()

    def sb(self, name, part, free, dt=F32):
        return self.es.enter_context(self.nc.sbuf_tensor(name, [part, free], dt))

    def psum(self, name, part=128, free=512, dt=F32):
        return self.es.enter_context(self.nc.psum_tensor(name, [part, free], dt))

    def _sem(self, name):
        if name not in self.sems:
            self.sems[name] = self.es.enter_context(self.nc.semaphore(name))
            self.semv[name] = 0
        return self.sems[name]

    def v(self, name):
        return self.semv.get(name, 0)

    def barrier(self, engines=("pe", "ve", "ac", "gp")):
        return [(s, self.v(s)) for s in engines if self.v(s) > 0]

    def emit(self, eng, fn, waits=(), sem=None, inc=1):
        waits = tuple(w for w in waits if w is not None and w[1] > 0)
        for s, _ in waits:
            self._sem(s)
        tgt = None
        if sem is not None:
            self._sem(sem)
            self.semv[sem] += inc
            tgt = self.semv[sem]
        self.q[eng].append((fn, waits, sem, inc))
        return tgt

    def dma(self, queue, out, in_, waits=(), sem=None):
        if sem is None:
            sem = queue + "dma"
        return self.emit(queue, lambda e: e.dma_start(out=out, in_=in_),
                         waits, sem=sem, inc=16)

    def build(self):
        block = self.es.enter_context(self.nc.Block())
        for eng in self.ENGINES:
            ops = self.q[eng]
            if not ops:
                continue

            def body(e, ops=ops):
                for fn, waits, sem, inc in ops:
                    for s, v in waits:
                        e.wait_ge(self.sems[s], v)
                    ins = fn(e)
                    if sem is not None:
                        ins.then_inc(self.sems[sem], inc)
            getattr(block, eng)(body)
        self.es.close()
        return self.nc


class BankPool:
    def __init__(self, p, prefix, n, free=512):
        self.p = p
        self.banks = [p.psum(f"{prefix}{i}", 128, free) for i in range(n)]
        self.free_at = [() for _ in self.banks]
        self.i = 0

    def acquire(self):
        b = self.i % len(self.banks)
        self.i += 1
        return b, self.banks[b], self.free_at[b]

    def release(self, b, waits):
        self.free_at[b] = tuple(waits)



def setup_consts(p, fvals=()):
    c = {}
    c["ones_bf"] = p.sb("ones_bf", 128, 1, BF16)
    c["ones1f"] = p.sb("ones1f", 1, 128, F32)
    p.emit("vector", lambda e: e.memset(c["ones_bf"][:, :], 1.0), sem="ve")
    p.emit("vector", lambda e: e.memset(c["ones1f"][:, :], 1.0), sem="ve")
    for i, fv in enumerate(fvals):
        t = p.sb(f"cf{i}", 128, 1, F32)
        p.emit("vector", lambda e, t=t, fv=fv: e.memset(t[:, :], fv), sem="ve")
        p.nc.const_aps.register(F32, float(fv), t[:, :])
    c["ready"] = ("ve", p.v("ve"))
    return c


def make_ln_res(p, Tmax):
    r = {}
    r["Tmax"] = Tmax
    r["sq"] = [p.sb("ln_sq0", 128, 512, BF16), p.sb("ln_sq1", 128, 512, BF16)]
    rows = p.sb("ln_rows", 1, 4 * Tmax, F32)
    r["srow"] = rows[0:1, 0:Tmax]
    r["qrow"] = rows[0:1, Tmax:2 * Tmax]
    r["tmp"] = rows[0:1, 2 * Tmax:3 * Tmax]
    r["mrrow"] = rows[0:1, 3 * Tmax:4 * Tmax]
    r["mb"] = p.sb("ln_mb", 128, Tmax, F32)
    r["rb"] = p.sb("ln_rb", 128, Tmax, F32)
    r["ps_s"] = p.psum("ln_ps_s", 128, 512)
    r["ps_q"] = p.psum("ln_ps_q", 128, 512)
    r["ps_b"] = p.psum("ln_ps_b", 128, 512)
    return r


def emit_ln(p, xb, T, consts, lr, *, xb_waits, z_out, n_kc=NKC, cstride=None,
            off=0):
    if cstride is None:
        cstride = T
    n_tc = T // 512
    bar = p.barrier()

    def C(kc, c0, c1):
        return (kc * cstride + off + c0, kc * cstride + off + c1)

    ev_s = ev_q = None
    sqv_prev = {}
    for tc in range(n_tc):
        for kc in range(n_kc):
            w = list(xb_waits) + [consts["ready"]] + bar if kc == 0 else []
            if kc == 0 and ev_s is not None:
                w.append(("ac", ev_s))
            p.emit("tensor",
                   lambda e, kc=kc, tc=tc: e.matmul(
                       lr["ps_s"][0:1, :],
                       consts["ones_bf"][:, :],
                       xb[:, slice(*C(kc, tc * 512, (tc + 1) * 512))],
                       start=(kc == 0), stop=(kc == n_kc - 1)),
                   waits=w, sem="pe")
        ev_s = p.emit("scalar",
                      lambda e, tc=tc: e.activation(
                          lr["srow"][0:1, tc * 512:(tc + 1) * 512],
                          lr["ps_s"][0:1, :], AF.Copy),
                      waits=[("pe", p.v("pe"))], sem="ac")
        for kc in range(n_kc):
            sqb = lr["sq"][kc % 2]
            w = list(xb_waits) + bar
            if (tc, kc) in sqv_prev or (tc > 0 or kc >= 2):
                prev = sqv_prev.get((kc % 2))
                if prev is not None:
                    w.append(("pe", prev))
            sv = p.emit("scalar",
                        lambda e, kc=kc, tc=tc, sqb=sqb: e.activation(
                            sqb[:, :],
                            xb[:, slice(*C(kc, tc * 512, (tc + 1) * 512))],
                            AF.Square),
                        waits=w, sem="ac")
            w2 = [("ac", sv)]
            if kc == 0 and ev_q is not None:
                w2.append(("ac", ev_q))
            mv = p.emit("tensor",
                        lambda e, kc=kc, sqb=sqb: e.matmul(
                            lr["ps_q"][0:1, :], consts["ones_bf"][:, :],
                            sqb[:, :], start=(kc == 0), stop=(kc == n_kc - 1)),
                        waits=w2, sem="pe")
            sqv_prev[kc % 2] = mv
        ev_q = p.emit("scalar",
                      lambda e, tc=tc: e.activation(
                          lr["qrow"][0:1, tc * 512:(tc + 1) * 512],
                          lr["ps_q"][0:1, :], AF.Copy),
                      waits=[("pe", p.v("pe"))], sem="ac")

    v3 = p.emit("scalar", lambda e: e.mul(lr["srow"][0:1, 0:T], lr["srow"][0:1, 0:T],
                                          1.0 / D),
                waits=[("ac", ev_s), ("ac", ev_q)], sem="ac")
    v4 = p.emit("vector", lambda e: e.tensor_mul(lr["tmp"][0:1, 0:T],
                                                 lr["srow"][0:1, 0:T],
                                                 lr["srow"][0:1, 0:T]),
                waits=[("ac", v3)], sem="ve")
    v5a = p.emit("vector",
                 lambda e: e.tensor_scalar(lr["qrow"][0:1, 0:T],
                                           lr["qrow"][0:1, 0:T], 1.0 / D,
                                           EPS_LN, ALU.mult, ALU.add),
                 waits=[], sem="ve")
    v5 = p.emit("vector",
                lambda e: e.tensor_sub(lr["qrow"][0:1, 0:T], lr["qrow"][0:1, 0:T],
                                       lr["tmp"][0:1, 0:T]),
                waits=[], sem="ve")
    v6 = p.emit("scalar", lambda e: e.activation(lr["qrow"][0:1, 0:T],
                                                 lr["qrow"][0:1, 0:T], AF.Sqrt),
                waits=[("ve", v5)], sem="ac")
    v6b = p.emit("vector", lambda e: e.tensor_copy(lr["tmp"][0:1, 0:T],
                                                   lr["srow"][0:1, 0:T]),
                 sem="ve")
    v7 = p.emit("vector", lambda e: e.reciprocal(lr["qrow"][0:1, 0:T],
                                                 lr["qrow"][0:1, 0:T]),
                waits=[("ac", v6)], sem="ve")
    v7b = p.emit("vector", lambda e: e.memset(lr["ps_b"] if False else
                                              lr["mb"][0:1, 0:8], 0.0),
                 sem="ve")
    v8 = p.emit("vector",
                lambda e: e.scalar_tensor_tensor(lr["mrrow"][0:1, 0:T],
                                                 lr["tmp"][0:1, 0:T], -1.0,
                                                 lr["qrow"][0:1, 0:T],
                                                 ALU.mult, ALU.mult),
                waits=[("ve", v7)], sem="ve")

    eprev = None
    for tc in range(n_tc):
        w = [consts["ready"], ("ve", v8)]
        if eprev is not None:
            w.append(("ac", eprev))
        g1 = p.emit("tensor",
                    lambda e, tc=tc: e.matmul(lr["ps_b"][:, :],
                                              consts["ones1f"][:, :],
                                              lr["qrow"][0:1, tc * 512:(tc + 1) * 512],
                                              start=True, stop=True),
                    waits=w, sem="pe")
        e1 = p.emit("scalar",
                    lambda e, tc=tc: e.activation(
                        lr["rb"][:, tc * 512:(tc + 1) * 512], lr["ps_b"][:, :],
                        AF.Copy),
                    waits=[("pe", g1)], sem="ac")
        g2 = p.emit("tensor",
                    lambda e, tc=tc: e.matmul(lr["ps_b"][:, :],
                                              consts["ones1f"][:, :],
                                              lr["mrrow"][0:1, tc * 512:(tc + 1) * 512],
                                              start=True, stop=True),
                    waits=[("ac", e1)], sem="pe")
        e2 = p.emit("scalar",
                    lambda e, tc=tc: e.activation(
                        lr["mb"][:, tc * 512:(tc + 1) * 512], lr["ps_b"][:, :],
                        AF.Copy),
                    waits=[("pe", g2)], sem="ac")
        eprev = e2
    bcast_done = ("ac", eprev)

    z_ready = []
    for kc in range(n_kc):
        p.emit("vector",
               lambda e, kc=kc: e.tensor_mul(z_out[:, slice(*C(kc, 0, T))],
                                             xb[:, slice(*C(kc, 0, T))],
                                             lr["rb"][:, 0:T]),
               waits=[bcast_done], sem="ve")
        vz = p.emit("vector",
                    lambda e, kc=kc: e.tensor_add(z_out[:, slice(*C(kc, 0, T))],
                                                  z_out[:, slice(*C(kc, 0, T))],
                                                  lr["mb"][:, 0:T]),
                    sem="ve")
        z_ready.append(("ve", vz))
    return z_ready, {"mb": lr["mb"], "rb": lr["rb"]}


def emit_gemm(p, tag, *, wdram, k_tile, n_kc, m_tile, n_mc, rhs_fn, T, pool,
              evac, rhs_waits=(), wbuf=None):
    n_tc = T // 512
    if wbuf is None:
        wbuf = [p.sb(f"w{tag}{i}", k_tile, n_kc * m_tile, BF16) for i in range(2)]
    nb = len(wbuf)
    slab_ready = [None] * n_mc
    group_end = [None] * n_mc
    pe0 = ("pe", p.v("pe"))

    def issue_dma(mc):
        w = [pe0] if mc < nb else [("pe", group_end[mc - nb])]
        slab_ready[mc] = p.dma("gpsimd", wbuf[mc % nb][:, :], wdram[mc], waits=w,
                               sem="wdma")

    for mc in range(min(nb, n_mc)):
        issue_dma(mc)

    for mc in range(n_mc):
        for tc in range(n_tc):
            bidx, bank, bfree = pool.acquire()
            for kc in range(n_kc):
                w = []
                if kc == 0:
                    w = [("wdma", slab_ready[mc])] + list(bfree) + list(rhs_waits)
                p.emit("tensor",
                       lambda e, mc=mc, tc=tc, kc=kc, bank=bank: e.matmul(
                           bank[0:m_tile, :],
                           wbuf[mc % nb][0:k_tile, kc * m_tile:(kc + 1) * m_tile],
                           rhs_fn(kc)[:, tc * 512:(tc + 1) * 512],
                           start=(kc == 0), stop=(kc == n_kc - 1)),
                       waits=w, sem="pe")
            done = evac(mc, tc, bank[0:m_tile, :], [("pe", p.v("pe"))])
            pool.release(bidx, [done])
        group_end[mc] = p.v("pe")
        if mc + nb < n_mc:
            issue_dma(mc + nb)
    return group_end[-1]


def evac_to(p, out_fn, func=AF.Copy, bias_sb=None, scale=1.0):
    if bias_sb is None:
        def ev(mc, tc, bank_ap, waits):
            v = p.emit("scalar",
                       lambda e, mc=mc, tc=tc, bank_ap=bank_ap: e.activation(
                           out_fn(mc, tc), bank_ap, func, bias=0.0, scale=scale),
                       waits=waits, sem="ac")
            return ("ac", v)
    else:
        assert func is AF.Copy and scale == 1.0

        def ev(mc, tc, bank_ap, waits):
            v = p.emit("vector",
                       lambda e, mc=mc, tc=tc, bank_ap=bank_ap: e.tensor_scalar(
                           out_fn(mc, tc), bank_ap, bias_sb[:, mc:mc + 1], None,
                           ALU.add),
                       waits=waits, sem="ve")
            return ("ve", v)
    return ev


def emit_sandwich_close(p, tag, inner_b, T, consts, lr, *, inner_waits, x_dram,
                        out_dram, n_kc=NKC, bufs=None, cstride=None, off=0):
    if cstride is None:
        cstride = T
    zw, _ = emit_ln(p, inner_b, T, consts, lr, xb_waits=inner_waits,
                    z_out=inner_b, n_kc=n_kc, cstride=cstride, off=off)
    if bufs is None:
        xch = [p.sb(f"xc{tag}{i}", 128, T, F32) for i in range(2)]
        och = [p.sb(f"oc{tag}{i}", 128, T, F32) for i in range(2)]
    else:
        xch, och = bufs
    st0 = ("stdma", p.v("stdma"))
    add_v = {}
    st_v = {}
    for kc in range(n_kc):
        w_x = list(inner_waits) if kc < 2 else []
        if kc >= 2:
            w_x.append(("ve", add_v[kc - 2]))
        xv = p.dma("sync", xch[kc % 2][:, :],
                   x_dram.rearrange("(kc p) t -> kc p t", p=128)[kc],
                   waits=w_x, sem="xre")
        w_a = [zw[kc], ("xre", xv)]
        w_a.append(("stdma", st_v[kc - 2]) if kc >= 2 else st0)
        av = p.emit("vector",
                    lambda e, kc=kc: e.tensor_add(
                        och[kc % 2][:, :], xch[kc % 2][:, :],
                        inner_b[:, kc * cstride + off: kc * cstride + off + T]),
                    waits=w_a, sem="ve")
        add_v[kc] = av
        st_v[kc] = p.dma("sync",
                         out_dram.rearrange("(kc p) t -> kc p t", p=128)[kc],
                         och[kc % 2][:, :], waits=[("ve", av)], sem="stdma")
    return st_v[n_kc - 1]


def finish(p):
    p.emit("sync", lambda e: e.nop(), waits=[("stdma", p.v("stdma"))])
    return p.build()



def build_l1ffn():
    p = P("l1ffn")
    TH = 1024
    x_d = p.din("x", [D, TPC])
    w1_d = p.din("w1", [32, 128, 8 * 128])
    b1_d = p.din("b1", [128, 32])
    w2_d = p.din("w2", [8, 128, 32 * 128])
    b2_d = p.din("b2", [128, 8])
    out_d = p.dout("out", [D, TPC])

    consts = setup_consts(p)
    lr = make_ln_res(p, TH)
    pool = BankPool(p, "gb", 4)
    b1_s = p.sb("b1s", 128, 32)
    b2_s = p.sb("b2s", 128, 8)
    p.dma("sync", b1_s[:, :], b1_d)
    bv = p.dma("sync", b2_s[:, :], b2_d)
    bias_ready = ("syncdma", bv)

    xb = p.sb("xb", 128, NKC * TH, BF16)
    z = p.sb("z", 128, NKC * TH, BF16)
    h1 = p.sb("h1", 128, 32 * TH, BF16)
    inner = p.sb("inner", 128, NKC * TH, BF16)
    w1buf = [p.sb(f"w1b{i}", 128, 8 * 128, BF16) for i in range(2)]
    w2buf = [p.sb(f"w2b{i}", 128, 32 * 128, BF16) for i in range(2)]
    clbufs = ([p.sb("clx0", 128, TH, F32), p.sb("clx1", 128, TH, F32)],
              [p.sb("clo0", 128, TH, F32), p.sb("clo1", 128, TH, F32)])

    for half in range(2):
        xs = x_d[:, half * TH:(half + 1) * TH]
        os_ = out_d[:, half * TH:(half + 1) * TH]
        bar = p.barrier()
        xv = p.dma("gpsimd", xb[:, :].rearrange("p (kc t) -> p kc t", kc=NKC),
                   xs.rearrange("(kc p) t -> p kc t", p=128), waits=bar,
                   sem="xldc")
        zw, _ = emit_ln(p, xb, TH, consts, lr, xb_waits=[("xldc", xv)], z_out=z)

        def ev1(mc, tc, bank_ap, waits):
            v = p.emit("vector",
                       lambda e, mc=mc, tc=tc, bank_ap=bank_ap: e.tensor_scalar(
                           h1[:, mc * TH + tc * 512: mc * TH + (tc) * 512 + 512],
                           bank_ap, b1_s[:, mc:mc + 1], None, ALU.add),
                       waits=list(waits) + [bias_ready], sem="ve")
            return ("ve", v)

        emit_gemm(p, f"g1h{half}", wdram=w1_d, k_tile=128, n_kc=8, m_tile=128,
                  n_mc=32, rhs_fn=lambda kc: z[:, kc * TH:(kc + 1) * TH],
                  T=TH, pool=pool, evac=ev1, rhs_waits=zw, wbuf=w1buf)
        h1_done = [("ve", p.v("ve"))]

        def ev2(mc, tc, bank_ap, waits):
            v = p.emit("vector",
                       lambda e, mc=mc, tc=tc, bank_ap=bank_ap: e.tensor_scalar(
                           inner[:, mc * TH + tc * 512: mc * TH + tc * 512 + 512],
                           bank_ap, b2_s[:, mc:mc + 1], None, ALU.add),
                       waits=list(waits) + [bias_ready], sem="ve")
            return ("ve", v)

        emit_gemm(p, f"g2h{half}", wdram=w2_d, k_tile=128, n_kc=32, m_tile=128,
                  n_mc=8, rhs_fn=lambda kc: h1[:, kc * TH:(kc + 1) * TH],
                  T=TH, pool=pool, evac=ev2, rhs_waits=h1_done, wbuf=w2buf)
        inner_waits = [("ve", p.v("ve"))]
        emit_sandwich_close(p, f"cl{half}", inner, TH, consts, lr,
                            inner_waits=inner_waits, x_dram=xs, out_dram=os_,
                            bufs=clbufs)
    return finish(p)


def host_l1ffn(inputs, stream):
    W1, b1 = inputs["L1_ffn_W1"], inputs["L1_ffn_b1"]
    W2, b2 = inputs["L1_ffn_W2"], inputs["L1_ffn_b2"]
    in_maps = []
    for c in range(NCORES):
        in_maps.append({
            "x": np.ascontiguousarray(stream[c]),
            "w1": mcform(np.ascontiguousarray(W1[c].T)),
            "b1": biasform(b1[c]),
            "w2": mcform(np.ascontiguousarray(W2[c].T)),
            "b2": biasform(b2[c]),
        })
    return in_maps


def run_l1ffn(inputs, stream):
    nc = build_l1ffn()
    res = run_bass_kernel_spmd(nc, host_l1ffn(inputs, stream),
                               core_ids=list(range(NCORES)))
    return [np.asarray(r["out"]) for r in res.results]



def build_l0ffn():
    p = P("l0ffn")
    T = TPC
    x_d = p.din("x", [D, T])
    bp_d = p.din("bigp", [2, 4, 128, 2048])
    bq_d = p.din("bigq", [2, 4, 128, 2048])
    bias_d = p.din("bias", [128, 8])
    out_d = p.dout("out", [D, T])

    consts = setup_consts(p)
    lr = make_ln_res(p, T)
    pool = BankPool(p, "gb", 4)

    bias_s = p.sb("bias_s", 128, 8)
    p.dma("sync", bias_s[:, :], bias_d)
    bias_ready = ("syncdma", p.v("syncdma"))

    bp_s = [p.sb(f"bp_s{i}", 128, 2048) for i in range(2)]
    bq_s = [p.sb(f"bq_s{i}", 128, 2048) for i in range(2)]
    tl = {}
    tmp = p.sb("tl_tmp", 128, 2048)
    acc = p.sb("tl_acc", 128, 2048)
    tl["r"] = p.sb("tl_r", 128, 2048, BF16)
    tl["i"] = p.sb("tl_i", 128, 2048, BF16)
    tl["in"] = p.sb("tl_in", 128, 2048, BF16)
    tl_ready = None
    for part, nm in ((0, "r"), (1, "i")):
        for cc in range(4):
            bw = [("ve", p.v("ve"))] if (part or cc >= 2) else []
            p.dma("sync", bp_s[cc % 2][:, :], bp_d[part][cc], waits=bw,
                  sem="syncdma")
            lv = p.dma("sync", bq_s[cc % 2][:, :], bq_d[part][cc], waits=bw,
                       sem="syncdma")
            w = [("syncdma", lv)]
            if cc == 0:
                p.emit("vector",
                       lambda e, c2=cc % 2: e.tensor_mul(acc[:, :],
                                                         bp_s[c2][:, :],
                                                         bq_s[c2][:, :]),
                       waits=w, sem="ve")
            else:
                p.emit("vector",
                       lambda e, c2=cc % 2: e.tensor_mul(tmp[:, :],
                                                         bp_s[c2][:, :],
                                                         bq_s[c2][:, :]),
                       waits=w, sem="ve")
                p.emit("vector", lambda e: e.tensor_add(acc[:, :], acc[:, :],
                                                        tmp[:, :]), sem="ve")
        p.emit("vector", lambda e, nm=nm: e.tensor_copy(tl[nm][:, :], acc[:, :]),
               sem="ve")
        if nm == "i":
            tl_ready = p.emit("vector",
                              lambda e: e.tensor_scalar_mul(tl["in"][:, :],
                                                            acc[:, :], -1.0),
                              sem="ve")
        else:
            tl_ready = p.v("ve")
    tl_wait = ("ve", tl_ready)

    def lhsT_fn(kc, mc):
        if kc < 4 and mc < 4:
            return tl["r"][:, kc * 512 + mc * 128: kc * 512 + mc * 128 + 128]
        if kc < 4:
            return tl["i"][:, kc * 512 + (mc - 4) * 128: kc * 512 + (mc - 4) * 128 + 128]
        if mc < 4:
            return tl["in"][:, (kc - 4) * 512 + mc * 128: (kc - 4) * 512 + mc * 128 + 128]
        return tl["r"][:, (kc - 4) * 512 + (mc - 4) * 128: (kc - 4) * 512 + (mc - 4) * 128 + 128]

    xb = p.sb("xb", 128, NKC * T, BF16)
    z = xb
    inner = p.sb("inner", 128, NKC * T, BF16)

    xv = p.dma("gpsimd", xb[:, :].rearrange("p (kc t) -> p kc t", kc=NKC),
               x_d.rearrange("(kc p) t -> p kc t", p=128), sem="xldc")
    zw, _ = emit_ln(p, xb, T, consts, lr, xb_waits=[("xldc", xv)], z_out=z)

    def ev(mc, tc, bank_ap, waits):
        v = p.emit("vector",
                   lambda e, mc=mc, tc=tc, bank_ap=bank_ap: e.tensor_scalar(
                       inner[:, mc * T + tc * 512: mc * T + tc * 512 + 512],
                       bank_ap, bias_s[:, mc:mc + 1], None, ALU.add),
                   waits=list(waits) + [bias_ready], sem="ve")
        return ("ve", v)

    n_tc = T // 512
    for mc in range(8):
        banks = []
        for tc in range(n_tc):
            bidx, bank, bfree = pool.acquire()
            banks.append((bidx, bank))
            for kc in range(8):
                w = []
                if kc == 0:
                    w = [tl_wait] + list(bfree) + zw
                p.emit("tensor",
                       lambda e, mc=mc, tc=tc, kc=kc, bank=bank: e.matmul(
                           bank[:, :], lhsT_fn(kc, mc),
                           z[:, kc * T + tc * 512: kc * T + (tc + 1) * 512],
                           start=(kc == 0), stop=(kc == 7)),
                       waits=w, sem="pe")
        ge = p.v("pe")
        for tc, (bidx, bank) in enumerate(banks):
            done = ev(mc, tc, bank[:, :], [("pe", ge)])
            pool.release(bidx, [done])

    emit_sandwich_close(p, "cl", inner, T, consts, lr,
                        inner_waits=[("ve", p.v("ve"))], x_dram=x_d, out_dram=out_d,
                        bufs=([bp_s[0], bp_s[1]], [tmp, acc]))
    return finish(p)


def host_l0ffn(inputs, stream):
    G1R, G2R = inputs["L0_ffn_G1_R"], inputs["L0_ffn_G2_R"]
    G1I, G2I = inputs["L0_ffn_G1_I"], inputs["L0_ffn_G2_I"]
    br, bi = inputs["L0_ffn_bias_real"], inputs["L0_ffn_bias_imag"]

    def expand(G1, G2):
        A = G1[0].transpose(1, 2, 0)
        Q = G2[:, :, 0, :].transpose(0, 2, 1)
        bigp = np.empty((4, 128, 2048), np.float32)
        bigq = np.empty((4, 128, 2048), np.float32)
        for c in range(4):
            fp = np.repeat(np.repeat(A[c], 16, axis=0), 16, axis=1)
            fq = np.tile(np.tile(Q[c], (32, 1)), (1, 32))
            bigp[c] = fp.reshape(4, 128, 512).transpose(1, 0, 2).reshape(128, 2048)
            bigq[c] = fq.reshape(4, 128, 512).transpose(1, 0, 2).reshape(128, 2048)
        return bigp, bigq

    bpr, bqr = expand(G1R, G2R)
    bpi, bqi = expand(G1I, G2I)
    bigp = np.stack([bpr, bpi]).astype(np.float32)
    bigq = np.stack([bqr, bqi]).astype(np.float32)
    bias = biasform(np.concatenate([br, bi]))
    in_maps = []
    for c in range(NCORES):
        in_maps.append({"x": np.ascontiguousarray(stream[c]),
                        "bigp": bigp, "bigq": bigq, "bias": bias})
    return in_maps


def run_l0ffn(inputs, stream):
    nc = build_l0ffn()
    res = run_bass_kernel_spmd(nc, host_l0ffn(inputs, stream),
                               core_ids=list(range(NCORES)))
    return [np.asarray(r["out"]) for r in res.results]



def build_l2ffn():
    p = P("l2ffn")
    TH = 1024
    x_d = p.din("x", [D, TPC])
    w1_d = p.din("w1", [32, 128, 8 * 128])
    w2_d = p.din("w2", [32, 128, 8 * 128])
    h1_d = p.din("h1", [32, 128, 32 * 128])
    h2_d = p.din("h2", [32, 128, 32 * 128])
    w3_d = p.din("w3", [8, 128, 32 * 128])
    s_d = p.din("scales", [128, 32 * 4 + 8])
    out_d = p.dout("out", [D, TPC])

    consts = setup_consts(p)
    lr = make_ln_res(p, TH)
    pool = BankPool(p, "gb", 4)
    s_s = p.sb("s_s", 128, 32 * 4 + 8)
    p.dma("sync", s_s[:, :], s_d)
    s_ready = ("syncdma", p.v("syncdma"))

    xb = p.sb("xb", 128, NKC * TH, BF16)
    z = xb
    ha = p.sb("ha", 128, 32 * TH, BF16)
    hb = p.sb("hb", 128, 32 * TH, BF16)
    wbufA = [p.sb(f"wA{i}", 128, 8 * 128, BF16) for i in range(2)]
    wbufB = [p.sb(f"wB{i}", 128, 32 * 128, BF16) for i in range(2)]
    clbufs = ([p.sb("clx0", 128, TH, F32), p.sb("clx1", 128, TH, F32)],
              [p.sb("clo0", 128, TH, F32), p.sb("clo1", 128, TH, F32)])

    for half in range(2):
        xs = x_d[:, half * TH:(half + 1) * TH]
        os_ = out_d[:, half * TH:(half + 1) * TH]
        bar = p.barrier()
        xv = p.dma("gpsimd", xb[:, :].rearrange("p (kc t) -> p kc t", kc=NKC),
                   xs.rearrange("(kc p) t -> p kc t", p=128), waits=bar,
                   sem="xldc")
        zw, _ = emit_ln(p, xb, TH, consts, lr, xb_waits=[("xldc", xv)], z_out=z)

        def ev_w1(mc, tc, bank_ap, waits):
            v = p.emit("scalar",
                       lambda e, mc=mc, tc=tc, bank_ap=bank_ap: e.activation(
                           ha[:, mc * TH + tc * 512: mc * TH + tc * 512 + 512],
                           bank_ap, AF.Silu, scale=s_s[:, mc:mc + 1]),
                       waits=list(waits) + [s_ready], sem="ac")
            return ("ac", v)

        emit_gemm(p, f"w1h{half}", wdram=w1_d, k_tile=128, n_kc=8, m_tile=128,
                  n_mc=32, rhs_fn=lambda kc: z[:, kc * TH:(kc + 1) * TH],
                  T=TH, pool=pool, evac=ev_w1, rhs_waits=zw, wbuf=wbufA)
        ha_done = ("ac", p.v("ac"))

        def ev_w2(mc, tc, bank_ap, waits):
            sl = ha[:, mc * TH + tc * 512: mc * TH + tc * 512 + 512]
            v = p.emit("vector",
                       lambda e, sl=sl, mc=mc, bank_ap=bank_ap: e.scalar_tensor_tensor(
                           sl, bank_ap, s_s[:, 32 + mc:32 + mc + 1], sl,
                           ALU.mult, ALU.mult),
                       waits=list(waits) + [s_ready, ha_done], sem="ve")
            return ("ve", v)

        emit_gemm(p, f"w2h{half}", wdram=w2_d, k_tile=128, n_kc=8, m_tile=128,
                  n_mc=32, rhs_fn=lambda kc: z[:, kc * TH:(kc + 1) * TH],
                  T=TH, pool=pool, evac=ev_w2, rhs_waits=zw, wbuf=wbufA)
        h_done = [("ve", p.v("ve"))]

        def ev_h1(mc, tc, bank_ap, waits):
            v = p.emit("scalar",
                       lambda e, mc=mc, tc=tc, bank_ap=bank_ap: e.activation(
                           hb[:, mc * TH + tc * 512: mc * TH + tc * 512 + 512],
                           bank_ap, AF.Gelu, scale=s_s[:, 64 + mc:64 + mc + 1]),
                       waits=list(waits) + [s_ready], sem="ac")
            return ("ac", v)

        emit_gemm(p, f"h1h{half}", wdram=h1_d, k_tile=128, n_kc=32, m_tile=128,
                  n_mc=32, rhs_fn=lambda kc: ha[:, kc * TH:(kc + 1) * TH],
                  T=TH, pool=pool, evac=ev_h1, rhs_waits=h_done, wbuf=wbufB)
        hb_done = [("ac", p.v("ac"))]

        def ev_h2(mc, tc, bank_ap, waits):
            v = p.emit("scalar",
                       lambda e, mc=mc, tc=tc, bank_ap=bank_ap: e.activation(
                           ha[:, mc * TH + tc * 512: mc * TH + tc * 512 + 512],
                           bank_ap, AF.Gelu, scale=s_s[:, 96 + mc:96 + mc + 1]),
                       waits=list(waits) + [s_ready], sem="ac")
            return ("ac", v)

        emit_gemm(p, f"h2h{half}", wdram=h2_d, k_tile=128, n_kc=32, m_tile=128,
                  n_mc=32, rhs_fn=lambda kc: hb[:, kc * TH:(kc + 1) * TH],
                  T=TH, pool=pool, evac=ev_h2, rhs_waits=hb_done, wbuf=wbufB)
        h2_done = [("ac", p.v("ac"))]

        def ev_w3(mc, tc, bank_ap, waits):
            v = p.emit("vector",
                       lambda e, mc=mc, tc=tc, bank_ap=bank_ap: e.tensor_scalar(
                           z[:, mc * TH + tc * 512: mc * TH + tc * 512 + 512],
                           bank_ap, s_s[:, 128 + mc:128 + mc + 1], None, ALU.mult),
                       waits=list(waits) + [s_ready], sem="ve")
            return ("ve", v)

        emit_gemm(p, f"w3h{half}", wdram=w3_d, k_tile=128, n_kc=32, m_tile=128,
                  n_mc=8, rhs_fn=lambda kc: ha[:, kc * TH:(kc + 1) * TH],
                  T=TH, pool=pool, evac=ev_w3, rhs_waits=h2_done, wbuf=wbufB)

        emit_sandwich_close(p, f"cl{half}", z, TH, consts, lr,
                            inner_waits=[("ve", p.v("ve"))], x_dram=xs,
                            out_dram=os_, bufs=clbufs)
    return finish(p)


def host_l2ffn(inputs, stream):
    scales = np.concatenate([
        biasform(inputs["L2_ffn_w1_s"]), biasform(inputs["L2_ffn_w2_s"]),
        biasform(inputs["L2_ffn_h1_s"]), biasform(inputs["L2_ffn_h2_s"]),
        biasform(inputs["L2_ffn_w3_s"])], axis=1)
    shared = {
        "w1": mcform(np.ascontiguousarray(inputs["L2_ffn_w1_w"].T)),
        "w2": mcform(np.ascontiguousarray(inputs["L2_ffn_w2_w"].T)),
        "h1": mcform(np.ascontiguousarray(inputs["L2_ffn_h1_w"].T)),
        "h2": mcform(np.ascontiguousarray(inputs["L2_ffn_h2_w"].T)),
        "w3": mcform(np.ascontiguousarray(inputs["L2_ffn_w3_w"].T)),
        "scales": np.ascontiguousarray(scales),
    }
    return [dict(shared, x=np.ascontiguousarray(stream[c])) for c in range(NCORES)]


def run_l2ffn(inputs, stream):
    nc = build_l2ffn()
    res = run_bass_kernel_spmd(nc, host_l2ffn(inputs, stream),
                               core_ids=list(range(NCORES)))
    return [np.asarray(r["out"]) for r in res.results]



def build_l3ffn(debug=False):
    p = P("l3ffn")
    TH = 1024
    NTCH = TH // 128
    x_d = p.din("x", [D, TPC])
    xtm_d = p.din("xtm", [TPC, D])
    wr_d = p.din("wr", [8, 128, 16])
    sg_d = p.din("sg", [8, 128, 8 * 128])
    su_d = p.din("su", [8, 128, 8 * 128])
    sd_d = p.din("sd", [8, 128, 1024])
    wg_d = p.din("wg", [16, 4, 128, 8 * 128])
    wu_d = p.din("wu", [16, 4, 128, 8 * 128])
    wd_d = p.din("wd", [16, 4, 128, 1024])
    out_d = p.dout("out", [TPC, D])

    consts = setup_consts(p)
    lr = make_ln_res(p, TH)
    pool = BankPool(p, "gb", 4)

    xb = p.sb("xb", 128, NKC * TH, BF16)
    z = xb
    hsh = p.sb("hsh", 128, NKC * TH, BF16)
    hga = p.sb("hga", 128, 4 * TH, BF16)
    acc = p.sb("acc", 128, NTCH * D)
    prow = p.sb("prow", 128, NTCH * 16)
    indt = p.sb("indt", 128, NTCH * 16)
    cwt = p.sb("cwt", 128, NTCH * 16)
    statB = p.sb("statB", 128, NTCH)
    statC = p.sb("statC", 128, NTCH)
    statD = p.sb("statD", 128, NTCH)
    sgbuf = [p.sb(f"sgb{i}", 128, 8 * 128, BF16) for i in range(2)]
    wgbuf = [p.sb(f"wgb{i}", 128, 8 * 128, BF16) for i in range(2)]
    sdbuf = p.sb("sdb", 128, 8 * 1024, BF16)
    wdbuf = [p.sb(f"wdb{i}", 128, 4 * 1024, BF16) for i in range(2)]
    wrbuf = p.sb("wrbuf", 128, 8 * 16, BF16)
    scol = p.sb("scol", 128, NTCH)
    qcol = p.sb("qcol", 128, NTCH)
    mcol = p.sb("mcol", 128, NTCH)
    rcol = p.sb("rcol", 128, NTCH)
    mrcol = p.sb("mrcol", 128, NTCH)
    trash = p.sb("trash", 128, D)
    xch = [p.sb(f"xtmc{i}", 128, D) for i in range(2)]
    och = [p.sb(f"otmc{i}", 128, D) for i in range(2)]

    dbg_prow = dbg_cwt = dbg_acc = None
    if debug:
        dbg_prow = p.dout("dbg_prow", [128, 8 * 16])
        dbg_cwt = p.dout("dbg_cwt", [128, 8 * 16])
        dbg_acc = p.dout("dbg_acc", [128, 8 * 1024])
        dbg_st = p.dout("dbg_st", [128, 40])
    wrv = p.dma("gpsimd", wrbuf[:, :].rearrange("p (kc f) -> p kc f", kc=8),
                wr_d.rearrange("kc p f -> p kc f"), sem="wdma")
    sdv = p.dma("gpsimd", sdbuf[:, :].rearrange("p (kc f) -> p kc f", kc=8),
                sd_d.rearrange("kc p f -> p kc f"), sem="wdma")

    def pch(tile, tch, w=16):
        return tile[:, tch * w:(tch + 1) * w]

    for half in range(2):
        bar = p.barrier()
        xv = p.dma("gpsimd", xb[:, :].rearrange("p (kc t) -> p kc t", kc=NKC),
                   x_d[:, half * TH:(half + 1) * TH]
                   .rearrange("(kc p) t -> p kc t", p=128),
                   waits=bar, sem="xldc")
        zw, _ = emit_ln(p, xb, TH, consts, lr, xb_waits=[("xldc", xv)], z_out=z)

        for tch in range(NTCH):
            bidx, bank, bfree = pool.acquire()
            for kc in range(NKC):
                w = []
                if kc == 0:
                    w = [("wdma", wrv)] + list(bfree) + zw
                p.emit("tensor",
                       lambda e, tch=tch, kc=kc, bank=bank: e.matmul(
                           bank[:, 0:16],
                           z[:, kc * TH + tch * 128: kc * TH + (tch + 1) * 128],
                           wrbuf[:, kc * 16:(kc + 1) * 16],
                           start=(kc == 0), stop=(kc == NKC - 1)),
                       waits=w, sem="pe")
            ev = p.emit("scalar",
                        lambda e, tch=tch, bank=bank: e.activation(
                            pch(prow, tch), bank[:, 0:16], AF.Copy),
                        waits=[("pe", p.v("pe"))], sem="ac")
            pool.release(bidx, [("ac", ev)])
        router_done = ("ac", p.v("ac"))

        for tch in range(NTCH):
            p.emit("vector",
                   lambda e, tch=tch: e.reduce_max(statB[:, tch:tch + 1],
                                                   pch(prow, tch),
                                                   axis=AX.X, negate=True),
                   waits=[router_done], sem="ve")
        vb = p.v("ve")
        for tch in range(NTCH):
            p.emit("scalar",
                   lambda e, tch=tch: e.activation(pch(prow, tch), pch(prow, tch),
                                                   AF.Exp,
                                                   bias=statB[:, tch:tch + 1],
                                                   scale=1.0,
                                                   accum_out=statC[:, tch:tch + 1]),
                   waits=[("ve", vb)], sem="ac")
        va = p.v("ac")
        p.emit("vector", lambda e: e.reciprocal(statC[:, 0:NTCH], statC[:, 0:NTCH]),
               waits=[("ac", va)], sem="ve")
        p.emit("vector", lambda e: e.memset(statD[:, :], 0.0), sem="ve")
        for tch in range(NTCH):
            p.emit("vector",
                   lambda e, tch=tch: e.tensor_scalar(pch(prow, tch),
                                                      pch(prow, tch),
                                                      statC[:, tch:tch + 1], None,
                                                      ALU.mult),
                   sem="ve")
        for tch in range(NTCH):
            p.emit("vector",
                   lambda e, tch=tch: e.reduce_max(statB[:, tch:tch + 1],
                                                   pch(prow, tch), axis=AX.X),
                   sem="ve")
        for tch in range(NTCH):
            p.emit("vector",
                   lambda e, tch=tch: e.tensor_scalar(pch(indt, tch),
                                                      pch(prow, tch),
                                                      statB[:, tch:tch + 1], None,
                                                      ALU.is_ge),
                   sem="ve")
        for tch in range(NTCH):
            p.emit("vector",
                   lambda e, tch=tch: e.scalar_tensor_tensor(
                       pch(indt, tch), pch(indt, tch), -1e9, pch(prow, tch),
                       ALU.mult, ALU.add),
                   sem="ve")
        for tch in range(NTCH):
            p.emit("vector",
                   lambda e, tch=tch: e.reduce_max(statD[:, tch:tch + 1],
                                                   pch(indt, tch), axis=AX.X),
                   sem="ve")
        for tch in range(NTCH):
            p.emit("vector",
                   lambda e, tch=tch: e.tensor_scalar(pch(indt, tch),
                                                      pch(prow, tch),
                                                      statD[:, tch:tch + 1], None,
                                                      ALU.is_ge),
                   sem="ve")
        for tch in range(NTCH):
            p.emit("vector",
                   lambda e, tch=tch: e.tensor_mul(pch(cwt, tch), pch(prow, tch),
                                                   pch(indt, tch)),
                   sem="ve")
        cw_done = ("ve", p.v("ve"))

        def ev_sg(mc, tc, bank_ap, waits):
            v = p.emit("scalar",
                       lambda e, mc=mc, tc=tc, bank_ap=bank_ap: e.activation(
                           hsh[:, mc * TH + tc * 512: mc * TH + tc * 512 + 512],
                           bank_ap, AF.Silu),
                       waits=waits, sem="ac")
            return ("ac", v)

        emit_gemm(p, f"sg{half}", wdram=sg_d, k_tile=128, n_kc=8, m_tile=128,
                  n_mc=8, rhs_fn=lambda kc: z[:, kc * TH:(kc + 1) * TH], T=TH,
                  pool=pool, evac=ev_sg, rhs_waits=zw, wbuf=sgbuf)
        sg_done = ("ac", p.v("ac"))

        def ev_su(mc, tc, bank_ap, waits):
            sl = hsh[:, mc * TH + tc * 512: mc * TH + tc * 512 + 512]
            v = p.emit("vector",
                       lambda e, sl=sl, bank_ap=bank_ap: e.scalar_tensor_tensor(
                           sl, bank_ap, 1.0, sl, ALU.mult, ALU.mult),
                       waits=list(waits) + [sg_done], sem="ve")
            return ("ve", v)

        emit_gemm(p, f"su{half}", wdram=su_d, k_tile=128, n_kc=8, m_tile=128,
                  n_mc=8, rhs_fn=lambda kc: z[:, kc * TH:(kc + 1) * TH], T=TH,
                  pool=pool, evac=ev_su, rhs_waits=zw, wbuf=sgbuf)
        hsh_done = [("ve", p.v("ve"))]

        def tm_gemm(hid, n_fc, wbuf_ap, wready, evac, rhs_ready):
            for tch in range(NTCH):
                for dc in range(2):
                    bidx, bank, bfree = pool.acquire()
                    for fc in range(n_fc):
                        w = []
                        if fc == 0:
                            w = [wready] + list(bfree) + list(rhs_ready)
                        p.emit("tensor",
                               lambda e, tch=tch, dc=dc, fc=fc, bank=bank: e.matmul(
                                   bank[:, :],
                                   hid[:, fc * TH + tch * 128: fc * TH + (tch + 1) * 128],
                                   wbuf_ap[:, fc * 1024 + dc * 512:
                                           fc * 1024 + dc * 512 + 512],
                                   start=(fc == 0), stop=(fc == n_fc - 1)),
                               waits=w, sem="pe")
                    done = evac(tch, dc, bank[:, :], [("pe", p.v("pe"))])
                    pool.release(bidx, [done])

        def ev_sd(tch, dc, bank_ap, waits):
            v = p.emit("vector",
                       lambda e, tch=tch, dc=dc, bank_ap=bank_ap: e.tensor_copy(
                           acc[:, tch * D + dc * 512: tch * D + dc * 512 + 512],
                           bank_ap),
                       waits=waits, sem="ve")
            return ("ve", v)

        tm_gemm(hsh, 8, sdbuf[:, :], ("wdma", sdv), ev_sd, hsh_done)

        wd_ready = [None] * 16
        wd_ready[0] = p.dma("gpsimd",
                            wdbuf[0][:, :].rearrange("p (fc f) -> p fc f", fc=4),
                            wd_d[0].rearrange("fc p f -> p fc f"), waits=[("pe", p.v("pe"))], sem="wdma")
        for ex in range(16):
            exbar = [("pe", p.v("pe"))]
            if ex + 1 < 16:
                wd_ready[ex + 1] = p.dma(
                    "gpsimd",
                    wdbuf[(ex + 1) % 2][:, :].rearrange("p (fc f) -> p fc f", fc=4),
                    wd_d[ex + 1].rearrange("fc p f -> p fc f"), waits=exbar, sem="wdma")

            def ev_g(mc, tc, bank_ap, waits):
                v = p.emit("scalar",
                           lambda e, mc=mc, tc=tc, bank_ap=bank_ap: e.activation(
                               hga[:, mc * TH + tc * 512: mc * TH + tc * 512 + 512],
                               bank_ap, AF.Silu),
                           waits=waits, sem="ac")
                return ("ac", v)

            emit_gemm(p, f"wg{half}_{ex}", wdram=wg_d[ex], k_tile=128, n_kc=8,
                      m_tile=128, n_mc=4,
                      rhs_fn=lambda kc: z[:, kc * TH:(kc + 1) * TH], T=TH,
                      pool=pool, evac=ev_g, rhs_waits=zw, wbuf=wgbuf)
            g_done = ("ac", p.v("ac"))

            def ev_u(mc, tc, bank_ap, waits):
                sl = hga[:, mc * TH + tc * 512: mc * TH + tc * 512 + 512]
                v = p.emit("vector",
                           lambda e, sl=sl, bank_ap=bank_ap: e.scalar_tensor_tensor(
                               sl, bank_ap, 1.0, sl, ALU.mult, ALU.mult),
                           waits=list(waits) + [g_done], sem="ve")
                return ("ve", v)

            emit_gemm(p, f"wu{half}_{ex}", wdram=wu_d[ex], k_tile=128, n_kc=8,
                      m_tile=128, n_mc=4,
                      rhs_fn=lambda kc: z[:, kc * TH:(kc + 1) * TH], T=TH,
                      pool=pool, evac=ev_u, rhs_waits=zw, wbuf=wgbuf)
            hga_done = [("ve", p.v("ve")), cw_done]

            def ev_wd(tch, dc, bank_ap, waits, ex=ex):
                v = p.emit("vector",
                           lambda e, tch=tch, dc=dc, bank_ap=bank_ap:
                           e.scalar_tensor_tensor(
                               acc[:, tch * D + dc * 512: tch * D + dc * 512 + 512],
                               bank_ap,
                               cwt[:, tch * 16 + ex: tch * 16 + ex + 1],
                               acc[:, tch * D + dc * 512: tch * D + dc * 512 + 512],
                               ALU.mult, ALU.add),
                           waits=waits, sem="ve")
                return ("ve", v)

            tm_gemm(hga, 4, wdbuf[ex % 2][:, :], ("wdma", wd_ready[ex]), ev_wd,
                    hga_done)

        acc_done = ("ve", p.v("ve"))
        if debug and half == 0:
            p.dma("sync", dbg_prow, prow[:, :], waits=[acc_done], sem="stdma")
            p.dma("sync", dbg_cwt, cwt[:, :], waits=[acc_done], sem="stdma")
            p.dma("sync", dbg_acc, acc[:, :], waits=[acc_done], sem="stdma")
        for tch in range(NTCH):
            p.emit("vector",
                   lambda e, tch=tch: e.reduce_sum(scol[:, tch:tch + 1],
                                                   acc[:, tch * D:(tch + 1) * D],
                                                   axis=AX.X),
                   waits=[acc_done], sem="ve")
        for tch in range(NTCH):
            p.emit("scalar",
                   lambda e, tch=tch: e.activation(trash[:, :],
                                                   acc[:, tch * D:(tch + 1) * D],
                                                   AF.Square,
                                                   accum_out=qcol[:, tch:tch + 1]),
                   waits=[acc_done], sem="ac")
        st_ready = [("ve", p.v("ve")), ("ac", p.v("ac"))]
        p.emit("vector", lambda e: e.tensor_scalar(mcol[:, :], scol[:, :],
                                                   1.0 / D, None, ALU.mult),
               waits=st_ready, sem="ve")
        p.emit("vector", lambda e: e.tensor_scalar(rcol[:, :], qcol[:, :],
                                                   1.0 / D, EPS_LN, ALU.mult,
                                                   ALU.add),
               sem="ve")
        p.emit("vector", lambda e: e.tensor_mul(mrcol[:, :], mcol[:, :],
                                                mcol[:, :]),
               sem="ve")
        p.emit("vector", lambda e: e.memset(trash[:, 0:8], 0.0), sem="ve")
        p.emit("vector", lambda e: e.tensor_sub(rcol[:, :], rcol[:, :],
                                                mrcol[:, :]),
               sem="ve")
        vv = p.emit("scalar", lambda e: e.activation(rcol[:, :], rcol[:, :],
                                                     AF.Sqrt),
                    waits=[("ve", p.v("ve"))], sem="ac")
        p.emit("vector", lambda e: e.tensor_copy(mrcol[:, :], mcol[:, :]),
               sem="ve")
        vv2 = p.emit("vector", lambda e: e.reciprocal(rcol[:, :], rcol[:, :]),
                     waits=[("ac", vv)], sem="ve")
        p.emit("vector", lambda e: e.memset(trash[:, 0:8], 0.0), sem="ve")
        vv3 = p.emit("vector",
                     lambda e: e.scalar_tensor_tensor(mrcol[:, :], mrcol[:, :],
                                                      -1.0, rcol[:, :],
                                                      ALU.mult, ALU.mult),
                     sem="ve")
        close_ready = ("ve", vv3)
        if debug and half == 0:
            for j, tile in enumerate((scol, qcol, mcol, rcol, mrcol)):
                p.dma("sync", dbg_st[:, j * 8:(j + 1) * 8], tile[:, :],
                      waits=[close_ready], sem="stdma")

        add_v = {}
        st_v = {}
        st0 = ("stdma", p.v("stdma"))
        for tch in range(NTCH):
            gtch = half * NTCH + tch
            w_x = [close_ready] if tch < 2 else [("ve", add_v[tch - 2])]
            xv2 = p.dma("sync", xch[tch % 2][:, :],
                        xtm_d.rearrange("(n p) d -> n p d", p=128)[gtch],
                        waits=w_x, sem="xre")
            wz = [close_ready, ("xre", xv2)]
            wz.append(("stdma", st_v[tch - 2]) if tch >= 2 else st0)
            p.emit("vector",
                   lambda e, tch=tch: e.tensor_scalar(
                       och[tch % 2][:, :], acc[:, tch * D:(tch + 1) * D],
                       rcol[:, tch:tch + 1], mrcol[:, tch:tch + 1],
                       ALU.mult, ALU.add),
                   waits=wz, sem="ve")
            av = p.emit("vector",
                        lambda e, tch=tch: e.tensor_add(och[tch % 2][:, :],
                                                        och[tch % 2][:, :],
                                                        xch[tch % 2][:, :]),
                        sem="ve")
            add_v[tch] = av
            st_v[tch] = p.dma("sync",
                              out_d.rearrange("(n p) d -> n p d", p=128)[gtch],
                              och[tch % 2][:, :], waits=[("ve", av)],
                              sem="stdma")
    return finish(p)


def host_l3ffn(inputs, stream):
    Wr = inputs["L3_ffn_Wr"]
    Sg, Su, Sd = inputs["L3_ffn_Sg"], inputs["L3_ffn_Su"], inputs["L3_ffn_Sd"]
    Wg, Wu, Wd = inputs["L3_ffn_Wg"], inputs["L3_ffn_Wu"], inputs["L3_ffn_Wd"]
    shared = {
        "wr": np.ascontiguousarray(Wr.reshape(8, 128, 16).astype(np.float32)),
        "sg": mcform(np.ascontiguousarray(Sg)),
        "su": mcform(np.ascontiguousarray(Su)),
        "sd": np.ascontiguousarray(Sd.reshape(8, 128, 1024).astype(np.float32)),
        "wg": np.stack([mcform(np.ascontiguousarray(Wg[e])) for e in range(16)]),
        "wu": np.stack([mcform(np.ascontiguousarray(Wu[e])) for e in range(16)]),
        "wd": np.stack([np.ascontiguousarray(Wd[e].reshape(4, 128, 1024))
                        for e in range(16)]).astype(np.float32),
    }
    return [dict(shared, x=np.ascontiguousarray(stream[c]),
                 xtm=np.ascontiguousarray(stream[c].T)) for c in range(NCORES)]


def run_l3ffn(inputs, stream):
    nc = build_l3ffn()
    res = run_bass_kernel_spmd(nc, host_l3ffn(inputs, stream),
                               core_ids=list(range(NCORES)))
    return [np.ascontiguousarray(np.asarray(r["out"]).T) for r in res.results]



EPS_L0 = 1e-6


def build_l0attn(debug=False):
    p = P("l0attn")
    T = S
    TH = 2048
    x_d = p.din("x", [D, T])
    wlam_d = p.din("wlam", [1, 128, 8 * 16])
    wb_d = p.din("wb", [1, 128, 8 * 16])
    blam_d = p.din("blam", [16, 1])
    wc_d = p.din("wc", [8, 16, 128])
    wgate_d = p.din("wgate", [8, 128, 8 * 128])
    rmsw_d = p.din("rmsw", [128, 8])
    out_d = p.dout("out", [D, T])

    consts = setup_consts(p)
    lr = make_ln_res(p, TH)
    pool = BankPool(p, "gb", 4)

    blam_s = p.sb("blam_s", 16, 1)
    rmsw_s = p.sb("rmsw_s", 128, 8)
    wc_s = p.sb("wc_s", 16, 8 * 128, BF16)
    p.dma("sync", blam_s[:, :], blam_d)
    bv = p.dma("sync", rmsw_s[:, :], rmsw_d)
    wcv = p.dma("gpsimd", wc_s[:, :].rearrange("p (mc f) -> p mc f", mc=8),
                wc_d.rearrange("mc p f -> p mc f"), sem="wdma")
    small_ready = ("syncdma", bv)

    xb = p.sb("xb", 128, NKC * T, BF16)
    inner = xb
    lam = p.sb("lam", 16, T)
    bxb = p.sb("bxb", 16, T, BF16)
    hsb = bxb
    u_t = p.sb("u_t", 128, 512, BF16)
    w_t = p.sb("w_t", 128, 8 * 512, BF16)
    sqs = p.sb("sqs", 128, 512, BF16)
    rrow2 = p.sb("rrow2", 1, 512)
    rb2 = p.sb("rb2", 128, 512)
    wlam_buf = [p.sb(f"wlamb{i}", 128, 8 * 16, BF16) for i in range(2)]
    wgate_s = p.sb("wgate_s", 128, 8 * 1024, BF16)

    xv = p.dma("gpsimd", xb[:, :].rearrange("p (kc t) -> p kc t", kc=NKC),
               x_d.rearrange("(kc p) t -> p kc t", p=128), sem="xldc")
    zw = []
    for h in range(2):
        zwh, _ = emit_ln(p, xb, TH, consts, lr, xb_waits=[("xldc", xv)],
                         z_out=xb, cstride=T, off=h * TH)
        zw += zwh
    if debug:
        dbg_z = p.dout("dbg_z", [128, NKC * T], BF16)
        p.dma("sync", dbg_z, xb[:, :], waits=zw, sem="stdma")

    def ev_lam(mc, tc, bank_ap, waits):
        v = p.emit("scalar",
                   lambda e, tc=tc, bank_ap=bank_ap: e.activation(
                       lam[0:16, tc * 512:(tc + 1) * 512], bank_ap, AF.Sigmoid,
                       bias=blam_s[:, 0:1], scale=1.0),
                   waits=list(waits) + [small_ready], sem="ac")
        return ("ac", v)

    emit_gemm(p, "lam", wdram=wlam_d, k_tile=128, n_kc=8, m_tile=16, n_mc=1,
              rhs_fn=lambda kc: xb[:, kc * T:(kc + 1) * T], T=T, pool=pool,
              evac=ev_lam, rhs_waits=zw, wbuf=wlam_buf)

    def ev_bx(mc, tc, bank_ap, waits):
        v = p.emit("scalar",
                   lambda e, tc=tc, bank_ap=bank_ap: e.activation(
                       bxb[0:16, tc * 512:(tc + 1) * 512], bank_ap, AF.Copy),
                   waits=waits, sem="ac")
        return ("ac", v)

    emit_gemm(p, "bx", wdram=wb_d, k_tile=128, n_kc=8, m_tile=16, n_mc=1,
              rhs_fn=lambda kc: xb[:, kc * T:(kc + 1) * T], T=T, pool=pool,
              evac=ev_bx, rhs_waits=zw, wbuf=wlam_buf)

    sv = p.emit("vector",
                lambda e: e.tensor_tensor_scan(hsb[0:16, :], lam[0:16, :],
                                               bxb[0:16, :], 0.0,
                                               ALU.mult, ALU.add),
                waits=[("ac", p.v("ac"))], sem="ve")
    hs_ready = ("ve", sv)

    n_tc = T // 512
    wgv = p.dma("gpsimd", wgate_s[:, :].rearrange("p (mc f) -> p mc f", mc=8),
                wgate_d.rearrange("mc p f -> p mc f"), sem="wdma")
    wg_ready = ("wdma", wgv)
    wv_ready = ("wdma", wcv)

    prev_wvv = None
    prev_inner_v = None
    for tc in range(n_tc):
        wtile_done = []
        for mc in range(NKC):
            b1, bankg, bfree1 = pool.acquire()
            for kc in range(NKC):
                w = []
                if kc == 0:
                    w = [wg_ready] + list(bfree1) + (zw if tc == 0 else [])
                p.emit("tensor",
                       lambda e, mc=mc, tc=tc, kc=kc, bankg=bankg: e.matmul(
                           bankg[:, :],
                           wgate_s[:, mc * 1024 + kc * 128: mc * 1024 + (kc + 1) * 128],
                           xb[:, kc * T + tc * 512: kc * T + (tc + 1) * 512],
                           start=(kc == 0), stop=(kc == NKC - 1)),
                       waits=w, sem="pe")
            gv = p.v("pe")
            uw = [("pe", gv)]
            if prev_wvv is not None:
                uw.append(("ve", prev_wvv))
            uv = p.emit("scalar",
                        lambda e, bankg=bankg: e.activation(u_t[:, :],
                                                            bankg[:, :],
                                                            AF.Sigmoid),
                        waits=uw, sem="ac")
            pool.release(b1, [("ac", uv)])
            b2, banky, bfree2 = pool.acquire()
            p.emit("tensor",
                   lambda e, mc=mc, tc=tc, banky=banky: e.matmul(
                       banky[:, :], wc_s[0:16, mc * 128:(mc + 1) * 128],
                       hsb[0:16, tc * 512:(tc + 1) * 512],
                       start=True, stop=True),
                   waits=[wv_ready, hs_ready] + list(bfree2), sem="pe")
            yv = p.v("pe")
            wvv = p.emit("vector",
                         lambda e, mc=mc, banky=banky: e.tensor_mul(
                             w_t[:, mc * 512:(mc + 1) * 512], u_t[:, :],
                             banky[:, :]),
                         waits=[("pe", yv), ("ac", uv)], sem="ve")
            pool.release(b2, [("ve", wvv)])
            prev_wvv = wvv
            wtile_done.append(("ve", wvv))
        sq_prev = None
        for mc in range(NKC):
            w = [wtile_done[mc]]
            if sq_prev is not None:
                w.append(("pe", sq_prev))
            qv = p.emit("scalar",
                        lambda e, mc=mc: e.activation(
                            sqs[:, :], w_t[:, mc * 512:(mc + 1) * 512],
                            AF.Square),
                        waits=w, sem="ac")
            mv = p.emit("tensor",
                        lambda e, mc=mc: e.matmul(
                            lr["ps_q"][0:1, :], consts["ones_bf"][:, :],
                            sqs[:, :], start=(mc == 0), stop=(mc == NKC - 1)),
                        waits=[("ac", qv), consts["ready"]], sem="pe")
            sq_prev = mv
        e1 = p.emit("scalar", lambda e: e.activation(rrow2[0:1, :],
                                                     lr["ps_q"][0:1, :], AF.Copy),
                    waits=[("pe", p.v("pe"))], sem="ac")
        e2 = p.emit("vector",
                    lambda e: e.tensor_scalar(rrow2[0:1, :], rrow2[0:1, :],
                                              1.0 / D, EPS_L0, ALU.mult,
                                              ALU.add),
                    waits=[("ac", e1)], sem="ve")
        e3 = p.emit("scalar", lambda e: e.activation(rrow2[0:1, :], rrow2[0:1, :],
                                                     AF.Sqrt),
                    waits=[("ve", e2)], sem="ac")
        e4 = p.emit("vector", lambda e: e.reciprocal(rrow2[0:1, :], rrow2[0:1, :]),
                    waits=[("ac", e3)], sem="ve")
        g1 = p.emit("tensor",
                    lambda e: e.matmul(lr["ps_b"][:, :], consts["ones1f"][:, :],
                                       rrow2[0:1, :], start=True, stop=True),
                    waits=[("ve", e4), ("ac", p.v("ac"))], sem="pe")
        w5 = [("pe", g1)]
        if prev_inner_v is not None:
            w5.append(("ve", prev_inner_v))
        e5 = p.emit("scalar", lambda e: e.activation(rb2[:, :], lr["ps_b"][:, :],
                                                     AF.Copy),
                    waits=w5, sem="ac")
        rdy = ("ac", e5)
        for mc in range(NKC):
            prev_inner_v = p.emit("vector",
                   lambda e, mc=mc, tc=tc: e.scalar_tensor_tensor(
                       inner[:, mc * T + tc * 512: mc * T + tc * 512 + 512],
                       w_t[:, mc * 512:(mc + 1) * 512], rmsw_s[:, mc:mc + 1],
                       rb2[:, :], ALU.mult, ALU.mult),
                   waits=[rdy, small_ready], sem="ve")
    inner_waits = [("ve", p.v("ve"))]
    if debug:
        dbg_lam = p.dout("dbg_lam", [16, T])
        dbg_hs = p.dout("dbg_hs", [16, T], BF16)
        dbg_inner = p.dout("dbg_inner", [128, NKC * T], BF16)
        p.dma("sync", dbg_lam, lam[0:16, :], waits=inner_waits, sem="stdma")
        p.dma("sync", dbg_hs, hsb[0:16, :], waits=inner_waits, sem="stdma")
        p.dma("sync", dbg_inner, inner[:, :], waits=inner_waits, sem="stdma")
    clbufs = ([p.sb("clx0", 128, TH, F32), p.sb("clx1", 128, TH, F32)],
              [p.sb("clo0", 128, TH, F32), p.sb("clo1", 128, TH, F32)])
    for h in range(2):
        emit_sandwich_close(p, f"cl{h}", inner, TH, consts, lr,
                            inner_waits=inner_waits,
                            x_dram=x_d[:, h * TH:(h + 1) * TH],
                            out_dram=out_d[:, h * TH:(h + 1) * TH],
                            cstride=T, off=h * TH, bufs=clbufs)
    return finish(p)


def host_l0attn(inputs, xfm_batches):
    Wl = inputs["L0_attn_W_lambda"]
    in_maps = []
    for c in range(NCORES):
        b = c // 2
        in_maps.append({
            "x": xfm_batches[b],
            "wlam": mcform(np.ascontiguousarray(Wl.T), m_tile=16),
            "wb": mcform(np.ascontiguousarray(inputs["L0_attn_W_B"].T), m_tile=16),
            "blam": np.ascontiguousarray(
                inputs["L0_attn_b_lambda"].reshape(16, 1).astype(np.float32)),
            "wc": mcform(np.ascontiguousarray(inputs["L0_attn_W_C"].T),
                         k_tile=16),
            "wgate": mcform(np.ascontiguousarray(inputs["L0_attn_W_gate"].T)),
            "rmsw": biasform(inputs["L0_attn_rms_weight"]),
        })
    return in_maps


def run_l0attn(inputs, xfm_batches):
    nc = build_l0attn()
    res = run_bass_kernel_spmd(nc, host_l0attn(inputs, xfm_batches),
                               core_ids=list(range(NCORES)))
    return [np.asarray(res.results[2 * b]["out"]) for b in range(4)]



def build_l1attn():
    p = P("l1attn")
    T = S
    TH = 2048
    x_d = p.din("x", [D, T])
    wpd_d = p.din("wpd", [8, 128, 8 * 128])
    wpc_d = p.din("wpc", [8, 128, 8 * 128])
    wpr_d = p.din("wpr", [8, 128, 8 * 128])
    wdl_d = p.din("wdl", [8, 128, 8 * 128])
    bd_d = p.din("bd", [128, 8])
    a1_d = p.din("a1", [128, 8])
    b1_d = p.din("b1", [128, 8])
    dsc_d = p.din("dsc", [128, 1])
    out_d = p.dout("out", [D, T])
    delta_d = p.dscratch("delta_sc", [D, T], BF16)
    inner_d = p.dscratch("inner_sc", [D, T], BF16)

    consts = setup_consts(p, fvals=(-1.0, 1.0))
    TQ = 1024
    lr = make_ln_res(p, TQ)
    pool = BankPool(p, "gb", 4)

    bd_s = p.sb("bd_s", 128, 8)
    a1_s = p.sb("a1_s", 128, 8)
    b1_s = p.sb("b1_s", 128, 8)
    d_s = p.sb("d_s", 128, 1)
    for nm, dd in (("bd_s", bd_d), ("a1_s", a1_d), ("b1_s", b1_d), ("dsc", dsc_d)):
        pass
    p.dma("sync", bd_s[:, :], bd_d)
    p.dma("sync", a1_s[:, :], a1_d)
    p.dma("sync", b1_s[:, :], b1_d)
    sv = p.dma("sync", d_s[:, :], dsc_d)
    small_ready = ("syncdma", sv)

    xb = p.sb("xb", 128, NKC * T, BF16)
    z = xb
    tpre = p.sb("tpre", 128, NKC * TH, BF16)
    drow = [p.sb(f"drow{i}", 128, T, BF16) for i in range(2)]
    cdrow = p.sb("cdrow", 128, T, BF16)
    sp0 = p.sb("sp0", 128, 512)
    sp1 = p.sb("sp1", 128, 512)
    sp2 = p.sb("sp2", 128, 512)
    dstage = p.sb("dstage", 128, 512, BF16)
    estage = p.sb("estage", 128, 512)
    ystage = p.sb("ystage", 128, 512)
    istage = [p.sb(f"istage{i}", 128, 512, BF16) for i in range(2)]
    ucc = p.sb("ucc", 128, 64)
    csc = p.sb("csc", 128, 64)
    uc_t = p.sb("uc_t", 128, 8)
    cs_t = p.sb("cs_t", 128, 8)
    c1_t = p.sb("c1_t", 128, 8)
    c2_t = p.sb("c2_t", 128, 8)
    tstage = p.sb("tstage", 128, 512)
    wbufA = [p.sb(f"wbA{i}", 128, 8 * 128, BF16) for i in range(2)]

    xv = p.dma("gpsimd", xb[:, :].rearrange("p (kc t) -> p kc t", kc=NKC),
               x_d.rearrange("(kc p) t -> p kc t", p=128), sem="xldc")
    zw = []
    for h in range(4):
        zwh, _ = emit_ln(p, xb, TQ, consts, lr, xb_waits=[("xldc", xv)],
                         z_out=xb, cstride=T, off=h * TQ)
        zw += zwh

    prev_sp = None
    for th in range(2):
        def ev_tp(mc, tc, bank_ap, waits, th=th):
            v = p.emit("scalar",
                       lambda e, mc=mc, tc=tc, bank_ap=bank_ap: e.activation(
                           tpre[:, mc * TH + tc * 512: mc * TH + tc * 512 + 512],
                           bank_ap, AF.Copy),
                       waits=waits, sem="ac")
            return ("ac", v)

        emit_gemm(p, f"tp{th}", wdram=wpd_d, k_tile=128, n_kc=8, m_tile=128,
                  n_mc=8,
                  rhs_fn=lambda kc, th=th: z[:, kc * T + th * TH: kc * T + (th + 1) * TH],
                  T=TH, pool=pool, evac=ev_tp, rhs_waits=zw, wbuf=wbufA)
        tpre_done = [("ac", p.v("ac"))]

        def ev_dl(mc, tc, bank_ap, waits, th=th):
            nonlocal_list = []
            w0 = list(waits) + [small_ready]
            if prev_spv[0] is not None:
                w0.append(("stdma2", prev_spv[0]))
            v0 = p.emit("vector",
                        lambda e, mc=mc, bank_ap=bank_ap: e.tensor_scalar(
                            sp0[:, :], bank_ap, bd_s[:, mc:mc + 1], None,
                            ALU.add),
                        waits=w0, sem="ve")
            v1 = p.emit("scalar",
                        lambda e: e.activation(sp1[:, :], sp0[:, :], AF.Abs),
                        waits=[("ve", v0)], sem="ac")
            p.emit("scalar",
                   lambda e: e.activation(sp1[:, :], sp1[:, :], AF.Exp,
                                          scale=-1.0),
                   sem="ac")
            p.emit("scalar",
                   lambda e: e.activation(sp1[:, :], sp1[:, :], AF.Ln,
                                          bias=1.0),
                   sem="ac")
            v4 = p.emit("scalar",
                        lambda e: e.activation(sp2[:, :], sp0[:, :], AF.Relu),
                        sem="ac")
            v5 = p.emit("vector",
                        lambda e: e.tensor_add(dstage[:, :], sp1[:, :],
                                               sp2[:, :]),
                        waits=[("ac", v4)], sem="ve")
            dv = p.dma("sync",
                       delta_d[mc * 128:(mc + 1) * 128,
                               th * TH + tc * 512: th * TH + tc * 512 + 512],
                       dstage[:, :], waits=[("ve", v5)], sem="stdma2")
            prev_spv[0] = dv
            return ("ve", v5)

        prev_spv = [prev_sp]
        emit_gemm(p, f"dl{th}", wdram=wdl_d, k_tile=128, n_kc=8, m_tile=128,
                  n_mc=8, rhs_fn=lambda kc: tpre[:, kc * TH:(kc + 1) * TH],
                  T=TH, pool=pool, evac=ev_dl, rhs_waits=tpre_done, wbuf=wbufA)
        prev_sp = prev_spv[0]
    delta_done = ("stdma2", p.v("stdma2"))

    def ev_cs(mc, tc, bank_ap, waits):
        v0 = p.emit("vector",
                    lambda e, mc=mc, tc=tc, bank_ap=bank_ap: e.tensor_mul(
                        tstage[:, :], bank_ap,
                        z[:, mc * T + tc * 512: mc * T + tc * 512 + 512]),
                    waits=waits, sem="ve")
        p.emit("vector",
               lambda e, mc=mc, tc=tc, bank_ap=bank_ap: e.reduce_sum(
                   csc[:, mc * 8 + tc: mc * 8 + tc + 1], bank_ap, axis=AX.X),
               sem="ve")
        v2 = p.emit("vector",
                    lambda e, mc=mc, tc=tc: e.reduce_sum(
                        ucc[:, mc * 8 + tc: mc * 8 + tc + 1], tstage[:, :],
                        axis=AX.X),
                    sem="ve")
        return ("ve", v2)

    emit_gemm(p, "cs", wdram=wpc_d, k_tile=128, n_kc=8, m_tile=128, n_mc=8,
              rhs_fn=lambda kc: z[:, kc * T:(kc + 1) * T], T=T, pool=pool,
              evac=ev_cs, rhs_waits=zw, wbuf=wbufA)
    cs_done = ("ve", p.v("ve"))
    for mc in range(8):
        p.emit("vector",
               lambda e, mc=mc: e.reduce_sum(uc_t[:, mc:mc + 1],
                                             ucc[:, mc * 8:(mc + 1) * 8],
                                             axis=AX.X),
               waits=[cs_done], sem="ve")
    for mc in range(8):
        p.emit("vector",
               lambda e, mc=mc: e.reduce_sum(cs_t[:, mc:mc + 1],
                                             csc[:, mc * 8:(mc + 1) * 8],
                                             axis=AX.X),
               sem="ve")
    p.emit("vector", lambda e: e.tensor_mul(c1_t[:, :], b1_s[:, :], uc_t[:, :]),
           waits=[small_ready], sem="ve")
    cv = p.emit("vector", lambda e: e.tensor_mul(c2_t[:, :], b1_s[:, :],
                                                 cs_t[:, :]),
                sem="ve")
    c_done = ("ve", cv)

    wres_s = p.sb("wres_s", 128, 8 * 1024, BF16)
    wres_v = p.dma("gpsimd", wres_s[:, :].rearrange("p (mc f) -> p mc f", mc=8),
                   wpr_d.rearrange("mc p f -> p mc f"), sem="wdma")
    dload = [None, None]
    dload[0] = p.dma("sync", drow[0][:, :], delta_d[0:128, :],
                     waits=[delta_done], sem="xre")
    prev_scan = None
    prev_exp = {}
    prev_inner = None
    for kc in range(NKC):
        if kc + 1 < NKC:
            w = []
            if kc >= 1:
                w = [("ve", prev_scan), ("ac", prev_exp.get(kc - 1, 0))]
            dload[(kc + 1) % 2] = p.dma(
                "sync", drow[(kc + 1) % 2][:, :],
                delta_d[(kc + 1) * 128:(kc + 2) * 128, :],
                waits=w, sem="xre")
        scv = p.emit("vector",
                     lambda e, kc=kc: e.tensor_tensor_scan(
                         cdrow[:, :], drow[kc % 2][:, :], drow[kc % 2][:, :],
                         0.0, ALU.add, ALU.bypass),
                     waits=[("xre", dload[kc % 2]), c_done] +
                           ([("ve", prev_scan)] if prev_scan else []),
                     sem="ve")
        prev_scan = scv
        for tc in range(T // 512):
            ev = p.emit("scalar",
                        lambda e, kc=kc, tc=tc: e.activation(
                            estage[:, :],
                            drow[kc % 2][:, tc * 512:(tc + 1) * 512], AF.Exp,
                            scale=a1_s[:, kc:kc + 1]),
                        waits=[("xre", dload[kc % 2])], sem="ac")
            prev_exp[kc] = ev
            t2 = p.emit("vector",
                        lambda e, kc=kc, tc=tc: e.tensor_scalar(
                            ystage[:, :], cdrow[:, tc * 512:(tc + 1) * 512],
                            c2_t[:, kc:kc + 1], None, ALU.mult),
                        waits=[("ve", scv)], sem="ve")
            t3 = p.emit("vector",
                        lambda e, kc=kc: e.scalar_tensor_tensor(
                            ystage[:, :], estage[:, :], c1_t[:, kc:kc + 1],
                            ystage[:, :], ALU.mult, ALU.add),
                        waits=[("ac", ev)], sem="ve")
            bidx, bank, bfree = pool.acquire()
            for kin in range(NKC):
                w = []
                if kin == 0:
                    w = [("wdma", wres_v)] + list(bfree)
                p.emit("tensor",
                       lambda e, kc=kc, tc=tc, kin=kin, bank=bank: e.matmul(
                           bank[:, :],
                           wres_s[:, kc * 1024 + kin * 128: kc * 1024 + (kin + 1) * 128],
                           z[:, kin * T + tc * 512: kin * T + (tc + 1) * 512],
                           start=(kin == 0), stop=(kin == NKC - 1)),
                       waits=w, sem="pe")
            iv = p.emit("vector",
                        lambda e, kc=kc, tc=tc, bank=bank: e.scalar_tensor_tensor(
                            istage[tc % 2][:, :], bank[:, :], d_s[:, 0:1],
                            ystage[:, :], ALU.mult, ALU.add),
                        waits=[("pe", p.v("pe")), ("ve", t3), small_ready] +
                              ([("stdma2", prev_inner)] if prev_inner else []),
                        sem="ve")
            pool.release(bidx, [("ve", iv)])
            prev_inner = p.dma("sync",
                               inner_d[kc * 128:(kc + 1) * 128,
                                       tc * 512:(tc + 1) * 512],
                               istage[tc % 2][:, :], waits=[("ve", iv)],
                               sem="stdma2")


    inner_done = ("stdma2", p.v("stdma2"))

    clbufs = ([p.sb("clx0", 128, TQ, F32), p.sb("clx1", 128, TQ, F32)],
              [p.sb("clo0", 128, TQ, F32), p.sb("clo1", 128, TQ, F32)])
    for th in range(4):
        ibar = p.barrier()
        ivv = p.dma("gpsimd",
                    tpre[:, 0:NKC * TQ].rearrange("p (kc t) -> p kc t", kc=NKC),
                    inner_d[:, th * TQ:(th + 1) * TQ]
                    .rearrange("(kc p) t -> p kc t", p=128),
                    waits=[inner_done] + ibar, sem="xldc")
        emit_sandwich_close(p, f"cl{th}", tpre, TQ, consts, lr,
                            inner_waits=[("xldc", ivv)],
                            x_dram=x_d[:, th * TQ:(th + 1) * TQ],
                            out_dram=out_d[:, th * TQ:(th + 1) * TQ],
                            bufs=clbufs)
    return finish(p)


def host_l1attn(inputs, xfm_batches):
    Wp = inputs["L1_attn_W_proj"]
    in_maps = []
    shared = {
        "wpd": mcform(np.ascontiguousarray(Wp[0:1024].T)),
        "wpc": mcform(np.ascontiguousarray(Wp[2048:3072].T)),
        "wpr": mcform(np.ascontiguousarray(Wp[3072:4096].T)),
        "wdl": mcform(np.ascontiguousarray(inputs["L1_attn_W_delta"].T)),
        "bd": biasform(inputs["L1_attn_b_delta"]),
        "a1": biasform(inputs["L1_attn_A"][0, 0]),
        "b1": biasform(inputs["L1_attn_Bp"][0, :, 0]),
        "dsc": np.ascontiguousarray(
            np.repeat(inputs["L1_attn_D"].astype(np.float32), 128).reshape(128, 1)),
    }
    for c in range(NCORES):
        in_maps.append(dict(shared, x=xfm_batches[c // 2]))
    return in_maps


def run_l1attn(inputs, xfm_batches):
    nc = build_l1attn()
    res = run_bass_kernel_spmd(nc, host_l1attn(inputs, xfm_batches),
                               core_ids=list(range(NCORES)))
    return [np.asarray(res.results[2 * b]["out"]) for b in range(4)]



def build_l2attn(debug=False, mini=None):
    p = P("l2attn")
    T = S
    NH = 8
    x_d = p.din("x", [D, T])
    wq_d = p.din("wq", [8, 128, 8 * 128])
    wk_d = p.din("wk", [8, 128, 8 * 128])
    wv_d = p.din("wv", [8, 128, 512])
    bq_d = p.din("bq", [128, 8])
    bk_d = p.din("bk", [128, 8])
    bv_d = p.din("bv", [1, 512])
    qpos_d = p.din("qpos", [4, T])
    kpos_d = p.din("kpos", [4, T])
    cmask_d = p.din("cmask", [4, 128, 512])
    attn_d = p.dout("attn", [T, 512], BF16)
    ksc_d = p.dscratch("ksc", [8, 128, T], BF16)

    consts = setup_consts(p, fvals=(0.125,))
    TQ = 512
    lr = make_ln_res(p, TQ)
    pool = BankPool(p, "gb", 4)

    bq_s = p.sb("bq_s", 128, 8)
    bk_s = p.sb("bk_s", 128, 8)
    bq8_s = p.sb("bq8_s", 128, 8)
    bv_s = p.sb("bv_s", 1, 512, BF16)
    onesrow = p.sb("onesrow", 1, 128, BF16)
    cmask_s = p.sb("cmask_s", 128, 4 * 512, BF16)
    bqv = p.dma("sync", bq_s[:, :], bq_d)
    bkv = p.dma("sync", bk_s[:, :], bk_d)
    bvv = p.dma("gpsimd", bv_s[:, :], bv_d, sem="xldc")
    mv = p.dma("gpsimd", cmask_s[:, :].rearrange("p (o f) -> p o f", o=4),
               cmask_d.rearrange("o p f -> p o f"), sem="mld")
    mask_ready = ("mld", mv)
    vv0 = p.emit("vector", lambda e: e.memset(onesrow[:, :], 1.0), sem="ve")
    b8 = p.emit("vector",
                lambda e: e.tensor_scalar(bq8_s[:, :], bq_s[:, :], 0.125, None,
                                          ALU.mult),
                waits=[("syncdma", bqv)], sem="ve")
    small_ready = [("syncdma", bkv), ("ve", b8), ("ve", vv0)]

    xb = p.sb("xb", 128, NKC * T, BF16)
    z = xb
    qt_a = p.sb("qt_a", 128, 4 * T, BF16)
    qt_b = p.sb("qt_b", 128, 4 * T, BF16)

    def qhat(h):
        return qt_a[:, (h % 4) * T:(h % 4 + 1) * T] if h < 4 else \
            qt_b[:, (h - 4) * T:(h - 3) * T]

    def khat(h):
        return xb[:, h * T:(h + 1) * T]

    vTx = p.sb("vTx", 128, 32 * 528, BF16)
    p_sb = [p.sb(f"p_sb{i}", 128, 512, BF16) for i in range(4)]
    rv_t = p.sb("rv_t", 128, 4)
    ast = [p.sb(f"ast{i}", 128, 64, BF16) for i in range(2)]
    kstage = [p.sb(f"kst{i}", 128, 512, BF16) for i in range(2)]
    wbufA = [p.sb(f"wbA{i}", 128, 8 * 128, BF16) for i in range(2)]

    xv = p.dma("gpsimd", xb[:, :].rearrange("p (kc t) -> p kc t", kc=NKC),
               x_d.rearrange("(kc p) t -> p kc t", p=128), sem="xldc")
    zw = []
    for hh in range(8):
        zwh, _ = emit_ln(p, xb, TQ, consts, lr, xb_waits=[("xldc", xv)],
                         z_out=xb, cstride=T, off=hh * TQ)
        zw += zwh

    wv_s = p.sb("wv_s", 128, 8 * 512, BF16)
    wvv = p.dma("gpsimd", wv_s[:, :].rearrange("p (kc f) -> p kc f", kc=8),
                wv_d.rearrange("kc p f -> p kc f"), sem="wdma")
    for tch in range(32):
        bidx, bank, bfree = pool.acquire()
        for kc in range(NKC):
            w = []
            if kc == 0:
                w = [("wdma", wvv)] + list(bfree) + zw + small_ready + \
                    [("xldc", bvv)]
            p.emit("tensor",
                   lambda e, tch=tch, kc=kc, bank=bank: e.matmul(
                       bank[:, :],
                       z[:, kc * T + tch * 128: kc * T + (tch + 1) * 128],
                       wv_s[:, kc * 512:(kc + 1) * 512],
                       start=(kc == 0), stop=False),
                   waits=w, sem="pe")
        p.emit("tensor",
               lambda e, bank=bank: e.matmul(bank[:, :], onesrow[0:1, :],
                                             bv_s[0:1, :], start=False,
                                             stop=True),
               waits=[], sem="pe")
        pv = p.v("pe")
        last = None
        for h in range(NH):
            last = p.emit("scalar",
                          lambda e, tch=tch, h=h, bank=bank: e.activation(
                              vTx[:, tch * 528 + h * 66: tch * 528 + h * 66 + 64],
                              bank[:, h * 64:(h + 1) * 64], AF.Copy),
                          waits=[("pe", pv)] if h == 0 else [], sem="ac")
        pool.release(bidx, [("ac", last)])
    ov = p.emit("vector",
                lambda e: e.memset(
                    vTx[:, :].rearrange("p (g c) -> p g c", c=66)[:, :, 64:65],
                    1.0),
                sem="ve")
    vT_done = [("ac", p.v("ac")), ("ve", ov)]

    def ev_q(mc, tc, bank_ap, waits):
        v = p.emit("vector",
                   lambda e, mc=mc, tc=tc, bank_ap=bank_ap: e.tensor_scalar(
                       qhat(mc)[:, tc * 512:(tc + 1) * 512],
                       bank_ap, 0.125, bq8_s[:, mc:mc + 1], ALU.mult, ALU.add),
                   waits=list(waits) + small_ready, sem="ve")
        return ("ve", v)

    emit_gemm(p, "q", wdram=wq_d, k_tile=128, n_kc=8, m_tile=128, n_mc=8,
              rhs_fn=lambda kc: z[:, kc * T:(kc + 1) * T], T=T, pool=pool,
              evac=ev_q, rhs_waits=zw, wbuf=wbufA)

    kst_v = [None, None]

    def ev_k(mc, tc, bank_ap, waits):
        idx = (mc * 8 + tc) % 2
        w = list(waits) + small_ready
        if kst_v[idx] is not None:
            w.append(("ksd", kst_v[idx]))
        v = p.emit("vector",
                   lambda e, bank_ap=bank_ap: e.tensor_scalar(
                       kstage[idx][:, :], bank_ap, bk_s[:, mc:mc + 1], None,
                       ALU.add),
                   waits=w, sem="ve")
        kst_v[idx] = p.dma("sync", ksc_d[mc, :, tc * 512:(tc + 1) * 512],
                           kstage[idx][:, :], waits=[("ve", v)], sem="ksd")
        return ("ve", v)

    emit_gemm(p, "k", wdram=wk_d, k_tile=128, n_kc=8, m_tile=128, n_mc=8,
              rhs_fn=lambda kc: z[:, kc * T:(kc + 1) * T], T=T, pool=pool,
              evac=ev_k, rhs_waits=zw, wbuf=wbufA)

    zdead = [("pe", p.v("pe")), ("ksd", p.v("ksd")), ("ve", p.v("ve"))]
    for h in range(NH):
        p.dma("gpsimd", khat(h), ksc_d[h], waits=zdead if h == 0 else [],
              sem="khl")
    big_done = ("khl", p.v("khl"))
    for h in range(NH):
        p.dma("gpsimd", khat(h)[64:68, :], kpos_d,
              waits=[big_done] if h == 0 else [], sem="mld")
        p.dma("gpsimd", qhat(h)[64:68, :], qpos_d,
              waits=zdead if h == 0 else [], sem="mld")
    hat_ready = ("mld", p.v("mld"))

    if debug:
        dq = p.dout("dq", [128, 8 * T], BF16)
        dk = p.dout("dk", [128, 8 * T], BF16)
        dv = p.dout("dv", [128, 32 * 528], BF16)
        dmask = p.dout("dmask", [128, 4 * 512], BF16)
        dbg_sc = p.dout("dbg_sc", [128, 512])
        wdbg = [hat_ready, ("ve", p.v("ve")), ("ac", p.v("ac")), mask_ready]
        for h in range(NH):
            p.dma("sync", dq[:, h * T:(h + 1) * T], qhat(h), waits=wdbg,
                  sem="stdma")
            p.dma("sync", dk[:, h * T:(h + 1) * T], khat(h), waits=wdbg,
                  sem="stdma")
        p.dma("sync", dv, vTx[:, :], waits=wdbg, sem="stdma")
        p.dma("sync", dmask, cmask_s[:, :], waits=wdbg, sem="stdma")

    ps_o = p.psum("ps_o", 128, 512)
    out_banks = [lr["ps_s"], lr["ps_q"], lr["ps_b"], ps_o]
    out_free = [p.barrier(("ac", "ve")) for _ in range(4)]
    pv_hist = []
    NHx, NQG = mini if mini else (NH, 8)
    for h in range(NHx):
        for qg in range(NQG):
            n_kt = 4 * qg + 4
            exp_v = {}
            pv_done = {}
            for kt in range(n_kt):
                bs, bank_s, bfree_s = pool.acquire()
                w1 = list(bfree_s)
                if kt == 0:
                    w1 += vT_done + [("ve", p.v("ve"))] + [hat_ready]
                p.emit("tensor",
                       lambda e, kt=kt, qg=qg, h=h, bank_s=bank_s: e.matmul(
                           bank_s[:, :],
                           khat(h)[:, kt * 128:(kt + 1) * 128],
                           qhat(h)[:, qg * 512:(qg + 1) * 512],
                           start=True, stop=True),
                       waits=w1, sem="pe")
                mm2 = p.v("pe")
                off = kt * 128 - qg * 512
                wexp = [("pe", mm2)]
                if off >= 0:
                    oi = off // 128
                    mk = p.emit("vector",
                                lambda e, oi=oi, bank_s=bank_s: e.tensor_add(
                                    bank_s[:, :], bank_s[:, :],
                                    cmask_s[:, oi * 512:(oi + 1) * 512]),
                                waits=[("pe", mm2), mask_ready], sem="ve")
                    wexp = [("ve", mk)]
                if len(pv_hist) >= 1:
                    wexp.append(("pe", pv_hist[-1]))
                ev = p.emit("scalar",
                            lambda e, kt=kt, bank_s=bank_s: e.activation(
                                p_sb[kt % 4][:, :], bank_s[:, :], AF.Exp),
                            waits=wexp, sem="ac")
                exp_v[kt] = ev
                pool.release(bs, [("ac", ev)])

                def emit_pv(ktp, h=h, n_kt=n_kt):
                    for qc in range(4):
                        w3 = [("ac", exp_v[ktp])]
                        if ktp == 0:
                            w3 += list(out_free[qc])
                        p.emit("tensor",
                               lambda e, ktp=ktp, qc=qc, h=h: e.matmul(
                                   out_banks[qc][:, 0:65],
                                   p_sb[ktp % 4][:, qc * 128:(qc + 1) * 128],
                                   vTx[:, ktp * 528 + h * 66:
                                       ktp * 528 + h * 66 + 65],
                                   start=(ktp == 0), stop=(ktp == n_kt - 1)),
                               waits=w3, sem="pe")
                    pv_done[ktp] = p.v("pe")
                    pv_hist.append(pv_done[ktp])
                if kt >= 3:
                    emit_pv(kt - 3)
            for ktp in range(max(0, n_kt - 3), n_kt):
                emit_pv(ktp)
            pvv = p.v("pe")
            for qc in range(4):
                p.emit("vector",
                       lambda e, qc=qc: e.reciprocal(rv_t[:, qc:qc + 1],
                                                     out_banks[qc][:, 64:65]),
                       waits=[("pe", pvv)], sem="ve")
            for qc in range(4):
                evv = p.emit("vector",
                             lambda e, qc=qc: e.tensor_scalar(
                                 ast[qc % 2][:, :], out_banks[qc][:, 0:64],
                                 rv_t[:, qc:qc + 1], None, ALU.mult),
                             waits=[("stdma", p.v("stdma"))] if qc < 2 else
                                   [("stdma", p.v("stdma") - 16)],
                             sem="ve")
                p.dma("sync",
                      attn_d[qg * 512 + qc * 128: qg * 512 + (qc + 1) * 128,
                             h * 64:(h + 1) * 64],
                      ast[qc % 2][:, :], waits=[("ve", evv)], sem="stdma")
                out_free[qc] = [("ve", evv)]
    return finish(p)


def build_l3attn(debug=False, mini=None):
    p = P("l3attn")
    T = S
    NH = 8
    x_d = p.din("x", [D, T])
    wq_d = p.din("wq", [8, 128, 8 * 128])
    wk_d = p.din("wk", [8, 128, 8 * 128])
    wv_d = p.din("wv", [8, 128, 512])
    bq_d = p.din("bq", [128, 8])
    bk_d = p.din("bk", [128, 8])
    bv_d = p.din("bv", [1, 512])
    cmask_d = p.din("cmask", [8, 128, 512])
    attn_d = p.dout("attn", [T, 512], BF16)
    ksc_d = p.dscratch("ksc", [8, 128, T], BF16)

    consts = setup_consts(p, fvals=(0.125,))
    TQ = 512
    lr = make_ln_res(p, TQ)
    pool = BankPool(p, "gb", 4)

    bq_s = p.sb("bq_s", 128, 8)
    bk_s = p.sb("bk_s", 128, 8)
    bq8_s = p.sb("bq8_s", 128, 8)
    bv_s = p.sb("bv_s", 1, 512, BF16)
    onesrow = p.sb("onesrow", 1, 128, BF16)
    cmask_s = p.sb("cmask_s", 128, 8 * 512, BF16)
    bqv = p.dma("sync", bq_s[:, :], bq_d)
    bkv = p.dma("sync", bk_s[:, :], bk_d)
    bvv = p.dma("gpsimd", bv_s[:, :], bv_d, sem="xldc")
    mv = p.dma("gpsimd", cmask_s[:, :].rearrange("p (o f) -> p o f", o=8),
               cmask_d.rearrange("o p f -> p o f"), sem="mld")
    mask_ready = ("mld", mv)
    vv0 = p.emit("vector", lambda e: e.memset(onesrow[:, :], 1.0), sem="ve")
    b8 = p.emit("vector",
                lambda e: e.tensor_scalar(bq8_s[:, :], bq_s[:, :], 0.125, None,
                                          ALU.mult),
                waits=[("syncdma", bqv)], sem="ve")
    small_ready = [("syncdma", bkv), ("ve", b8), ("ve", vv0)]

    xb = p.sb("xb", 128, NKC * T, BF16)
    z = xb
    qt_a = p.sb("qt_a", 128, 4 * T, BF16)
    qt_b = p.sb("qt_b", 128, 4 * T, BF16)

    def qhat(h):
        return qt_a[:, (h % 4) * T:(h % 4 + 1) * T] if h < 4 else \
            qt_b[:, (h - 4) * T:(h - 3) * T]

    def khat(h):
        return xb[:, h * T:(h + 1) * T]

    vTx = p.sb("vTx", 128, 32 * 528, BF16)
    p_sb = [p.sb(f"p_sb{i}", 128, 512, BF16) for i in range(4)]
    rv_t = p.sb("rv_t", 128, 4)
    ast = [p.sb(f"ast{i}", 128, 64, BF16) for i in range(2)]
    kstage = [p.sb(f"kst{i}", 128, 512, BF16) for i in range(2)]
    wbufA = [p.sb(f"wbA{i}", 128, 8 * 128, BF16) for i in range(2)]

    xv = p.dma("gpsimd", xb[:, :].rearrange("p (kc t) -> p kc t", kc=NKC),
               x_d.rearrange("(kc p) t -> p kc t", p=128), sem="xldc")
    zw = []
    for hh in range(8):
        zwh, _ = emit_ln(p, xb, TQ, consts, lr, xb_waits=[("xldc", xv)],
                         z_out=xb, cstride=T, off=hh * TQ)
        zw += zwh

    wv_s = p.sb("wv_s", 128, 8 * 512, BF16)
    wvv = p.dma("gpsimd", wv_s[:, :].rearrange("p (kc f) -> p kc f", kc=8),
                wv_d.rearrange("kc p f -> p kc f"), sem="wdma")
    for tch in range(32):
        bidx, bank, bfree = pool.acquire()
        for kc in range(NKC):
            w = []
            if kc == 0:
                w = [("wdma", wvv)] + list(bfree) + zw + small_ready + \
                    [("xldc", bvv)]
            p.emit("tensor",
                   lambda e, tch=tch, kc=kc, bank=bank: e.matmul(
                       bank[:, :],
                       z[:, kc * T + tch * 128: kc * T + (tch + 1) * 128],
                       wv_s[:, kc * 512:(kc + 1) * 512],
                       start=(kc == 0), stop=False),
                   waits=w, sem="pe")
        p.emit("tensor",
               lambda e, bank=bank: e.matmul(bank[:, :], onesrow[0:1, :],
                                             bv_s[0:1, :], start=False,
                                             stop=True),
               waits=[], sem="pe")
        pv = p.v("pe")
        last = None
        for h in range(NH):
            last = p.emit("scalar",
                          lambda e, tch=tch, h=h, bank=bank: e.activation(
                              vTx[:, tch * 528 + h * 66: tch * 528 + h * 66 + 64],
                              bank[:, h * 64:(h + 1) * 64], AF.Copy),
                          waits=[("pe", pv)] if h == 0 else [], sem="ac")
        pool.release(bidx, [("ac", last)])
    ov = p.emit("vector",
                lambda e: e.memset(
                    vTx[:, :].rearrange("p (g c) -> p g c", c=66)[:, :, 64:65],
                    1.0),
                sem="ve")
    vT_done = [("ac", p.v("ac")), ("ve", ov)]

    def ev_q(mc, tc, bank_ap, waits):
        v = p.emit("vector",
                   lambda e, mc=mc, tc=tc, bank_ap=bank_ap: e.tensor_scalar(
                       qhat(mc)[:, tc * 512:(tc + 1) * 512],
                       bank_ap, 0.125, bq8_s[:, mc:mc + 1], ALU.mult, ALU.add),
                   waits=list(waits) + small_ready, sem="ve")
        return ("ve", v)

    emit_gemm(p, "q", wdram=wq_d, k_tile=128, n_kc=8, m_tile=128, n_mc=8,
              rhs_fn=lambda kc: z[:, kc * T:(kc + 1) * T], T=T, pool=pool,
              evac=ev_q, rhs_waits=zw, wbuf=wbufA)

    kst_v = [None, None]

    def ev_k(mc, tc, bank_ap, waits):
        idx = (mc * 8 + tc) % 2
        w = list(waits) + small_ready
        if kst_v[idx] is not None:
            w.append(("ksd", kst_v[idx]))
        v = p.emit("vector",
                   lambda e, bank_ap=bank_ap: e.tensor_scalar(
                       kstage[idx][:, :], bank_ap, bk_s[:, mc:mc + 1], None,
                       ALU.add),
                   waits=w, sem="ve")
        kst_v[idx] = p.dma("sync", ksc_d[mc, :, tc * 512:(tc + 1) * 512],
                           kstage[idx][:, :], waits=[("ve", v)], sem="ksd")
        return ("ve", v)

    emit_gemm(p, "k", wdram=wk_d, k_tile=128, n_kc=8, m_tile=128, n_mc=8,
              rhs_fn=lambda kc: z[:, kc * T:(kc + 1) * T], T=T, pool=pool,
              evac=ev_k, rhs_waits=zw, wbuf=wbufA)

    zdead = [("pe", p.v("pe")), ("ksd", p.v("ksd")), ("ve", p.v("ve"))]
    for h in range(NH):
        p.dma("gpsimd", khat(h), ksc_d[h], waits=zdead if h == 0 else [],
              sem="mld")
    hat_ready = ("mld", p.v("mld"))

    if debug:
        dq = p.dout("dq", [128, 8 * T], BF16)
        dk = p.dout("dk", [128, 8 * T], BF16)
        dv = p.dout("dv", [128, 32 * 528], BF16)
        dmask = p.dout("dmask", [128, 4 * 512], BF16)
        dbg_sc = p.dout("dbg_sc", [128, 512])
        wdbg = [hat_ready, ("ve", p.v("ve")), ("ac", p.v("ac")), mask_ready]
        for h in range(NH):
            p.dma("sync", dq[:, h * T:(h + 1) * T], qhat(h), waits=wdbg,
                  sem="stdma")
            p.dma("sync", dk[:, h * T:(h + 1) * T], khat(h), waits=wdbg,
                  sem="stdma")
        p.dma("sync", dv, vTx[:, :], waits=wdbg, sem="stdma")
        p.dma("sync", dmask, cmask_s[:, :], waits=wdbg, sem="stdma")

    ps_o = p.psum("ps_o", 128, 512)
    out_banks = [lr["ps_s"], lr["ps_q"], lr["ps_b"], ps_o]
    out_free = [p.barrier(("ac", "ve")) for _ in range(4)]
    pv_hist = []
    NHx, NQG = mini if mini else (NH, 8)
    for h in range(NHx):
        for qg in range(NQG):
            kts = list(range(max(0, 4 * (qg - 1)), 4 * qg + 4))
            exp_v = {}
            pv_done = {}
            for ki, kt in enumerate(kts):
                bs, bank_s, bfree_s = pool.acquire()
                w1 = list(bfree_s)
                if ki == 0:
                    w1 += vT_done + [("ve", p.v("ve"))] + [hat_ready]
                p.emit("tensor",
                       lambda e, kt=kt, qg=qg, h=h, bank_s=bank_s: e.matmul(
                           bank_s[:, :],
                           khat(h)[:, kt * 128:(kt + 1) * 128],
                           qhat(h)[:, qg * 512:(qg + 1) * 512],
                           start=True, stop=True),
                       waits=w1, sem="pe")
                mm2 = p.v("pe")
                off = kt * 128 - qg * 512
                oi = (off + 512) // 128
                mk = p.emit("vector",
                            lambda e, oi=oi, bank_s=bank_s: e.tensor_add(
                                bank_s[:, :], bank_s[:, :],
                                cmask_s[:, oi * 512:(oi + 1) * 512]),
                            waits=[("pe", mm2), mask_ready], sem="ve")
                wexp = [("ve", mk)]
                if len(pv_hist) >= 1:
                    wexp.append(("pe", pv_hist[-1]))
                ev = p.emit("scalar",
                            lambda e, ki=ki, bank_s=bank_s: e.activation(
                                p_sb[ki % 4][:, :], bank_s[:, :], AF.Exp),
                            waits=wexp, sem="ac")
                exp_v[ki] = ev
                pool.release(bs, [("ac", ev)])

                def emit_pv(kip, h=h, kts=kts):
                    ktp = kts[kip]
                    for qc in range(4):
                        w3 = [("ac", exp_v[kip])]
                        if kip == 0:
                            w3 += list(out_free[qc])
                        p.emit("tensor",
                               lambda e, ktp=ktp, kip=kip, qc=qc, h=h: e.matmul(
                                   out_banks[qc][:, 0:65],
                                   p_sb[kip % 4][:, qc * 128:(qc + 1) * 128],
                                   vTx[:, ktp * 528 + h * 66:
                                       ktp * 528 + h * 66 + 65],
                                   start=(kip == 0), stop=(kip == len(kts) - 1)),
                               waits=w3, sem="pe")
                    pv_done[kip] = p.v("pe")
                    pv_hist.append(pv_done[kip])
                if ki >= 3:
                    emit_pv(ki - 3)
            for kip in range(max(0, len(kts) - 3), len(kts)):
                emit_pv(kip)
            pvv = p.v("pe")
            for qc in range(4):
                p.emit("vector",
                       lambda e, qc=qc: e.reciprocal(rv_t[:, qc:qc + 1],
                                                     out_banks[qc][:, 64:65]),
                       waits=[("pe", pvv)], sem="ve")
            for qc in range(4):
                evv = p.emit("vector",
                             lambda e, qc=qc: e.tensor_scalar(
                                 ast[qc % 2][:, :], out_banks[qc][:, 0:64],
                                 rv_t[:, qc:qc + 1], None, ALU.mult),
                             waits=[("stdma", p.v("stdma"))] if qc < 2 else
                                   [("stdma", p.v("stdma") - 16)],
                             sem="ve")
                p.dma("sync",
                      attn_d[qg * 512 + qc * 128: qg * 512 + (qc + 1) * 128,
                             h * 64:(h + 1) * 64],
                      ast[qc % 2][:, :], waits=[("ve", evv)], sem="stdma")
                out_free[qc] = [("ve", evv)]
    return finish(p)


def host_l3attn(inputs, xfm_batches):
    Wq, Wk, Wv = (inputs["L3_attn_Wq"], inputs["L3_attn_Wk"],
                  inputs["L3_attn_Wv"])
    cmask = np.zeros((8, 128, 512), np.float32)
    kk = np.arange(128)[:, None]
    qq = np.arange(512)[None, :]
    for oi in range(8):
        off = oi * 128 - 512
        d = qq - kk - off
        cmask[oi] = np.where((d >= 0) & (d < 512), 0.0, -1e30)

    def padheads(w):
        wp = np.zeros((1024, 1024), np.float32)
        for h in range(8):
            wp[h * 128:h * 128 + 64] = w[h * 64:(h + 1) * 64]
        return wp

    zeros8 = np.zeros((128, 8), np.float32)
    zv = np.zeros((1, 512), np.float32)
    in_maps = []
    for c in range(NCORES):
        b, g = c // 2, c % 2
        hs = slice(512 * g, 512 * (g + 1))
        in_maps.append({
            "x": xfm_batches[b],
            "wq": mcform(np.ascontiguousarray(padheads(Wq.T[:, hs].T).T)),
            "wk": mcform(np.ascontiguousarray(padheads(Wk.T[:, hs].T).T)),
            "wv": np.ascontiguousarray(Wv[hs].T.reshape(8, 128, 512)),
            "bq": zeros8, "bk": zeros8, "bv": zv,
            "cmask": cmask,
        })
    return in_maps


def run_l3attn(inputs, xfm_batches):
    nc = build_l3attn()
    res = run_bass_kernel_spmd(nc, host_l3attn(inputs, xfm_batches),
                               core_ids=list(range(NCORES)))
    outs = []
    for b in range(4):
        outs.append(np.concatenate(
            [np.asarray(res.results[2 * b]["attn"]),
             np.asarray(res.results[2 * b + 1]["attn"])], axis=1))
    return outs


def host_l2attn(inputs, xfm_batches):
    wqkv = inputs["L2_attn_w_qkv"]
    bqkv = inputs["L2_attn_b_qkv"]
    qpos = np.zeros((4, S), np.float32)
    kpos = np.zeros((4, S), np.float32)
    pos = np.arange(S)
    qpos[0] = -(pos & ~255).astype(np.float32)
    qpos[1] = -(pos & 255).astype(np.float32)
    qpos[2] = 1.0
    qpos[3] = 1.0
    kpos[0] = 1.0
    kpos[1] = 1.0
    kpos[2] = (pos & ~255).astype(np.float32)
    kpos[3] = (pos & 255).astype(np.float32)
    cmask = np.zeros((4, 128, 512), np.float32)
    kk = np.arange(128)[:, None]
    qq = np.arange(512)[None, :]
    for oi in range(4):
        cmask[oi] = np.where(qq - kk >= oi * 128, 0.0, -1e30)
    def padheads(w):
        wp = np.zeros((1024, 1024), np.float32)
        for h in range(8):
            wp[h * 128:h * 128 + 64] = w[h * 64:(h + 1) * 64]
        return wp

    def padbias(bb):
        bp = np.zeros(1024, np.float32)
        for h in range(8):
            bp[h * 128:h * 128 + 64] = bb[h * 64:(h + 1) * 64]
        return bp

    in_maps = []
    for c in range(NCORES):
        b, g = c // 2, c % 2
        qs = slice(512 * g, 512 * (g + 1))
        ks = slice(1024 + 512 * g, 1024 + 512 * (g + 1))
        vs = slice(2048 + 512 * g, 2048 + 512 * (g + 1))
        in_maps.append({
            "x": xfm_batches[b],
            "wq": mcform(np.ascontiguousarray(padheads(wqkv[qs]).T)),
            "wk": mcform(np.ascontiguousarray(padheads(wqkv[ks]).T)),
            "wv": np.ascontiguousarray(wqkv[vs].T.reshape(8, 128, 512)),
            "bq": biasform(padbias(bqkv[qs])),
            "bk": biasform(padbias(bqkv[ks])),
            "bv": np.ascontiguousarray(bqkv[vs].reshape(1, 512)),
            "qpos": qpos, "kpos": kpos, "cmask": cmask,
        })
    return in_maps


def run_l2attn(inputs, xfm_batches):
    nc = build_l2attn()
    res = run_bass_kernel_spmd(nc, host_l2attn(inputs, xfm_batches),
                               core_ids=list(range(NCORES)))
    outs = []
    for b in range(4):
        outs.append(np.concatenate(
            [np.asarray(res.results[2 * b]["attn"]),
             np.asarray(res.results[2 * b + 1]["attn"])], axis=1))
    return outs



def build_attnout():
    p = P("attnout")
    x_d = p.din("x", [D, TPC])
    attn_d = p.din("attn", [D, TPC], BF16)
    wo_d = p.din("wo", [8, 128, 8 * 128])
    bo_d = p.din("bo", [128, 8])
    out_d = p.dout("out", [D, TPC])

    consts = setup_consts(p)
    lr = make_ln_res(p, TPC)
    pool = BankPool(p, "gb", 4)
    bo_s = p.sb("bo_s", 128, 8)
    bv = p.dma("sync", bo_s[:, :], bo_d)
    bias_ready = ("syncdma", bv)

    ab = p.sb("ab", 128, NKC * TPC, BF16)
    inner = p.sb("inner", 128, NKC * TPC, BF16)
    av = p.dma("sync", ab[:, :].rearrange("p (kc t) -> p kc t", kc=NKC),
               attn_d.rearrange("(kc p) t -> p kc t", p=128), sem="xre")

    def ev(mc, tc, bank_ap, waits):
        v = p.emit("vector",
                   lambda e, mc=mc, tc=tc, bank_ap=bank_ap: e.tensor_scalar(
                       inner[:, mc * TPC + tc * 512: mc * TPC + tc * 512 + 512],
                       bank_ap, bo_s[:, mc:mc + 1], None, ALU.add),
                   waits=list(waits) + [bias_ready], sem="ve")
        return ("ve", v)

    emit_gemm(p, "o", wdram=wo_d, k_tile=128, n_kc=8, m_tile=128, n_mc=8,
              rhs_fn=lambda kc: ab[:, kc * TPC:(kc + 1) * TPC], T=TPC,
              pool=pool, evac=ev, rhs_waits=[("xre", av)])
    emit_sandwich_close(p, "cl", inner, TPC, consts, lr,
                        inner_waits=[("ve", p.v("ve"))], x_dram=x_d,
                        out_dram=out_d)
    return finish(p)


def host_attnout(w_out, b_out, attn_tm_batches, stream):
    wo = mcform(np.ascontiguousarray(w_out.T))
    bo = biasform(b_out)
    in_maps = []
    for c in range(NCORES):
        b, half = c // 2, c % 2
        a_fm = np.ascontiguousarray(
            attn_tm_batches[b][half * TPC:(half + 1) * TPC].T)
        in_maps.append({"x": np.ascontiguousarray(stream[c]), "attn": a_fm,
                        "wo": wo, "bo": bo})
    return in_maps


def run_attnout(w_out, b_out, attn_tm_batches, stream):
    nc = build_attnout()
    res = run_bass_kernel_spmd(nc, host_attnout(w_out, b_out, attn_tm_batches,
                                                stream),
                               core_ids=list(range(NCORES)))
    return [np.asarray(r["out"]) for r in res.results]



def _batches_to_stream(xfm):
    return [np.ascontiguousarray(xfm[c // 2][:, (c % 2) * TPC:(c % 2 + 1) * TPC])
            for c in range(NCORES)]

def _stream_to_batches(stream):
    return [np.ascontiguousarray(np.concatenate([stream[2 * b], stream[2 * b + 1]],
                                                axis=1)) for b in range(B)]

def kernel(**inputs):
    inputs = {k: np.asarray(v) for k, v in inputs.items()}
    x0 = inputs["L0_attn_x"].astype(np.float32)
    xfm = [np.ascontiguousarray(x0[b].T) for b in range(B)]

    xfm = run_l0attn(inputs, xfm)
    stream = _batches_to_stream(xfm)
    stream = run_l0ffn(inputs, stream)

    xfm = _stream_to_batches(stream)
    xfm = run_l1attn(inputs, xfm)
    stream = _batches_to_stream(xfm)
    stream = run_l1ffn(inputs, stream)

    xfm = _stream_to_batches(stream)
    attn = run_l2attn(inputs, xfm)
    stream = run_attnout(inputs["L2_attn_w_out"], inputs["L2_attn_b_out"],
                         attn, stream)
    stream = run_l2ffn(inputs, stream)

    xfm = _stream_to_batches(stream)
    attn = run_l3attn(inputs, xfm)
    stream = run_attnout(inputs["L3_attn_Wo"],
                         np.zeros(D, np.float32), attn, stream)
    stream = run_l3ffn(inputs, stream)

    out = np.empty((B, S, D), np.float32)
    for b in range(B):
        out[b] = np.concatenate([stream[2 * b], stream[2 * b + 1]], axis=1).T
    return out
